# version 08
# baseline (speedup 1.0000x reference)
import numpy as np
import ml_dtypes
import concourse.bass as bass
import concourse.mybir as mybir
from concourse.bass_utils import run_bass_kernel_spmd

F32 = mybir.dt.float32
BF16 = mybir.dt.bfloat16
I32 = mybir.dt.int32
AF = mybir.ActivationFunctionType
ALU = mybir.AluOpType
AX = mybir.AxisListType

D = 1024
DC = D // 128
TC = 4096
TSEQ = 2048
NB = TC // 512
BN = 512
NCORES = 8

BF16NP = ml_dtypes.bfloat16


ENGS = ("t", "v", "s", "p", "sd", "cd", "gd")
SEQMAP = {"tensor": ("t",), "vector": ("v",), "scalar": ("s", "cd"),
          "gpsimd": ("p", "gd"), "sync": ("sd",)}


class Op:
    __slots__ = ("eng", "fn", "after", "target", "n_inc", "tag", "idx", "key")

    def __init__(self, eng, fn, after, n_inc, tag, idx, key):
        self.eng = eng
        self.fn = fn
        self.after = list(after)
        self.n_inc = n_inc
        self.target = None
        self.tag = tag
        self.idx = idx
        self.key = key


class Buf:
    __slots__ = ("name", "last_writer", "readers")

    def __init__(self, name):
        self.name = name
        self.last_writer = None
        self.readers = []


class Prog:

    def __init__(self, nc, name="prog"):
        self.nc = nc
        self.name = name
        self.ops = []
        self.sems = {e: nc.alloc_semaphore(f"{name}_{e}") for e in ("t", "v", "s", "p")}
        self.counts = {e: 0 for e in ("t", "v", "s", "p")}

    def _key_sem(self, key):
        km = getattr(self, "_keymap", None)
        if km is None:
            km = self._keymap = {}
        if key in km:
            return km[key]
        slot = f"dmapool{len(km)}"
        km[key] = slot
        if slot not in self.sems:
            self.sems[slot] = self.nc.alloc_semaphore(f"{self.name}_{slot}")
            self.counts[slot] = 0
        return slot

    def add(self, eng, fn, after=(), tag="", key=None):
        after = [a for a in after if a is not None]
        if eng in ("sd", "cd", "gd"):
            assert key is not None, f"DMA op needs a sem key: {tag}"
            key = self._key_sem(key)
            n_inc = 16
        else:
            key = eng
            n_inc = 1
        if getattr(self, "_fence", None) is not None:
            after.append(self._fence)
        op = Op(eng, fn, after, n_inc, tag, len(self.ops), key)
        self.ops.append(op)
        return op

    def fence(self, tag="fence", exclude=()):
        last = {}
        for op in self.ops:
            if op in exclude:
                continue
            last[op.key] = op
        f = self.add("v", lambda e: e.engine_nop(), after=list(last.values()),
                     tag=tag)
        self._fence = f
        self._keymap = {}
        return f

    def rd(self, op, *bufs):
        for b in bufs:
            if b is None:
                continue
            if b.last_writer is not None and b.last_writer is not op:
                op.after.append(b.last_writer)
            b.readers.append(op)
        return op

    def wr(self, op, *bufs):
        for b in bufs:
            if b is None:
                continue
            if b.last_writer is not None and b.last_writer is not op:
                op.after.append(b.last_writer)
            op.after.extend(r for r in b.readers if r is not op)
            b.last_writer = op
            b.readers = []
        return op

    def run(self):
        for op in self.ops:
            self.counts[op.key] += op.n_inc
            op.target = self.counts[op.key]
        nc = self.nc
        seq_of = {}
        for seq, engs in SEQMAP.items():
            for e in engs:
                seq_of[e] = seq
        seq_ops = {seq: [o for o in self.ops if seq_of[o.eng] == seq]
                   for seq in SEQMAP}
        COMPUTE = ("t", "v", "s", "p")

        def body(seqname):
            def _(eng):
                hw = {}
                for op in seq_ops[seqname]:
                    waits = {}
                    for dep in op.after:
                        if (dep.eng == op.eng and dep.eng == "t"
                                and dep.idx < op.idx):
                            continue
                        if dep.target > hw.get(dep.key, 0):
                            waits[dep.key] = max(waits.get(dep.key, 0),
                                                 dep.target)
                    for k2 in sorted(waits):
                        eng.wait_ge(self.sems[k2], waits[k2])
                        hw[k2] = waits[k2]
                    last = op.fn(eng)
                    assert last is not None, f"op fn returned None: {op.tag}"
                    last.then_inc(self.sems[op.key], op.n_inc)
            return _

        with nc.Block() as blk:
            if seq_ops["tensor"]:
                blk.tensor(body("tensor"))
            if seq_ops["vector"]:
                blk.vector(body("vector"))
            if seq_ops["scalar"]:
                blk.scalar(body("scalar"))
            if seq_ops["gpsimd"]:
                blk.gpsimd(body("gpsimd"))
            if seq_ops["sync"]:
                blk.sync(body("sync"))

        targets = dict(self.counts)
        with nc.Block() as blk2:
            @blk2.sync
            def _(sy):
                for k, tv in targets.items():
                    if tv:
                        sy.wait_ge(self.sems[k], tv)


def mmgroup(prog, psum_items, pairs, after=(), bufs_rd=(), bufs_wr=(),
            tag="mm", start=True, stop=True):
    n = len(pairs)
    psum_ap = psum_items[0]

    def fn(t):
        ins = None
        for j, (lh, rh) in enumerate(pairs):
            ins = t.matmul(psum_ap, lh, rh,
                           start=(start and j == 0), stop=(stop and j == n - 1))
        return ins

    op = prog.add("t", fn, after, tag)
    prog.rd(op, *bufs_rd)
    prog.wr(op, psum_items[1], *bufs_wr)
    return op



_SCOPES = []


def scope_begin():
    _SCOPES.append([])


def scope_end():
    for g in reversed(_SCOPES.pop()):
        g.__exit__(None, None, None)


def sb(nc, name, shape, dtype, **kw):
    if _SCOPES:
        g = nc.sbuf_tensor(name, shape, dtype, **kw)
        h = g.__enter__()
        _SCOPES[-1].append(g)
    else:
        h = nc.alloc_sbuf_tensor(name, shape, dtype, **kw)
    return (h[:], Buf(name), h)


def psum(nc, name, shape, dtype=F32):
    if _SCOPES:
        g = nc.psum_tensor(name, shape, dtype)
        h = g.__enter__()
        _SCOPES[-1].append(g)
    else:
        h = nc.alloc_psum_tensor(name, shape, dtype)
    return (h[:], Buf(name), h)


def dram(nc, name, shape, dtype=F32, kind="Internal"):
    h = nc.dram_tensor(name, shape, dtype, kind=kind)
    return h



def mk_ln_scratch(nc, tag="", psA=None, psB=None):
    sc = {}
    sc["xb"] = [sb(nc, f"lnxb{tag}{c}", [128, BN], BF16) for c in range(DC)]
    sc["sq"] = [sb(nc, f"lnsq{tag}{c}", [128, BN], BF16) for c in range(DC)]
    sc["psA"] = psA if psA is not None else psum(nc, f"lnpsA{tag}", [128, BN])
    sc["psB"] = psB if psB is not None else psum(nc, f"lnpsB{tag}", [128, BN])
    for nm in ("mean", "m2", "var", "rstd", "mr"):
        sc[nm] = sb(nc, f"ln{nm}{tag}", [128, BN], F32)
    sc["tmp"] = [sb(nc, f"lntmp{tag}{i}", [128, BN], F32) for i in range(2)]
    return sc


def emit_ln(prog, X8b, XLb, sc, ones, eps=1e-5, stats=None, norm=True):
    xb, sq = sc["xb"], sc["sq"]
    rstd = sc["rstd"] if stats is None else stats["rstd"]
    mr = sc["mr"] if stats is None else stats["mr"]
    for c in range(DC):
        op = prog.add("v", lambda e, c=c: e.tensor_copy(xb[c][0], X8b[c][0]),
                      tag=f"lncast{c}")
        prog.rd(op, X8b[c][1]); prog.wr(op, xb[c][1])
        sop = prog.add("s", lambda e, c=c: e.activation(sq[c][0], X8b[c][0], AF.Square),
                       tag=f"lnsq{c}")
        prog.rd(sop, X8b[c][1]); prog.wr(sop, sq[c][1])

    mmgroup(prog, sc["psA"], [(ones[0], xb[c][0]) for c in range(DC)],
            bufs_rd=[ones[1]] + [xb[c][1] for c in range(DC)], tag="ln_sum")
    mmgroup(prog, sc["psB"], [(ones[0], sq[c][0]) for c in range(DC)],
            bufs_rd=[ones[1]] + [sq[c][1] for c in range(DC)], tag="ln_ssq")

    mean, m2, var = sc["mean"], sc["m2"], sc["var"]
    o = prog.add("v", lambda e: e.tensor_scalar(mean[0], sc["psA"][0], 1.0 / D, None, ALU.mult), tag="ln_mean")
    prog.rd(o, sc["psA"][1]); prog.wr(o, mean[1])
    o = prog.add("v", lambda e: e.tensor_mul(m2[0], mean[0], mean[0]), tag="ln_m2")
    prog.rd(o, mean[1]); prog.wr(o, m2[1])
    o = prog.add("v", lambda e: e.scalar_tensor_tensor(var[0], sc["psB"][0], 1.0 / D, m2[0], ALU.mult, ALU.subtract), tag="ln_var")
    prog.rd(o, sc["psB"][1], m2[1]); prog.wr(o, var[1])
    o = prog.add("v", lambda e: e.tensor_scalar(var[0], var[0], eps, None, ALU.add), tag="ln_veps")
    prog.rd(o, var[1]); prog.wr(o, var[1])
    o = prog.add("v", lambda e: e.reciprocal(var[0], var[0]), tag="ln_rec")
    prog.rd(o, var[1]); prog.wr(o, var[1])
    o = prog.add("s", lambda e: e.activation(rstd[0], var[0], AF.Sqrt), tag="ln_rstd")
    prog.rd(o, var[1]); prog.wr(o, rstd[1])
    o = prog.add("v", lambda e: e.tensor_mul(mr[0], mean[0], rstd[0]), tag="ln_mr")
    prog.rd(o, mean[1], rstd[1]); prog.wr(o, mr[1])
    if stats is not None and "mean" in stats:
        mo = prog.add("v", lambda e: e.tensor_copy(stats["mean"][0], mean[0]), tag="ln_meancp")
        prog.rd(mo, mean[1]); prog.wr(mo, stats["mean"][1])

    if not norm:
        return
    for c in range(DC):
        eng = "v"
        tm = sc["tmp"][c % 2]
        o1 = prog.add(eng, lambda e, c=c, tm=tm: e.tensor_mul(tm[0], X8b[c][0], rstd[0]), tag=f"lnn1_{c}")
        prog.rd(o1, X8b[c][1], rstd[1]); prog.wr(o1, tm[1])
        o2 = prog.add(eng, lambda e, c=c, tm=tm: e.tensor_sub(XLb[c][0], tm[0], mr[0]), tag=f"lnn2_{c}")
        prog.rd(o2, tm[1], mr[1]); prog.wr(o2, XLb[c][1])



def emit_cast_to_dram(prog, src_d, dst, tag):
    op = prog.add("gd", lambda e: e.dma_start(dst[0][:], src_d[:]), tag=tag,
                  key=dst[1].name)
    prog.wr(op, dst[1])
    return op


def emit_transpose_stage(prog, src_bf, dst_T, rows, cols, stage, tag):
    for c in range(cols // 128):
        st = stage[c % 2]
        lo = prog.add("sd", lambda e, c=c, st=st: e.dma_start(
            st[0][:, 0:rows], src_bf[0][:][:, c * 128:(c + 1) * 128],
            transpose=True), tag=f"{tag}_t{c}", key=st[1].name)
        prog.rd(lo, src_bf[1])
        prog.wr(lo, st[1])
        so = prog.add("sd", lambda e, c=c, st=st: e.dma_start(
            dst_T[0][:][c], st[0][:, 0:rows]), tag=f"{tag}_s{c}", key=st[1].name)
        prog.rd(so, st[1])
        prog.wr(so, dst_T[1])



HID = 3072
HC = HID // 128


def build_ffn(nc, prog, xin, xout, wg_d, wu_d, wd_d, cst, pre=None, pfx="f", precast=None):
    ones = cst["ones128"]

    if pre is None:
        if precast is not None:
            wg_bf, wu_bf, wd_bf = precast
        else:
            wg_bf = (dram(nc, f"{pfx}wg_bf", [HID, D], BF16), Buf(f"{pfx}wg_bf"))
            wu_bf = (dram(nc, f"{pfx}wu_bf", [HID, D], BF16), Buf(f"{pfx}wu_bf"))
            wd_bf = (dram(nc, f"{pfx}wd_bf", [D, HID], BF16), Buf(f"{pfx}wd_bf"))
            emit_cast_to_dram(prog, wg_d, wg_bf, "cast_wg")
            emit_cast_to_dram(prog, wu_d, wu_bf, "cast_wu")
            emit_cast_to_dram(prog, wd_d, wd_bf, "cast_wd")
        wgT = (dram(nc, f"{pfx}wgT", [DC, 128, HID], BF16), Buf(f"{pfx}wgT"))
        wuT = (dram(nc, f"{pfx}wuT", [DC, 128, HID], BF16), Buf(f"{pfx}wuT"))
        wdT = (dram(nc, f"{pfx}wdT", [HC, 128, D], BF16), Buf(f"{pfx}wdT"))
        stg = [sb(nc, f"{pfx}wstg{i}", [128, max(HID, D)], BF16,
                  is_transpose_dest=True) for i in range(2)]
        emit_transpose_stage(prog, wg_bf, wgT, HID, D, stg, "wgT")
        emit_transpose_stage(prog, wu_bf, wuT, HID, D, stg, "wuT")
        emit_transpose_stage(prog, wd_bf, wdT, D, HID, stg, "wdT")
    else:
        wgT, wuT, wdT = pre["wgT"], pre["wuT"], pre["wdT"]

    GW = 256
    NGS = HID // GW
    wg_sl = [sb(nc, f"{pfx}wgsl{i}", [128, DC * GW], BF16) for i in range(2)]
    wu_sl = [sb(nc, f"{pfx}wusl{i}", [128, DC * GW], BF16) for i in range(2)]
    DW = 256
    NDS = D // DW
    wd_sl = [sb(nc, f"{pfx}wdsl{i}", [128, HC * DW], BF16) for i in range(2)]

    lnsc = mk_ln_scratch(nc, pfx)
    xbat = [sb(nc, f"{pfx}xb{i}", [128, DC * BN], F32) for i in range(2)]
    X8b = [[(xbat[i][0][:, c * BN:(c + 1) * BN], xbat[i][1]) for c in range(DC)]
           for i in range(2)]
    XLb = [[sb(nc, f"{pfx}xl{c}_{i}", [128, BN], BF16) for c in range(DC)] for i in range(2)]
    sg = [sb(nc, f"{pfx}sg{a}", [128, BN], BF16) for a in range(2)]
    hh = [sb(nc, f"{pfx}hh{a}", [128, BN], BF16) for a in range(HC)]
    xob = [sb(nc, f"{pfx}xo{i}", [128, DC * BN], F32) for i in range(2)]
    xo = [[(xob[i][0][:, c * BN:(c + 1) * BN], xob[i][1]) for c in range(DC)]
          for i in range(2)]
    pp = [psum(nc, f"{pfx}ps{i}", [128, BN]) for i in range(4)]
    SCALE2 = 1.0 / (127.0 * 127.0)

    pi = 0
    for b in range(NB):
        dbl = b % 2
        src = xin[:].rearrange("c p t -> p c t")[:, :, b * BN:(b + 1) * BN]
        op = prog.add("sd", lambda e, d=xbat[dbl][0].rearrange("p (c t) -> p c t", c=DC), s=src: e.dma_start(d, s),
                      tag=f"ldx{b}", key=xbat[dbl][1].name)
        prog.wr(op, xbat[dbl][1])
        emit_ln(prog, X8b[dbl], XLb[dbl], lnsc, ones)

        for g in range(NGS):
            wsl = wg_sl[g % 2]
            usl = wu_sl[g % 2]
            src = wgT[0][:].rearrange("c p h -> p c h")[:, :, g * GW:(g + 1) * GW]
            lo = prog.add("cd", lambda e, d=wsl[0], s=src: e.dma_start(d, s), tag=f"ldwg{b}_{g}", key=wsl[1].name)
            prog.rd(lo, wgT[1]); prog.wr(lo, wsl[1])
            src = wuT[0][:].rearrange("c p h -> p c h")[:, :, g * GW:(g + 1) * GW]
            lo = prog.add("cd", lambda e, d=usl[0], s=src: e.dma_start(d, s), tag=f"ldwu{b}_{g}", key=usl[1].name)
            prog.rd(lo, wuT[1]); prog.wr(lo, usl[1])
            for ha in (2 * g, 2 * g + 1):
                off = (ha - 2 * g) * 128
                pg = pp[pi % 4]; pi += 1
                mmgroup(prog, pg,
                        [(wsl[0][:, c * GW + off: c * GW + off + 128], XLb[dbl][c][0]) for c in range(DC)],
                        bufs_rd=[wsl[1]] + [XLb[dbl][c][1] for c in range(DC)],
                        tag=f"gate{b}_{ha}")
                sgt = sg[ha % 2]
                o = prog.add("s", lambda e, d=sgt[0], s=pg[0]: e.activation(
                    d, s, AF.Silu, scale=1.0 / 127.0), tag=f"silu{b}_{ha}")
                prog.rd(o, pg[1]); prog.wr(o, sgt[1])
                pu = pp[pi % 4]; pi += 1
                mmgroup(prog, pu,
                        [(usl[0][:, c * GW + off: c * GW + off + 128], XLb[dbl][c][0]) for c in range(DC)],
                        bufs_rd=[usl[1]] + [XLb[dbl][c][1] for c in range(DC)],
                        tag=f"up{b}_{ha}")
                o = prog.add("v", lambda e, d=hh[ha][0], s=pu[0], g2=sgt[0]: e.tensor_mul(d, g2, s), tag=f"hmul{b}_{ha}")
                prog.rd(o, pu[1], sgt[1]); prog.wr(o, hh[ha][1])

        for ds in range(NDS):
            dsl = wd_sl[ds % 2]
            src = wdT[0][:].rearrange("a p d -> p a d")[:, :, ds * DW:(ds + 1) * DW]
            lo = prog.add("cd", lambda e, d=dsl[0], s=src: e.dma_start(d, s), tag=f"ldwd{b}_{ds}", key=dsl[1].name)
            prog.rd(lo, wdT[1]); prog.wr(lo, dsl[1])
            for c in (2 * ds, 2 * ds + 1):
                off = (c - 2 * ds) * 128
                pd = pp[pi % 4]; pi += 1
                mmgroup(prog, pd,
                        [(dsl[0][:, a * DW + off: a * DW + off + 128], hh[a][0]) for a in range(HC)],
                        bufs_rd=[dsl[1]] + [hh[a][1] for a in range(HC)],
                        tag=f"down{b}_{c}")
                o = prog.add("v", lambda e, d=xo[dbl][c][0], s=pd[0], x=X8b[dbl][c][0]:
                             e.scalar_tensor_tensor(d, s, SCALE2, x, ALU.mult, ALU.add), tag=f"res{b}_{c}")
                prog.rd(o, pd[1], X8b[dbl][c][1]); prog.wr(o, xo[dbl][c][1])
        dst = xout[:].rearrange("c p t -> p c t")[:, :, b * BN:(b + 1) * BN]
        so = prog.add("sd", lambda e, d=dst, s=xob[dbl][0].rearrange("p (c t) -> p c t", c=DC): e.dma_start(d, s),
                      tag=f"st{b}", key=xob[dbl][1].name)
        prog.rd(so, xob[dbl][1])



NH = 16
HD = 64
NQT = TSEQ // 128
NQB = TSEQ // 512


def bias_cols(nc, prog, pfx, name, b_dram, pp, dim=D):
    nch = dim // 128
    row = sb(nc, f"{pfx}_{name}row", [1, dim], F32)
    op = prog.add("sd", lambda e: e.dma_start(row[0], b_dram[:].rearrange("(o f) -> o f", o=1)),
                  tag=f"{pfx}ldb_{name}", key=row[1].name)
    prog.wr(op, row[1])
    cols = sb(nc, f"{pfx}_{name}cols", [128, nch], F32)
    ones1 = sb(nc, f"{pfx}_{name}one", [1, 1], F32)
    o = prog.add("v", lambda e: e.memset(ones1[0], 1.0), tag=f"{pfx}ones1_{name}")
    prog.wr(o, ones1[1])
    for c in range(nch):
        ps_ = pp[c % 2]
        mo = mmgroup(prog, (ps_[0][:, 0:1], ps_[1]), [(row[0][:, c * 128:(c + 1) * 128], ones1[0])],
                     bufs_rd=[row[1], ones1[1]], tag=f"{pfx}bT_{name}{c}")
        eo = prog.add("v", lambda e, c=c, ps_=ps_: e.tensor_copy(cols[0][:, c:c + 1], ps_[0][:, 0:1]),
                      tag=f"{pfx}bTe_{name}{c}")
        prog.rd(eo, ps_[1]); prog.wr(eo, cols[1])
    return cols


def bias_bc(nc, prog, pfx, name, b_dram, onesf, pp):
    row = sb(nc, f"{pfx}_{name}row", [1, D], F32)
    op = prog.add("sd", lambda e: e.dma_start(row[0], b_dram[:].rearrange("(o f) -> o f", o=1)),
                  tag=f"{pfx}ldbr_{name}", key=row[1].name)
    prog.wr(op, row[1])
    bc = sb(nc, f"{pfx}_{name}bc", [128, D], F32)
    for half in range(2):
        ps_ = pp[half % 2]
        mo = mmgroup(prog, ps_, [(onesf[0][0:1, :], row[0][:, half * 512:(half + 1) * 512])],
                     bufs_rd=[row[1]], tag=f"{pfx}bc_{name}{half}")
        eo = prog.add("v", lambda e, half=half, ps_=ps_: e.tensor_copy(bc[0][:, half * 512:(half + 1) * 512], ps_[0]),
                      tag=f"{pfx}bce_{name}{half}")
        prog.rd(eo, ps_[1]); prog.wr(eo, bc[1])
    return bc


def load_wslice(nc, prog, slot, w_dram, wbuf, m, tag):
    src = w_dram[:].rearrange("(kc p) f -> p kc f", p=128)[:, :, m * 128:(m + 1) * 128]
    lo = prog.add("gd", lambda e: e.dma_start(slot[0], src), tag=tag, key=slot[1].name)
    if wbuf is not None:
        prog.rd(lo, wbuf)
    prog.wr(lo, slot[1])
    return lo


def emit_out_proj(nc, prog, pfx, a_scr, xin, xout, wo_d, bo_c, wsl, pp,
                  xbat, X8b, scale=None):
    ab_buf = [sb(nc, f"{pfx}_abb0", [128, DC * BN], BF16)] * 2
    xob = [sb(nc, f"{pfx}_xo0", [128, DC * BN], F32)] * 2
    for b in range(NB):
        dbl = b % 2
        src = a_scr[0][:].rearrange("c p t -> p c t")[:, :, b * BN:(b + 1) * BN]
        op = prog.add("sd", lambda e, d=ab_buf[dbl][0].rearrange("p (c t) -> p c t", c=DC), s2=src: e.dma_start(d, s2),
                      tag=f"{pfx}lda{b}", key=ab_buf[dbl][1].name)
        prog.rd(op, a_scr[1]); prog.wr(op, ab_buf[dbl][1])
        src = xin[:].rearrange("c p t -> p c t")[:, :, b * BN:(b + 1) * BN]
        op = prog.add("sd", lambda e, d=xbat[dbl][0].rearrange("p (c t) -> p c t", c=DC), s2=src: e.dma_start(d, s2),
                      tag=f"{pfx}ldx2{b}", key=xbat[dbl][1].name)
        prog.wr(op, xbat[dbl][1])
        for m in range(DC):
            slot = wsl[m % 4]
            load_wslice(nc, prog, slot, wo_d, None, m, f"{pfx}lwo{b}{m}")
            pg = pp[m % 2]
            mmgroup(prog, pg,
                    [(slot[0][:, kc * 128:(kc + 1) * 128], ab_buf[dbl][0][:, kc * BN:(kc + 1) * BN]) for kc in range(DC)],
                    bufs_rd=[slot[1], ab_buf[dbl][1]],
                    tag=f"{pfx}wo{b}_{m}")
            if bo_c is not None:
                o = prog.add("v", lambda e, d=xob[dbl][0][:, m * BN:(m + 1) * BN], p2=pg, m=m, x=X8b[dbl][m][0]:
                             e.scalar_tensor_tensor(d, p2[0], bo_c[0][:, m:m + 1], x, ALU.add, ALU.add),
                             tag=f"{pfx}res{b}_{m}")
                prog.rd(o, pg[1], bo_c[1], xbat[dbl][1]); prog.wr(o, xob[dbl][1])
            else:
                sc2 = 1.0 if scale is None else scale
                o = prog.add("v", lambda e, d=xob[dbl][0][:, m * BN:(m + 1) * BN], p2=pg, m=m, x=X8b[dbl][m][0], sc2=sc2:
                             e.scalar_tensor_tensor(d, p2[0], sc2, x, ALU.mult, ALU.add),
                             tag=f"{pfx}res{b}_{m}")
                prog.rd(o, pg[1], xbat[dbl][1]); prog.wr(o, xob[dbl][1])
        dst = xout[:].rearrange("c p t -> p c t")[:, :, b * BN:(b + 1) * BN]
        so = prog.add("sd", lambda e, d=dst, s2=xob[dbl][0].rearrange("p (c t) -> p c t", c=DC): e.dma_start(d, s2),
                      tag=f"{pfx}st{b}", key=xob[dbl][1].name)
        prog.rd(so, xob[dbl][1])


def build_attn_b(nc, prog, xin, xout, wd, cst, pfx="ab", eps=1e-5):
    ones = cst["ones128"]
    onesf = cst["onesf"]
    identbf = cst["identbf"]
    causal = cst["causal128"]

    q_scr = (dram(nc, f"{pfx}_qscr", [DC, 128, TSEQ], BF16), Buf(f"{pfx}_qscr"))
    k_scr = (dram(nc, f"{pfx}_kscr", [DC, 128, TSEQ], BF16), Buf(f"{pfx}_kscr"))
    v_scr = (dram(nc, f"{pfx}_vscr", [NQT, 128, D], BF16), Buf(f"{pfx}_vscr"))
    a_scr = (dram(nc, f"{pfx}_ascr", [DC, 128, TC], BF16), Buf(f"{pfx}_ascr"))

    pp = [psum(nc, f"{pfx}_pp{i}", [128, BN]) for i in range(2)]
    sc_ps = psum(nc, f"{pfx}_scps", [128, 2048])
    sc_bk = [Buf(f"{pfx}_scb{i}") for i in range(4)]
    pt_ps = psum(nc, f"{pfx}_ptps", [128, BN], BF16)
    pv_ps = psum(nc, f"{pfx}_pvps", [128, BN])

    lnsc = mk_ln_scratch(nc, pfx, psA=pp[0], psB=pp[1])

    xbat = [sb(nc, f"{pfx}_xb0", [128, DC * BN], F32)] * 2
    X8b = [[(xbat[i][0][:, c * BN:(c + 1) * BN], xbat[i][1]) for c in range(DC)]
           for i in range(2)]
    XLb = [[sb(nc, f"{pfx}_xl{c}_{i}", [128, BN], BF16) for c in range(DC)] for i in range(2)]
    wsl = [sb(nc, f"{pfx}_wsl{i}", [128, D], BF16) for i in range(4)]
    wv_t = sb(nc, f"{pfx}_wvt", [128, DC * D], BF16)
    qb_buf = [sb(nc, f"{pfx}_qbb0", [128, DC * BN], BF16)] * 2
    kb_buf = [sb(nc, f"{pfx}_kbb0", [128, DC * BN], BF16)] * 2
    vb_buf = [sb(nc, f"{pfx}_vbb0", [128, 4 * D], BF16)] * 2

    bq_c = bias_cols(nc, prog, pfx, "bq", wd["bq"], pp)
    bk_c = bias_cols(nc, prog, pfx, "bk", wd["bk"], pp)
    bo_c = bias_cols(nc, prog, pfx, "bo", wd["bo"], pp)
    bv_b = bias_bc(nc, prog, pfx, "bv", wd["bv"], onesf, pp)

    lo = prog.add("gd", lambda e: e.dma_start(
        wv_t[0], wd["wv"][:].rearrange("(kc p) f -> p kc f", p=128)),
        tag=f"{pfx}ldwv", key=wv_t[1].name)
    prog.wr(lo, wv_t[1])

    SCALE_Q = 1.0 / np.sqrt(HD)

    for s in range(2):
        for nb in range(NQB):
            dbl = nb % 2
            tok0 = s * TSEQ + nb * BN
            src = xin[:].rearrange("c p t -> p c t")[:, :, tok0:tok0 + BN]
            op = prog.add("sd", lambda e, d=xbat[dbl][0].rearrange("p (c t) -> p c t", c=DC), s2=src: e.dma_start(d, s2),
                          tag=f"{pfx}ldx{s}_{nb}", key=xbat[dbl][1].name)
            prog.wr(op, xbat[dbl][1])
            emit_ln(prog, X8b[dbl], XLb[dbl], lnsc, ones, eps=eps)

            for which, wname, bias, scl, obuf in (
                    (0, "wq", bq_c, SCALE_Q, qb_buf), (1, "wk", bk_c, 1.0, kb_buf)):
                for m in range(DC):
                    slot = wsl[(which * DC + m) % 4]
                    load_wslice(nc, prog, slot, wd[wname], None, m, f"{pfx}lw{wname}{s}{nb}{m}")
                    pg = pp[m % 2]
                    mmgroup(prog, pg,
                            [(slot[0][:, kc * 128:(kc + 1) * 128], XLb[dbl][kc][0]) for kc in range(DC)],
                            bufs_rd=[slot[1]] + [XLb[dbl][kc][1] for kc in range(DC)],
                            tag=f"{pfx}p{wname}{s}_{nb}_{m}")
                    dst = obuf[dbl][0][:, m * BN:(m + 1) * BN]
                    o = prog.add("v", lambda e, d=dst, p2=pg, b3=bias, m=m, scl=scl:
                                 e.tensor_scalar(d, p2[0], b3[0][:, m:m + 1], scl, ALU.add, ALU.mult),
                                 tag=f"{pfx}e{wname}{s}_{nb}_{m}")
                    prog.rd(o, pg[1], bias[1]); prog.wr(o, obuf[dbl][1])
            so = prog.add("sd", lambda e, d=q_scr[0][:].rearrange("c p t -> p c t")[:, :, nb * BN:(nb + 1) * BN],
                          s2=qb_buf[dbl][0].rearrange("p (c t) -> p c t", c=DC): e.dma_start(d, s2),
                          tag=f"{pfx}stq{s}_{nb}", key=qb_buf[dbl][1].name)
            prog.rd(so, qb_buf[dbl][1]); prog.wr(so, q_scr[1])
            so = prog.add("sd", lambda e, d=k_scr[0][:].rearrange("c p t -> p c t")[:, :, nb * BN:(nb + 1) * BN],
                          s2=kb_buf[dbl][0].rearrange("p (c t) -> p c t", c=DC): e.dma_start(d, s2),
                          tag=f"{pfx}stk{s}_{nb}", key=kb_buf[dbl][1].name)
            prog.rd(so, kb_buf[dbl][1]); prog.wr(so, k_scr[1])

            for tt in range(4):
                for fh in range(2):
                    pg = pp[(tt * 2 + fh) % 2]
                    mmgroup(prog, pg,
                            [(XLb[dbl][kc][0][:, tt * 128:(tt + 1) * 128],
                              wv_t[0][:, kc * D + fh * 512: kc * D + (fh + 1) * 512]) for kc in range(DC)],
                            bufs_rd=[wv_t[1]] + [XLb[dbl][kc][1] for kc in range(DC)],
                            tag=f"{pfx}pv{s}_{nb}_{tt}_{fh}")
                    dst = vb_buf[dbl][0][:, tt * D + fh * 512: tt * D + (fh + 1) * 512]
                    o = prog.add("v", lambda e, d=dst, p2=pg, fh=fh:
                                 e.tensor_add(d, p2[0], bv_b[0][:, fh * 512:(fh + 1) * 512]),
                                 tag=f"{pfx}ev{s}_{nb}_{tt}_{fh}")
                    prog.rd(o, pg[1], bv_b[1]); prog.wr(o, vb_buf[dbl][1])
            so = prog.add("sd", lambda e, d=v_scr[0][:].rearrange("t p f -> p t f")[:, nb * 4:(nb + 1) * 4, :],
                          s2=vb_buf[dbl][0].rearrange("p (t f) -> p t f", t=4): e.dma_start(d, s2),
                          tag=f"{pfx}stv{s}_{nb}", key=vb_buf[dbl][1].name)
            prog.rd(so, vb_buf[dbl][1]); prog.wr(so, v_scr[1])

        if s == 0:
            scope_begin()
            q_h = [sb(nc, f"{pfx}_qh{i}", [64, TSEQ], BF16) for i in range(2)]
            k_h = [sb(nc, f"{pfx}_kh{i}", [64, TSEQ], BF16) for i in range(2)]
            v_hp = [sb(nc, f"{pfx}_vhp{i}", [128, NQT * 128], BF16) for i in range(2)]
            P_sb = [sb(nc, f"{pfx}_P{i}", [128, 2048], BF16) for i in range(4)]
            PT_sb = [sb(nc, f"{pfx}_PT{i}", [128, BN], BF16) for i in range(2)]
            a_hd = [sb(nc, f"{pfx}_ah{i}", [64, TSEQ], BF16) for i in range(2)]
            nmax = sb(nc, f"{pfx}_nmax", [128, 1], F32)
            cmax = sb(nc, f"{pfx}_cmax", [128, 4], F32)
            rsum = sb(nc, f"{pfx}_rsum", [128, 1], F32)
            rcp = [sb(nc, f"{pfx}_rcp{i}", [128, 1], F32) for i in range(4)]
            build_attn_b.attn_bufs = (q_h, k_h, v_hp, P_sb, PT_sb, a_hd, nmax, rsum, rcp, cmax)
        else:
            q_h, k_h, v_hp, P_sb, PT_sb, a_hd, nmax, rsum, rcp, cmax = build_attn_b.attn_bufs

        for h in range(NH):
            hb = h % 2
            qh, kh, ah = q_h[hb], k_h[hb], a_hd[hb]
            c0 = h // 2
            r0 = (h % 2) * 64
            lo = prog.add("sd", lambda e, d=qh[0], s2=q_scr[0][:][c0, r0:r0 + 64, :]: e.dma_start(d, s2),
                          tag=f"{pfx}ldqh{s}_{h}", key=qh[1].name)
            prog.rd(lo, q_scr[1]); prog.wr(lo, qh[1])
            lo = prog.add("sd", lambda e, d=kh[0], s2=k_scr[0][:][c0, r0:r0 + 64, :]: e.dma_start(d, s2),
                          tag=f"{pfx}ldkh{s}_{h}", key=kh[1].name)
            prog.rd(lo, k_scr[1]); prog.wr(lo, kh[1])
            if h % 2 == 0:
                vhp = v_hp[(h // 2) % 2]
                lo = prog.add("gd", lambda e, d=vhp[0].rearrange("p (t f) -> p t f", t=NQT),
                              s2=v_scr[0][:].rearrange("t p f -> p t f")[:, :, h * 64:(h + 2) * 64]: e.dma_start(d, s2),
                              tag=f"{pfx}ldvh{s}_{h}", key=vhp[1].name)
                prog.rd(lo, v_scr[1]); prog.wr(lo, vhp[1])
            else:
                vhp = v_hp[(h // 2) % 2]
            vcol = (h % 2) * 64

            for qb in range(NQB):
                for t4 in range(4):
                    i = qb * 4 + t4
                    valid = (i + 1) * 128
                    nch = (valid + 511) // 512
                    for kb in range(nch):
                        w = min(512, valid - kb * 512)
                        mmgroup(prog, (sc_ps[0][:, kb * 512: kb * 512 + w], sc_bk[kb]),
                                [(qh[0][:, i * 128:(i + 1) * 128], kh[0][:, kb * 512: kb * 512 + w])],
                                bufs_rd=[qh[1], kh[1]], tag=f"{pfx}sc{s}_{h}_{i}_{kb}")
                        if kb == nch - 1:
                            o = prog.add("v", lambda e, i=i, valid=valid: e.tensor_add(
                                sc_ps[0][:, i * 128:valid], sc_ps[0][:, i * 128:valid], causal[0]),
                                tag=f"{pfx}mask{s}_{h}_{i}")
                            prog.rd(o, causal[1], sc_bk[kb]); prog.wr(o, sc_bk[kb])
                        o = prog.add("v", lambda e, kb=kb, w=w: e.reduce_max(
                            cmax[0][:, kb:kb + 1], sc_ps[0][:, kb * 512: kb * 512 + w], axis=AX.X, negate=True),
                            tag=f"{pfx}cmax{s}_{h}_{i}_{kb}")
                        prog.rd(o, sc_bk[kb]); prog.wr(o, cmax[1])
                    o = prog.add("v", lambda e, nch=nch: e.tensor_reduce(
                        nmax[0], cmax[0][:, 0:nch], axis=AX.X, op=ALU.min),
                        tag=f"{pfx}max{s}_{h}_{i}")
                    prog.rd(o, cmax[1]); prog.wr(o, nmax[1])
                    o = prog.add("s", lambda e, t4=t4, valid=valid: e.activation(
                        P_sb[t4][0][:, 0:valid], sc_ps[0][:, 0:valid], AF.Exp,
                        bias=nmax[0], accum_out=rsum[0]),
                        tag=f"{pfx}exp{s}_{h}_{i}")
                    prog.rd(o, nmax[1], *sc_bk[0:nch]); prog.wr(o, P_sb[t4][1], rsum[1])
                    o = prog.add("v", lambda e, t4=t4: e.reciprocal(rcp[t4][0], rsum[0]),
                                 tag=f"{pfx}rcp{s}_{h}_{i}")
                    prog.rd(o, rsum[1]); prog.wr(o, rcp[t4][1])
                    o = prog.add("v", lambda e, t4=t4, valid=valid: e.tensor_scalar(
                        P_sb[t4][0][:, 0:valid], P_sb[t4][0][:, 0:valid], rcp[t4][0], None, ALU.mult),
                        tag=f"{pfx}pscale{s}_{h}_{i}")
                    prog.rd(o, rcp[t4][1], P_sb[t4][1]); prog.wr(o, P_sb[t4][1])

                nj = (qb + 1) * 4
                for j in range(nj):
                    ptb = PT_sb[j % 2]
                    t4min = max(0, j - qb * 4)
                    for t4 in range(t4min, 4):
                        to = prog.add("t", lambda e, j=j, t4=t4: e.transpose(
                            pt_ps[0][:, t4 * 128:(t4 + 1) * 128],
                            P_sb[t4][0][:, j * 128:(j + 1) * 128], identbf[0]),
                            tag=f"{pfx}tr{s}_{h}_{qb}_{j}_{t4}")
                        prog.rd(to, P_sb[t4][1], identbf[1])
                        prog.wr(to, pt_ps[1])
                    eo = prog.add("v" if j % 2 == 0 else "s",
                                  lambda e, t4min=t4min, ptb=ptb: (e.tensor_copy if hasattr(e, "tensor_copy") else e.copy)(
                                      ptb[0][:, t4min * 128:512], pt_ps[0][:, t4min * 128:512]),
                                  tag=f"{pfx}pte{s}_{h}_{qb}_{j}")
                    prog.rd(eo, pt_ps[1]); prog.wr(eo, ptb[1])
                    mmgroup(prog, (pv_ps[0][0:64, t4min * 128:512], pv_ps[1]),
                            [(vhp[0][:, j * 128 + vcol: j * 128 + vcol + 64], ptb[0][:, t4min * 128:512])],
                            bufs_rd=[vhp[1], ptb[1]], tag=f"{pfx}pv{s}_{h}_{qb}_{j}",
                            start=(j == 0), stop=(j == nj - 1))
                eo = prog.add("v", lambda e, qb=qb, ah=ah: e.tensor_copy(
                    ah[0][:, qb * 512:(qb + 1) * 512], pv_ps[0][0:64, :]),
                    tag=f"{pfx}pve{s}_{h}_{qb}")
                prog.rd(eo, pv_ps[1]); prog.wr(eo, ah[1])

            so = prog.add("sd", lambda e, d=a_scr[0][:][c0, r0:r0 + 64, s * TSEQ:(s + 1) * TSEQ], s2=ah[0]: e.dma_start(d, s2),
                          tag=f"{pfx}sta{s}_{h}", key=ah[1].name)
            prog.rd(so, ah[1]); prog.wr(so, a_scr[1])

    scope_end()
    emit_out_proj(nc, prog, pfx, a_scr, xin, xout, wd["wo"], bo_c, wsl, pp,
                  xbat, X8b)



def emit_elu1(prog, pfx, dst, pg, tmp1, tmp2, tag):
    o1 = prog.add("v", lambda e: e.tensor_scalar(tmp1[0], pg[0], 0.0, None, ALU.min), tag=f"{pfx}el1{tag}")
    prog.rd(o1, pg[1]); prog.wr(o1, tmp1[1])
    o2 = prog.add("s", lambda e: e.activation(tmp2[0], tmp1[0], AF.Exp), tag=f"{pfx}el2{tag}")
    prog.rd(o2, tmp1[1]); prog.wr(o2, tmp2[1])
    o3 = prog.add("v", lambda e: e.scalar_tensor_tensor(dst[0], pg[0], 0.0, tmp2[0], ALU.max, ALU.add), tag=f"{pfx}el3{tag}")
    prog.rd(o3, pg[1], tmp2[1]); prog.wr(o3, dst[1])
    return o3


def build_attn_lin(nc, prog, xin, xout, wd, cst, pfx="al", eps_attn=1e-6, stages="abcdeotu"):
    ones = cst["ones128"]
    identbf = cst["identbf"]

    q_scr = (dram(nc, f"{pfx}_qscr", [DC, 128, TSEQ], BF16), Buf(f"{pfx}_qscr"))
    a_scr = (dram(nc, f"{pfx}_ascr", [DC, 128, TC], BF16), Buf(f"{pfx}_ascr"))

    pp = [psum(nc, f"{pfx}_pp{i}", [128, BN]) for i in range(2)]
    kv_ps = psum(nc, f"{pfx}_kvps", [128, BN])
    ks_ps = [psum(nc, f"{pfx}_ksps{i}", [1, BN]) for i in range(2)]
    z_ps = psum(nc, f"{pfx}_zps", [128, 16])
    ot_ps = psum(nc, f"{pfx}_otps", [128, BN])
    at_ps = psum(nc, f"{pfx}_atps", [128, BN], BF16)

    lnsc = mk_ln_scratch(nc, pfx, psA=pp[0], psB=pp[1])
    xbat = [sb(nc, f"{pfx}_xb0", [128, DC * BN], F32)] * 2
    X8b = [[(xbat[i][0][:, c * BN:(c + 1) * BN], xbat[i][1]) for c in range(DC)]
           for i in range(2)]
    XLb = [[sb(nc, f"{pfx}_xl{c}_{i}", [128, BN], BF16) for c in range(DC)] for i in range(2)]
    wsl = [sb(nc, f"{pfx}_wsl{i}", [128, D], BF16) for i in range(4)]
    wk_t = sb(nc, f"{pfx}_wkt", [128, DC * D], BF16)
    wv_t = sb(nc, f"{pfx}_wvt", [128, DC * D], BF16)
    qb_buf = [sb(nc, f"{pfx}_qbb0", [128, DC * BN], BF16)] * 2
    ktile = [sb(nc, f"{pfx}_kt{i}", [128, D], BF16) for i in range(2)]
    vtile = [sb(nc, f"{pfx}_vt{i}", [128, D], BF16) for i in range(2)]
    tmp1 = sb(nc, f"{pfx}_tmp1", [128, BN], F32)
    tmp2 = sb(nc, f"{pfx}_tmp2", [128, BN], F32)
    kv_eo = [sb(nc, f"{pfx}_kvsb{i}", [64, BN], BF16) for i in range(2)]
    kv_acc = sb(nc, f"{pfx}_kvacc", [128, BN], F32)
    ksrow = sb(nc, f"{pfx}_ksrow", [1, D], BF16)
    KS_eo = [sb(nc, f"{pfx}_KS{i}", [64, 16], BF16) for i in range(2)]
    zc = sb(nc, f"{pfx}_zc", [128, 16], F32)
    otok = sb(nc, f"{pfx}_otok", [128, D], BF16)
    ones1b = sb(nc, f"{pfx}_ones1b", [1, 1], BF16)
    qld_eo = [sb(nc, f"{pfx}_qld{i}", [64, DC * BN], BF16) for i in range(2)]
    abat = [sb(nc, f"{pfx}_ab0", [128, DC * BN], BF16)] * 2

    bo_c = bias_cols(nc, prog, pfx, "bo", wd["bo"], pp)
    o = prog.add("v", lambda e: e.memset(ones1b[0], 1.0), tag=f"{pfx}one1b")
    prog.wr(o, ones1b[1])

    for nm, t in (("wk", wk_t), ("wv", wv_t)):
        lo = prog.add("gd", lambda e, t=t, nm=nm: e.dma_start(
            t[0], wd[nm][:].rearrange("(kc p) f -> p kc f", p=128)),
            tag=f"{pfx}ld{nm}", key=t[1].name)
        prog.wr(lo, t[1])

    for sq in range(2):
        if "b" not in stages:
            break
        for nb in range(NQB):
            dbl = nb % 2
            tok0 = sq * TSEQ + nb * BN
            src = xin[:].rearrange("c p t -> p c t")[:, :, tok0:tok0 + BN]
            op = prog.add("sd", lambda e, d=xbat[dbl][0].rearrange("p (c t) -> p c t", c=DC), s2=src: e.dma_start(d, s2),
                          tag=f"{pfx}ldx{sq}_{nb}", key=xbat[dbl][1].name)
            prog.wr(op, xbat[dbl][1])
            emit_ln(prog, X8b[dbl], XLb[dbl], lnsc, ones)

            for m in range(DC):
                slot = wsl[m % 4]
                load_wslice(nc, prog, slot, wd["wq"], None, m, f"{pfx}lwq{sq}{nb}{m}")
                pg = pp[m % 2]
                mmgroup(prog, pg,
                        [(slot[0][:, kc * 128:(kc + 1) * 128], XLb[dbl][kc][0]) for kc in range(DC)],
                        bufs_rd=[slot[1]] + [XLb[dbl][kc][1] for kc in range(DC)],
                        tag=f"{pfx}pq{sq}_{nb}_{m}")
                emit_elu1(prog, pfx, (qb_buf[dbl][0][:, m * BN:(m + 1) * BN], qb_buf[dbl][1]),
                          pg, tmp1, tmp2, f"q{sq}_{nb}_{m}")
            so = prog.add("sd", lambda e, d=q_scr[0][:].rearrange("c p t -> p c t")[:, :, nb * BN:(nb + 1) * BN],
                          s2=qb_buf[dbl][0].rearrange("p (c t) -> p c t", c=DC): e.dma_start(d, s2),
                          tag=f"{pfx}stq{sq}_{nb}", key=qb_buf[dbl][1].name)
            prog.rd(so, qb_buf[dbl][1]); prog.wr(so, q_scr[1])

            for tt in range(4):
                gt = nb * 4 + tt
                kt, vt = ktile[tt % 2], vtile[tt % 2]
                for fh in range(2):
                    pg = pp[fh % 2]
                    mmgroup(prog, pg,
                            [(XLb[dbl][kc][0][:, tt * 128:(tt + 1) * 128],
                              wk_t[0][:, kc * D + fh * 512: kc * D + (fh + 1) * 512]) for kc in range(DC)],
                            bufs_rd=[wk_t[1]] + [XLb[dbl][kc][1] for kc in range(DC)],
                            tag=f"{pfx}pk{sq}_{gt}_{fh}")
                    emit_elu1(prog, pfx, (kt[0][:, fh * 512:(fh + 1) * 512], kt[1]),
                              pg, tmp1, tmp2, f"k{sq}_{gt}_{fh}")
                for fh in range(2):
                    pg = pp[fh % 2]
                    mmgroup(prog, pg,
                            [(XLb[dbl][kc][0][:, tt * 128:(tt + 1) * 128],
                              wv_t[0][:, kc * D + fh * 512: kc * D + (fh + 1) * 512]) for kc in range(DC)],
                            bufs_rd=[wv_t[1]] + [XLb[dbl][kc][1] for kc in range(DC)],
                            tag=f"{pfx}pv{sq}_{gt}_{fh}")
                    o = prog.add("v", lambda e, d=vt[0][:, fh * 512:(fh + 1) * 512], p2=pg: e.tensor_copy(d, p2[0]),
                                 tag=f"{pfx}ev{sq}_{gt}_{fh}")
                    prog.rd(o, pg[1]); prog.wr(o, vt[1])
                for h in range(NH):
                    r0 = (h % 2) * 64
                    c0 = (h // 2) * 64
                    mmgroup(prog, (kv_ps[0][r0:r0 + 64, c0:c0 + 64], kv_ps[1]),
                            [(kt[0][:, h * 64:(h + 1) * 64], vt[0][:, h * 64:(h + 1) * 64])],
                            bufs_rd=[kt[1], vt[1]], tag=f"{pfx}kv{sq}_{gt}_{h}")
                if gt == 0:
                    o = prog.add("v", lambda e: e.tensor_copy(kv_acc[0], kv_ps[0]), tag=f"{pfx}kva{sq}_{gt}")
                else:
                    o = prog.add("v", lambda e: e.tensor_add(kv_acc[0], kv_acc[0], kv_ps[0]), tag=f"{pfx}kva{sq}_{gt}")
                prog.rd(o, kv_ps[1], kv_acc[1]); prog.wr(o, kv_acc[1])
                for fh in range(2):
                    mmgroup(prog, ks_ps[fh],
                            [(ones[0][:, 0:1], kt[0][:, fh * 512:(fh + 1) * 512])],
                            bufs_rd=[ones[1], kt[1]], tag=f"{pfx}ks{sq}_{gt}_{fh}",
                            start=(gt == 0), stop=(gt == NQT - 1))

        if "c" not in stages:
            continue
        for i in range(2):
            o = prog.add("v", lambda e, i=i: e.tensor_copy(kv_eo[i][0], kv_acc[0][i * 64:(i + 1) * 64, :]),
                         tag=f"{pfx}kve{sq}_{i}")
            prog.rd(o, kv_acc[1]); prog.wr(o, kv_eo[i][1])
        for fh in range(2):
            o = prog.add("v", lambda e, fh=fh: e.tensor_copy(ksrow[0][:, fh * 512:(fh + 1) * 512], ks_ps[fh][0]),
                         tag=f"{pfx}kse{sq}_{fh}")
            prog.rd(o, ks_ps[fh][1]); prog.wr(o, ksrow[1])
        for i in range(2):
            o = prog.add("v", lambda e, i=i: e.memset(KS_eo[i][0], 0.0), tag=f"{pfx}KS0{sq}_{i}")
            prog.wr(o, KS_eo[i][1])
        for h in range(NH):
            c = h // 2
            r0 = (h % 2) * 64
            pg = pp[h % 2]
            mmgroup(prog, (pg[0][0:64, 0:1], pg[1]),
                    [(ksrow[0][:, c * 128 + r0: c * 128 + r0 + 64], ones1b[0])],
                    bufs_rd=[ksrow[1], ones1b[1]], tag=f"{pfx}ksT{sq}{h}")
            o = prog.add("v", lambda e, h=h, pg=pg: e.tensor_copy(
                KS_eo[h % 2][0][:, h:h + 1], pg[0][0:64, 0:1]),
                tag=f"{pfx}ksTe{sq}{h}")
            prog.rd(o, pg[1]); prog.wr(o, KS_eo[h % 2][1])

        if "d" not in stages:
            continue
        for nb in range(NQB):
            dbl = nb % 2
            for i in range(2):
                src = q_scr[0][:].rearrange("c p t -> p c t")[i * 64:(i + 1) * 64, :, nb * BN:(nb + 1) * BN]
                op = prog.add("sd", lambda e, d=qld_eo[i][0].rearrange("p (c t) -> p c t", c=DC), s2=src: e.dma_start(d, s2),
                              tag=f"{pfx}ldq{sq}_{nb}_{i}", key=qld_eo[i][1].name)
                prog.rd(op, q_scr[1]); prog.wr(op, qld_eo[i][1])
            for tt in range(4):
                for c in range(DC):
                    mmgroup(prog, (z_ps[0][:, 2 * c:2 * c + 2], z_ps[1]),
                            [(qld_eo[i][0][:, c * BN + tt * 128: c * BN + (tt + 1) * 128],
                              KS_eo[i][0][:, 2 * c:2 * c + 2]) for i in range(2)],
                            bufs_rd=[qld_eo[0][1], qld_eo[1][1], KS_eo[0][1], KS_eo[1][1]],
                            tag=f"{pfx}z{sq}_{nb}_{tt}_{c}")
                o = prog.add("v", lambda e: e.tensor_scalar(zc[0], z_ps[0], eps_attn, None, ALU.add),
                             tag=f"{pfx}zeps{sq}_{nb}_{tt}")
                prog.rd(o, z_ps[1]); prog.wr(o, zc[1])
                o = prog.add("v", lambda e: e.reciprocal(zc[0], zc[0]), tag=f"{pfx}zrec{sq}_{nb}_{tt}")
                prog.rd(o, zc[1]); prog.wr(o, zc[1])
                for half in range(2 if "o" in stages else 0):
                    for h in range(half * 8, half * 8 + 8):
                        i = h % 2
                        c0 = h // 2
                        mmgroup(prog, (ot_ps[0][:, (h % 8) * 64:(h % 8 + 1) * 64], ot_ps[1]),
                                [(qld_eo[i][0][:, c0 * BN + tt * 128: c0 * BN + (tt + 1) * 128],
                                  kv_eo[i][0][:, c0 * 64:(c0 + 1) * 64])],
                                bufs_rd=[qld_eo[i][1], kv_eo[i][1]], tag=f"{pfx}o{sq}_{nb}_{tt}_{h}")
                    for h in range(half * 8, half * 8 + 8):
                        if "u" in stages:
                            o = prog.add("v", lambda e, h=h: e.tensor_scalar(
                                otok[0][:, h * 64:(h + 1) * 64], ot_ps[0][:, (h % 8) * 64:(h % 8 + 1) * 64],
                                zc[0][:, h:h + 1], None, ALU.mult),
                                tag=f"{pfx}od{sq}_{nb}_{tt}_{h}")
                        else:
                            o = prog.add("v", lambda e, h=h: e.tensor_copy(
                                otok[0][:, h * 64:(h + 1) * 64], ot_ps[0][:, (h % 8) * 64:(h % 8 + 1) * 64]),
                                tag=f"{pfx}od{sq}_{nb}_{tt}_{h}")
                        prog.rd(o, ot_ps[1], zc[1]); prog.wr(o, otok[1])
                for c in range(DC if "t" in stages else 0):
                    sl = at_ps[0][:, (c % 4) * 128:(c % 4 + 1) * 128]
                    to = prog.add("t", lambda e, c=c, sl=sl: e.transpose(
                        sl, otok[0][:, c * 128:(c + 1) * 128], identbf[0]),
                        tag=f"{pfx}tr{sq}_{nb}_{tt}_{c}")
                    prog.rd(to, otok[1], identbf[1]); prog.wr(to, at_ps[1])
                    eo = prog.add("v" if c % 2 == 0 else "s",
                                  lambda e, c=c, sl=sl, dbl=dbl, tt=tt:
                                  (e.tensor_copy if hasattr(e, "tensor_copy") else e.copy)(
                                      abat[dbl][0][:, c * BN + tt * 128: c * BN + (tt + 1) * 128], sl),
                                  tag=f"{pfx}tre{sq}_{nb}_{tt}_{c}")
                    prog.rd(eo, at_ps[1]); prog.wr(eo, abat[dbl][1])
            so = prog.add("sd", lambda e, d=a_scr[0][:].rearrange("c p t -> p c t")[:, :, sq * TSEQ + nb * BN: sq * TSEQ + (nb + 1) * BN],
                          s2=abat[dbl][0].rearrange("p (c t) -> p c t", c=DC): e.dma_start(d, s2),
                          tag=f"{pfx}sta{sq}_{nb}", key=abat[dbl][1].name)
            prog.rd(so, abat[dbl][1]); prog.wr(so, a_scr[1])

    if "e" in stages:
        emit_out_proj(nc, prog, pfx, a_scr, xin, xout, wd["wo"], bo_c, wsl, pp,
                      xbat, X8b)


def launch_attn_lin(x_shards, wq, wk, wv, wo, bo):
    consts = host_consts()
    nc = bass.Bass("TRN2", target_bir_lowering=False)
    xin = nc.dram_tensor("xin", [DC, 128, TC], F32, kind="ExternalInput")
    xout = nc.dram_tensor("xout", [DC, 128, TC], F32, kind="ExternalOutput")
    wd = {}
    for nm in ("wq", "wk", "wv", "wo"):
        wd[nm] = nc.dram_tensor(nm, [D, D], F32, kind="ExternalInput")
    wd["bo"] = nc.dram_tensor("bo", [D], F32, kind="ExternalInput")
    prog = Prog(nc, "al")
    cst = load_consts(nc, prog, consts, ["ones128", "onesf", "identbf"])
    build_attn_lin(nc, prog, xin, xout, wd, cst)
    prog.run()
    in_maps = []
    for c in range(NCORES):
        m = {"xin": to_fm(x_shards[c]), "wq": wq, "wk": wk, "wv": wv, "wo": wo, "bo": bo,
             "ones128": consts["ones128"], "onesf": consts["onesf"],
             "identbf": consts["identbf"]}
        in_maps.append(m)
    res = run_bass_kernel_spmd(nc, in_maps, core_ids=list(range(NCORES)))
    return [from_fm(r["xout"]) for r in res.results]



def build_attn_pa(nc, prog, xin, xout, wd, cst, pfx="pa", eps=1e-5, stages="abcd"):
    ones = cst["ones128"]
    identbf = cst["identbf"]
    SCL = float(64.0 ** -0.25)

    xd_scr = (dram(nc, f"{pfx}_xdscr", [DC, 128, TSEQ], BF16), Buf(f"{pfx}_xdscr"))
    a_scr = (dram(nc, f"{pfx}_ascr", [DC, 128, TC], BF16), Buf(f"{pfx}_ascr"))

    pp = [psum(nc, f"{pfx}_pp{i}", [128, BN]) for i in range(2)]
    sc_ps = psum(nc, f"{pfx}_scps", [128, 2048])
    sc_bk = [Buf(f"{pfx}_scb{i}") for i in range(4)]
    pt_ps = psum(nc, f"{pfx}_ptps", [128, BN], BF16)
    pv_ps = psum(nc, f"{pfx}_pvps", [128, BN])

    lnsc = mk_ln_scratch(nc, pfx, psA=pp[0], psB=pp[1])
    xbat = [sb(nc, f"{pfx}_xb0", [128, DC * BN], F32)] * 2
    X8b = [[(xbat[i][0][:, c * BN:(c + 1) * BN], xbat[i][1]) for c in range(DC)]
           for i in range(2)]
    XLb = [[sb(nc, f"{pfx}_xl{c}_{i}", [128, BN], BF16) for c in range(DC)] for i in range(2)]
    wsl = [sb(nc, f"{pfx}_wsl{i}", [128, D], BF16) for i in range(4)]
    xdb = [sb(nc, f"{pfx}_xdb0", [128, DC * BN], BF16)] * 2

    tok_raw = sb(nc, f"{pfx}_tokraw", [64, NH * 64], BF16)
    tokT = sb(nc, f"{pfx}_tokT", [64, NH * 64], BF16)
    tqT = sb(nc, f"{pfx}_tqT", [64, NH * 64], BF16)
    tvT = sb(nc, f"{pfx}_tvT", [64, NH * 64], BF16)

    bo_c = bias_cols(nc, prog, pfx, "bu", wd["b_unify"], pp)

    for nm, dstT in (("tokeys", tokT), ("toqueries", tqT), ("tovalues", tvT)):
        lo = prog.add("gd", lambda e, nm=nm: e.dma_start(
            tok_raw[0].rearrange("p (h k) -> p h k", h=NH),
            wd[nm][:].rearrange("h i k -> i h k")),
            tag=f"{pfx}ld{nm}", key=tok_raw[1].name)
        prog.wr(lo, tok_raw[1])
        for h in range(NH):
            sl = pt_ps[0][0:64, (h % 8) * 64:(h % 8 + 1) * 64]
            to = prog.add("t", lambda e, h=h, sl=sl: e.transpose(
                sl, tok_raw[0][:, h * 64:(h + 1) * 64], identbf[0][0:64, 0:64]),
                tag=f"{pfx}tT{nm}{h}")
            prog.rd(to, tok_raw[1], identbf[1]); prog.wr(to, pt_ps[1])
            eo = prog.add("v", lambda e, h=h, sl=sl, dstT=dstT: e.tensor_copy(
                dstT[0][:, h * 64:(h + 1) * 64], sl), tag=f"{pfx}tTe{nm}{h}")
            prog.rd(eo, pt_ps[1]); prog.wr(eo, dstT[1])

    for sq in range(2):
        if "b" not in stages:
            break
        for nb in range(NQB):
            dbl = nb % 2
            tok0 = sq * TSEQ + nb * BN
            src = xin[:].rearrange("c p t -> p c t")[:, :, tok0:tok0 + BN]
            op = prog.add("sd", lambda e, d=xbat[dbl][0].rearrange("p (c t) -> p c t", c=DC), s2=src: e.dma_start(d, s2),
                          tag=f"{pfx}ldx{sq}_{nb}", key=xbat[dbl][1].name)
            prog.wr(op, xbat[dbl][1])
            emit_ln(prog, X8b[dbl], XLb[dbl], lnsc, ones)
            for m in range(DC):
                slot = wsl[m % 4]
                load_wslice(nc, prog, slot, wd["W_down"], None, m, f"{pfx}lwd{sq}{nb}{m}")
                pg = pp[m % 2]
                mmgroup(prog, pg,
                        [(slot[0][:, kc * 128:(kc + 1) * 128], XLb[dbl][kc][0]) for kc in range(DC)],
                        bufs_rd=[slot[1]] + [XLb[dbl][kc][1] for kc in range(DC)],
                        tag=f"{pfx}pd{sq}_{nb}_{m}")
                o = prog.add("v", lambda e, d=xdb[dbl][0][:, m * BN:(m + 1) * BN], p2=pg: e.tensor_copy(d, p2[0]),
                             tag=f"{pfx}ed{sq}_{nb}_{m}")
                prog.rd(o, pg[1]); prog.wr(o, xdb[dbl][1])
            so = prog.add("sd", lambda e, d=xd_scr[0][:].rearrange("c p t -> p c t")[:, :, nb * BN:(nb + 1) * BN],
                          s2=xdb[dbl][0].rearrange("p (c t) -> p c t", c=DC): e.dma_start(d, s2),
                          tag=f"{pfx}stxd{sq}_{nb}", key=xdb[dbl][1].name)
            prog.rd(so, xdb[dbl][1]); prog.wr(so, xd_scr[1])

        if "c" not in stages:
            continue
        if sq == 0:
            scope_begin()
            xd_h = [sb(nc, f"{pfx}_xdh{i}", [64, TSEQ], BF16) for i in range(2)]
            q_h = [sb(nc, f"{pfx}_qh{i}", [64, TSEQ], BF16) for i in range(2)]
            k_h = [sb(nc, f"{pfx}_kh{i}", [64, TSEQ], BF16) for i in range(2)]
            v_h = [sb(nc, f"{pfx}_vh{i}", [128, NQT * 64], BF16) for i in range(2)]
            P_sb = [sb(nc, f"{pfx}_P{i}", [128, 2048], BF16) for i in range(4)]
            PT_sb = [sb(nc, f"{pfx}_PT{i}", [128, BN], BF16) for i in range(2)]
            a_hd = [sb(nc, f"{pfx}_ah{i}", [64, TSEQ], BF16) for i in range(2)]
            nmax = sb(nc, f"{pfx}_nmax", [128, 1], F32)
            cmax = sb(nc, f"{pfx}_cmax", [128, 4], F32)
            rsum = sb(nc, f"{pfx}_rsum", [128, 1], F32)
            rcp = [sb(nc, f"{pfx}_rcp{i}", [128, 1], F32) for i in range(4)]
            build_attn_pa.bufs = (xd_h, q_h, k_h, v_h, P_sb, PT_sb, a_hd, nmax, rsum, rcp, cmax)
        else:
            xd_h, q_h, k_h, v_h, P_sb, PT_sb, a_hd, nmax, rsum, rcp, cmax = build_attn_pa.bufs

        for h in range(NH):
            hb = h % 2
            xh, qh, kh, vh, ah = xd_h[hb], q_h[hb], k_h[hb], v_h[hb], a_hd[hb]
            c0 = h // 2
            r0 = (h % 2) * 64
            lo = prog.add("sd", lambda e, d=xh[0], s2=xd_scr[0][:][c0, r0:r0 + 64, :]: e.dma_start(d, s2),
                          tag=f"{pfx}ldxh{sq}_{h}", key=xh[1].name)
            prog.rd(lo, xd_scr[1]); prog.wr(lo, xh[1])
            for kb in range(4):
                pg = pp[kb % 2]
                mmgroup(prog, (pg[0][0:64, :], pg[1]),
                        [(tokT[0][:, h * 64:(h + 1) * 64], xh[0][:, kb * 512:(kb + 1) * 512])],
                        bufs_rd=[tokT[1], xh[1]], tag=f"{pfx}kk{sq}_{h}_{kb}")
                o = prog.add("v", lambda e, d=kh[0][:, kb * 512:(kb + 1) * 512], p2=pg: e.tensor_scalar(
                    d, p2[0][0:64, :], SCL, None, ALU.mult), tag=f"{pfx}ekk{sq}_{h}_{kb}")
                prog.rd(o, pg[1]); prog.wr(o, kh[1])
                pg = pp[(kb + 1) % 2]
                mmgroup(prog, (pg[0][0:64, :], pg[1]),
                        [(tqT[0][:, h * 64:(h + 1) * 64], xh[0][:, kb * 512:(kb + 1) * 512])],
                        bufs_rd=[tqT[1], xh[1]], tag=f"{pfx}qq{sq}_{h}_{kb}")
                o = prog.add("v", lambda e, d=qh[0][:, kb * 512:(kb + 1) * 512], p2=pg: e.tensor_scalar(
                    d, p2[0][0:64, :], SCL, None, ALU.mult), tag=f"{pfx}eqq{sq}_{h}_{kb}")
                prog.rd(o, pg[1]); prog.wr(o, qh[1])
            for half in range(2):
                pg = pp[half % 2]
                for tt in range(8):
                    gt = half * 8 + tt
                    mmgroup(prog, (pg[0][:, tt * 64:(tt + 1) * 64], pg[1]),
                            [(xh[0][:, gt * 128:(gt + 1) * 128], tvT[0][:, h * 64:(h + 1) * 64])],
                            bufs_rd=[xh[1], tvT[1]], tag=f"{pfx}vv{sq}_{h}_{gt}")
                o = prog.add("v", lambda e, d=vh[0][:, half * 512:(half + 1) * 512], p2=pg: e.tensor_copy(d, p2[0]),
                             tag=f"{pfx}evv{sq}_{h}_{half}")
                prog.rd(o, pg[1]); prog.wr(o, vh[1])

            for qb in range(NQB):
                for t4 in range(4):
                    i = qb * 4 + t4
                    for kb in range(4):
                        mmgroup(prog, (sc_ps[0][:, kb * 512:(kb + 1) * 512], sc_bk[kb]),
                                [(qh[0][:, i * 128:(i + 1) * 128], kh[0][:, kb * 512:(kb + 1) * 512])],
                                bufs_rd=[qh[1], kh[1]], tag=f"{pfx}sc{sq}_{h}_{i}_{kb}")
                        o = prog.add("v", lambda e, kb=kb: e.reduce_max(
                            cmax[0][:, kb:kb + 1], sc_ps[0][:, kb * 512:(kb + 1) * 512], axis=AX.X, negate=True),
                            tag=f"{pfx}cmax{sq}_{h}_{i}_{kb}")
                        prog.rd(o, sc_bk[kb]); prog.wr(o, cmax[1])
                    o = prog.add("v", lambda e: e.tensor_reduce(
                        nmax[0], cmax[0][:, 0:4], axis=AX.X, op=ALU.min),
                        tag=f"{pfx}max{sq}_{h}_{i}")
                    prog.rd(o, cmax[1]); prog.wr(o, nmax[1])
                    o = prog.add("s", lambda e, t4=t4: e.activation(
                        P_sb[t4][0][:], sc_ps[0][:], AF.Exp,
                        bias=nmax[0], accum_out=rsum[0]),
                        tag=f"{pfx}exp{sq}_{h}_{i}")
                    prog.rd(o, nmax[1], *sc_bk); prog.wr(o, P_sb[t4][1], rsum[1])
                    o = prog.add("v", lambda e, t4=t4: e.reciprocal(rcp[t4][0], rsum[0]),
                                 tag=f"{pfx}rcp{sq}_{h}_{i}")
                    prog.rd(o, rsum[1]); prog.wr(o, rcp[t4][1])
                    o = prog.add("v", lambda e, t4=t4: e.tensor_scalar(
                        P_sb[t4][0][:], P_sb[t4][0][:], rcp[t4][0], None, ALU.mult),
                        tag=f"{pfx}pscale{sq}_{h}_{i}")
                    prog.rd(o, rcp[t4][1], P_sb[t4][1]); prog.wr(o, P_sb[t4][1])

                for j in range(NQT):
                    ptb = PT_sb[j % 2]
                    for t4 in range(4):
                        to = prog.add("t", lambda e, j=j, t4=t4: e.transpose(
                            pt_ps[0][:, t4 * 128:(t4 + 1) * 128],
                            P_sb[t4][0][:, j * 128:(j + 1) * 128], identbf[0]),
                            tag=f"{pfx}tr{sq}_{h}_{qb}_{j}_{t4}")
                        prog.rd(to, P_sb[t4][1], identbf[1])
                        prog.wr(to, pt_ps[1])
                    eo = prog.add("v" if j % 2 == 0 else "s",
                                  lambda e, ptb=ptb: (e.tensor_copy if hasattr(e, "tensor_copy") else e.copy)(
                                      ptb[0][:], pt_ps[0][:]),
                                  tag=f"{pfx}pte{sq}_{h}_{qb}_{j}")
                    prog.rd(eo, pt_ps[1]); prog.wr(eo, ptb[1])
                    mmgroup(prog, (pv_ps[0][0:64, :], pv_ps[1]),
                            [(vh[0][:, j * 64:(j + 1) * 64], ptb[0][:])],
                            bufs_rd=[vh[1], ptb[1]], tag=f"{pfx}pv{sq}_{h}_{qb}_{j}",
                            start=(j == 0), stop=(j == NQT - 1))
                eo = prog.add("v", lambda e, qb=qb, ah=ah: e.tensor_copy(
                    ah[0][:, qb * 512:(qb + 1) * 512], pv_ps[0][0:64, :]),
                    tag=f"{pfx}pve{sq}_{h}_{qb}")
                prog.rd(eo, pv_ps[1]); prog.wr(eo, ah[1])

            so = prog.add("sd", lambda e, d=a_scr[0][:][c0, r0:r0 + 64, sq * TSEQ:(sq + 1) * TSEQ], s2=ah[0]: e.dma_start(d, s2),
                          tag=f"{pfx}sta{sq}_{h}", key=ah[1].name)
            prog.rd(so, ah[1]); prog.wr(so, a_scr[1])

    if "c" in stages:
        scope_end()
    if "d" in stages:
        emit_out_proj(nc, prog, pfx, a_scr, xin, xout, wd["W_unify"], bo_c, wsl, pp,
                      xbat, X8b)


def launch_attn_pa(x_shards, W_down, tokeys, toqueries, tovalues, W_unify, b_unify):
    consts = host_consts()
    nc = bass.Bass("TRN2", target_bir_lowering=False)
    xin = nc.dram_tensor("xin", [DC, 128, TC], F32, kind="ExternalInput")
    xout = nc.dram_tensor("xout", [DC, 128, TC], F32, kind="ExternalOutput")
    wd = {"W_down": nc.dram_tensor("W_down", [D, D], F32, kind="ExternalInput"),
          "W_unify": nc.dram_tensor("W_unify", [D, D], F32, kind="ExternalInput"),
          "b_unify": nc.dram_tensor("b_unify", [D], F32, kind="ExternalInput"),
          "tokeys": nc.dram_tensor("tokeys", [NH, 64, 64], F32, kind="ExternalInput"),
          "toqueries": nc.dram_tensor("toqueries", [NH, 64, 64], F32, kind="ExternalInput"),
          "tovalues": nc.dram_tensor("tovalues", [NH, 64, 64], F32, kind="ExternalInput")}
    prog = Prog(nc, "pa")
    cst = load_consts(nc, prog, consts, ["ones128", "identbf"])
    build_attn_pa(nc, prog, xin, xout, wd, cst)
    prog.run()
    in_maps = []
    for c in range(NCORES):
        m = {"xin": to_fm(x_shards[c]), "W_down": W_down, "W_unify": W_unify,
             "b_unify": b_unify, "tokeys": tokeys, "toqueries": toqueries,
             "tovalues": tovalues, "ones128": consts["ones128"],
             "identbf": consts["identbf"]}
        in_maps.append(m)
    res = run_bass_kernel_spmd(nc, in_maps, core_ids=list(range(NCORES)))
    return [from_fm(r["xout"]) for r in res.results]



KVR = 256
QR = 512
RD = 32


def build_attn_mla(nc, prog, xin, xout, wd, cst, pfx="ml", eps=1e-5):
    eps_rms = 1e-6
    ones = cst["ones128"]
    identbf = cst["identbf"]
    causal = cst["causal128"]
    rcos = cst["ropecos"]
    rsin = cst["ropesin"]
    rRT = cst["ropeRT"]
    SCLQ = float(1.0 / np.sqrt(96.0))

    qc_scr = (dram(nc, f"{pfx}_qcscr", [4, 128, TSEQ], BF16), Buf(f"{pfx}_qcscr"))
    qd_scr = (dram(nc, f"{pfx}_qdscr", [DC, 128, TSEQ], BF16), Buf(f"{pfx}_qdscr"))
    qr_scr = (dram(nc, f"{pfx}_qrscr", [4, 128, TSEQ], BF16), Buf(f"{pfx}_qrscr"))
    kc_scr = (dram(nc, f"{pfx}_kcscr", [DC, 128, TSEQ], BF16), Buf(f"{pfx}_kcscr"))
    kr_scr = (dram(nc, f"{pfx}_krscr", [4, 128, TSEQ], BF16), Buf(f"{pfx}_krscr"))
    v_scr = (dram(nc, f"{pfx}_vscr", [NQT, 128, D], BF16), Buf(f"{pfx}_vscr"))
    a_scr = (dram(nc, f"{pfx}_ascr", [DC, 128, TC], BF16), Buf(f"{pfx}_ascr"))

    pp = [psum(nc, f"{pfx}_pp{i}", [128, BN]) for i in range(2)]
    sc_ps = psum(nc, f"{pfx}_scps", [128, 2048])
    sc_bk = [Buf(f"{pfx}_scb{i}") for i in range(4)]
    pt_ps = psum(nc, f"{pfx}_ptps", [128, BN], BF16)
    pv_ps = psum(nc, f"{pfx}_pvps", [128, BN])

    lnsc = mk_ln_scratch(nc, pfx, psA=pp[0], psB=pp[1])
    xbat = [sb(nc, f"{pfx}_xb0", [128, DC * BN], F32)] * 2
    X8b = [[(xbat[i][0][:, c * BN:(c + 1) * BN], xbat[i][1]) for c in range(DC)]
           for i in range(2)]
    XLb = [[sb(nc, f"{pfx}_xl{c}_{i}", [128, BN], BF16) for c in range(DC)] for i in range(2)]
    wsl = [sb(nc, f"{pfx}_wsl{i}", [128, D], BF16) for i in range(4)]
    wvd_t = sb(nc, f"{pfx}_wvdt", [128, 2 * D], BF16)
    kvc_raw = sb(nc, f"{pfx}_kvcr", [128, 2 * BN], F32)
    kvc_sq = sb(nc, f"{pfx}_kvcsq", [128, 2 * BN], BF16)
    rmsbc = sb(nc, f"{pfx}_rmsbc", [128, BN], F32)
    kvn = [sb(nc, f"{pfx}_kvn{c2}", [128, BN], BF16) for c2 in range(2)]
    qcb = sb(nc, f"{pfx}_qcb", [128, 4 * BN], BF16)
    outb = sb(nc, f"{pfx}_outb", [128, DC * BN], BF16)
    vb = sb(nc, f"{pfx}_vb", [128, D], BF16)
    rt1 = sb(nc, f"{pfx}_rt1", [128, BN], F32)
    rt2 = sb(nc, f"{pfx}_rt2", [128, BN], F32)
    rraw = sb(nc, f"{pfx}_rraw", [128, BN], BF16)

    gkv_c = bias_cols(nc, prog, pfx, "gkv", wd["g_kv"], pp, dim=KVR)

    lo = prog.add("gd", lambda e: e.dma_start(
        wvd_t[0], wd["W_vdec"][:].rearrange("(kc p) f -> p kc f", p=128)),
        tag=f"{pfx}ldwvd", key=wvd_t[1].name)
    prog.wr(lo, wvd_t[1])

    def stream_w(w_key, K_, m, slot, tag):
        src = wd[w_key][:].rearrange("(kc p) f -> p kc f", p=128)[:, :, m * 128:(m + 1) * 128]
        lo = prog.add("gd", lambda e, d=slot[0][:, 0:K_]: e.dma_start(
            d.rearrange("p (kc f) -> p kc f", f=128), src), tag=tag, key=slot[1].name)
        prog.wr(lo, slot[1])
        return lo

    def emit_rope(dbl, m, pg, dst, dstbuf, tok0, scale, tag):
        o = prog.add("v", lambda e: e.tensor_copy(rraw[0], pg[0]), tag=f"{pfx}rr{tag}")
        prog.rd(o, pg[1]); prog.wr(o, rraw[1])
        mmgroup(prog, pg, [(rRT[0], rraw[0])], bufs_rd=[rRT[1], rraw[1]],
                tag=f"{pfx}rot{tag}")
        o = prog.add("v", lambda e: e.scalar_tensor_tensor(
            rt1[0], rraw[0], scale, rcos[0][:, tok0:tok0 + BN], ALU.mult, ALU.mult),
            tag=f"{pfx}rc{tag}")
        prog.rd(o, rraw[1], rcos[1]); prog.wr(o, rt1[1])
        o = prog.add("v", lambda e: e.scalar_tensor_tensor(
            rt2[0], pg[0], scale, rsin[0][:, tok0:tok0 + BN], ALU.mult, ALU.mult),
            tag=f"{pfx}rs{tag}")
        prog.rd(o, pg[1], rsin[1]); prog.wr(o, rt2[1])
        o = prog.add("v", lambda e, dst=dst: e.tensor_add(dst, rt1[0], rt2[0]), tag=f"{pfx}ra{tag}")
        prog.rd(o, rt1[1], rt2[1]); prog.wr(o, dstbuf)

    for sq in range(2):
        for nb in range(NQB):
            dbl = nb % 2
            tok0s = sq * TSEQ + nb * BN
            tok0 = nb * BN
            src = xin[:].rearrange("c p t -> p c t")[:, :, tok0s:tok0s + BN]
            op = prog.add("sd", lambda e, d=xbat[dbl][0].rearrange("p (c t) -> p c t", c=DC), s2=src: e.dma_start(d, s2),
                          tag=f"{pfx}ldx{sq}_{nb}", key=xbat[dbl][1].name)
            prog.wr(op, xbat[dbl][1])
            emit_ln(prog, X8b[dbl], XLb[dbl], lnsc, ones)

            for m in range(2):
                slot = wsl[m % 4]
                stream_w("W_kv", D, m, slot, f"{pfx}lkv{sq}{nb}{m}")
                pg = pp[m % 2]
                mmgroup(prog, pg,
                        [(slot[0][:, kc * 128:(kc + 1) * 128], XLb[dbl][kc][0]) for kc in range(DC)],
                        bufs_rd=[slot[1]] + [XLb[dbl][kc][1] for kc in range(DC)],
                        tag=f"{pfx}kvc{sq}_{nb}_{m}")
                o = prog.add("v", lambda e, m=m, pg=pg: e.tensor_copy(kvc_raw[0][:, m * BN:(m + 1) * BN], pg[0]),
                             tag=f"{pfx}ekvc{sq}_{nb}_{m}")
                prog.rd(o, pg[1]); prog.wr(o, kvc_raw[1])
                o = prog.add("s", lambda e, m=m: e.activation(kvc_sq[0][:, m * BN:(m + 1) * BN],
                                                              kvc_raw[0][:, m * BN:(m + 1) * BN], AF.Square),
                             tag=f"{pfx}sqkv{sq}_{nb}_{m}")
                prog.rd(o, kvc_raw[1]); prog.wr(o, kvc_sq[1])
            mmgroup(prog, pp[0], [(ones[0], kvc_sq[0][:, m2 * BN:(m2 + 1) * BN]) for m2 in range(2)],
                    bufs_rd=[ones[1], kvc_sq[1]], tag=f"{pfx}ssq{sq}_{nb}")
            o = prog.add("v", lambda e: e.tensor_scalar(rmsbc[0], pp[0][0], 1.0 / KVR, eps_rms, ALU.mult, ALU.add),
                         tag=f"{pfx}rms1{sq}_{nb}")
            prog.rd(o, pp[0][1]); prog.wr(o, rmsbc[1])
            o = prog.add("v", lambda e: e.reciprocal(rmsbc[0], rmsbc[0]), tag=f"{pfx}rms2{sq}_{nb}")
            prog.rd(o, rmsbc[1]); prog.wr(o, rmsbc[1])
            o = prog.add("s", lambda e: e.activation(rmsbc[0], rmsbc[0], AF.Sqrt), tag=f"{pfx}rms3{sq}_{nb}")
            prog.rd(o, rmsbc[1]); prog.wr(o, rmsbc[1])
            for m in range(2):
                o = prog.add("v", lambda e, m=m: e.tensor_mul(rt1[0], kvc_raw[0][:, m * BN:(m + 1) * BN], rmsbc[0]),
                             tag=f"{pfx}kvn1{sq}_{nb}_{m}")
                prog.rd(o, kvc_raw[1], rmsbc[1]); prog.wr(o, rt1[1])
                o = prog.add("v", lambda e, m=m: e.tensor_scalar(kvn[m][0], rt1[0], gkv_c[0][:, m:m + 1], None, ALU.mult),
                             tag=f"{pfx}kvn2{sq}_{nb}_{m}")
                prog.rd(o, rt1[1], gkv_c[1]); prog.wr(o, kvn[m][1])

            for m in range(4):
                slot = wsl[m % 4]
                stream_w("W_q", D, m, slot, f"{pfx}lwq{sq}{nb}{m}")
                pg = pp[m % 2]
                mmgroup(prog, pg,
                        [(slot[0][:, kc * 128:(kc + 1) * 128], XLb[dbl][kc][0]) for kc in range(DC)],
                        bufs_rd=[slot[1]] + [XLb[dbl][kc][1] for kc in range(DC)],
                        tag=f"{pfx}qc{sq}_{nb}_{m}")
                o = prog.add("v", lambda e, m=m, pg=pg: e.tensor_copy(qcb[0][:, m * BN:(m + 1) * BN], pg[0]),
                             tag=f"{pfx}eqc{sq}_{nb}_{m}")
                prog.rd(o, pg[1]); prog.wr(o, qcb[1])
            so = prog.add("sd", lambda e, d=qc_scr[0][:].rearrange("c p t -> p c t")[:, :, tok0:tok0 + BN],
                          s2=qcb[0].rearrange("p (c t) -> p c t", c=4): e.dma_start(d, s2),
                          tag=f"{pfx}stqc{sq}_{nb}", key=qcb[1].name)
            prog.rd(so, qcb[1]); prog.wr(so, qc_scr[1])

            for m in range(DC):
                slot = wsl[m % 4]
                stream_w("W_kdec", KVR, m, slot, f"{pfx}lkd{sq}{nb}{m}")
                pg = pp[m % 2]
                mmgroup(prog, pg,
                        [(slot[0][:, c2 * 128:(c2 + 1) * 128], kvn[c2][0]) for c2 in range(2)],
                        bufs_rd=[slot[1], kvn[0][1], kvn[1][1]],
                        tag=f"{pfx}kd{sq}_{nb}_{m}")
                o = prog.add("v", lambda e, m=m, pg=pg: e.tensor_copy(outb[0][:, m * BN:(m + 1) * BN], pg[0]),
                             tag=f"{pfx}ekd{sq}_{nb}_{m}")
                prog.rd(o, pg[1]); prog.wr(o, outb[1])
            so = prog.add("sd", lambda e, d=kc_scr[0][:].rearrange("c p t -> p c t")[:, :, tok0:tok0 + BN],
                          s2=outb[0].rearrange("p (c t) -> p c t", c=DC): e.dma_start(d, s2),
                          tag=f"{pfx}stkc{sq}_{nb}", key=outb[1].name)
            prog.rd(so, outb[1]); prog.wr(so, kc_scr[1])

            for tt in range(4):
                for fh in range(2):
                    pg = pp[fh % 2]
                    mmgroup(prog, pg,
                            [(kvn[c2][0][:, tt * 128:(tt + 1) * 128],
                              wvd_t[0][:, c2 * D + fh * 512: c2 * D + (fh + 1) * 512]) for c2 in range(2)],
                            bufs_rd=[wvd_t[1], kvn[0][1], kvn[1][1]],
                            tag=f"{pfx}pv{sq}_{nb}_{tt}_{fh}")
                    o = prog.add("v", lambda e, d=vb[0][:, fh * 512:(fh + 1) * 512], p2=pg: e.tensor_copy(d, p2[0]),
                                 tag=f"{pfx}ev{sq}_{nb}_{tt}_{fh}")
                    prog.rd(o, pg[1]); prog.wr(o, vb[1])
                so = prog.add("sd", lambda e, d=v_scr[0][:][nb * 4 + tt], s2=vb[0]: e.dma_start(d, s2),
                              tag=f"{pfx}stv{sq}_{nb}_{tt}", key=vb[1].name)
                prog.rd(so, vb[1]); prog.wr(so, v_scr[1])

            for m in range(4):
                slot = wsl[m % 4]
                stream_w("W_krope", D, m, slot, f"{pfx}lkr{sq}{nb}{m}")
                pg = pp[m % 2]
                mmgroup(prog, pg,
                        [(slot[0][:, kc * 128:(kc + 1) * 128], XLb[dbl][kc][0]) for kc in range(DC)],
                        bufs_rd=[slot[1]] + [XLb[dbl][kc][1] for kc in range(DC)],
                        tag=f"{pfx}kr{sq}_{nb}_{m}")
                emit_rope(dbl, m, pg, outb[0][:, m * BN:(m + 1) * BN], outb[1], tok0, 1.0, f"kr{sq}_{nb}_{m}")
            so = prog.add("sd", lambda e, d=kr_scr[0][:].rearrange("c p t -> p c t")[:, :, tok0:tok0 + BN],
                          s2=outb[0][:, 0:4 * BN].rearrange("p (c t) -> p c t", c=4): e.dma_start(d, s2),
                          tag=f"{pfx}stkr{sq}_{nb}", key=outb[1].name)
            prog.rd(so, outb[1]); prog.wr(so, kr_scr[1])

            for m in range(4):
                slot = wsl[m % 4]
                stream_w("W_qrope", QR, m, slot, f"{pfx}lqr{sq}{nb}{m}")
                pg = pp[m % 2]
                mmgroup(prog, pg,
                        [(slot[0][:, c4 * 128:(c4 + 1) * 128], qcb[0][:, c4 * BN:(c4 + 1) * BN]) for c4 in range(4)],
                        bufs_rd=[slot[1], qcb[1]],
                        tag=f"{pfx}qr{sq}_{nb}_{m}")
                emit_rope(dbl, m, pg, outb[0][:, m * BN:(m + 1) * BN], outb[1], tok0, SCLQ, f"qr{sq}_{nb}_{m}")
            so = prog.add("sd", lambda e, d=qr_scr[0][:].rearrange("c p t -> p c t")[:, :, tok0:tok0 + BN],
                          s2=outb[0][:, 0:4 * BN].rearrange("p (c t) -> p c t", c=4): e.dma_start(d, s2),
                          tag=f"{pfx}stqr{sq}_{nb}", key=outb[1].name)
            prog.rd(so, outb[1]); prog.wr(so, qr_scr[1])

            for m in range(DC):
                slot = wsl[m % 4]
                stream_w("W_qdec", QR, m, slot, f"{pfx}lqd{sq}{nb}{m}")
                pg = pp[m % 2]
                mmgroup(prog, pg,
                        [(slot[0][:, c4 * 128:(c4 + 1) * 128], qcb[0][:, c4 * BN:(c4 + 1) * BN]) for c4 in range(4)],
                        bufs_rd=[slot[1], qcb[1]],
                        tag=f"{pfx}qd{sq}_{nb}_{m}")
                o = prog.add("v", lambda e, m=m, pg=pg: e.tensor_scalar(
                    outb[0][:, m * BN:(m + 1) * BN], pg[0], SCLQ, None, ALU.mult),
                    tag=f"{pfx}eqd{sq}_{nb}_{m}")
                prog.rd(o, pg[1]); prog.wr(o, outb[1])
            so = prog.add("sd", lambda e, d=qd_scr[0][:].rearrange("c p t -> p c t")[:, :, tok0:tok0 + BN],
                          s2=outb[0].rearrange("p (c t) -> p c t", c=DC): e.dma_start(d, s2),
                          tag=f"{pfx}stqd{sq}_{nb}", key=outb[1].name)
            prog.rd(so, outb[1]); prog.wr(so, qd_scr[1])

        if sq == 0:
            scope_begin()
            qc_h = [sb(nc, f"{pfx}_qch{i}", [64, TSEQ], BF16) for i in range(2)]
            qr_h = [sb(nc, f"{pfx}_qrh{i}", [32, TSEQ], BF16) for i in range(2)]
            kc_h = [sb(nc, f"{pfx}_kch{i}", [64, TSEQ], BF16) for i in range(2)]
            kr_h = [sb(nc, f"{pfx}_krh{i}", [32, TSEQ], BF16) for i in range(2)]
            v_hp = [sb(nc, f"{pfx}_vhp{i}", [128, NQT * 128], BF16) for i in range(2)]
            P_sb = [sb(nc, f"{pfx}_P{i}", [128, 2048], BF16) for i in range(4)]
            PT_sb = [sb(nc, f"{pfx}_PT{i}", [128, BN], BF16) for i in range(2)]
            a_hd = [sb(nc, f"{pfx}_ah{i}", [64, TSEQ], BF16) for i in range(2)]
            nmax = sb(nc, f"{pfx}_nmax", [128, 1], F32)
            cmax = sb(nc, f"{pfx}_cmax", [128, 4], F32)
            rsum = sb(nc, f"{pfx}_rsum", [128, 1], F32)
            rcp = [sb(nc, f"{pfx}_rcp{i}", [128, 1], F32) for i in range(4)]
            build_attn_mla.bufs = (qc_h, qr_h, kc_h, kr_h, v_hp, P_sb, PT_sb, a_hd, nmax, rsum, rcp, cmax)
        else:
            qc_h, qr_h, kc_h, kr_h, v_hp, P_sb, PT_sb, a_hd, nmax, rsum, rcp, cmax = build_attn_mla.bufs

        for h in range(NH):
            hb = h % 2
            qch, qrh, kch, krh, ah = qc_h[hb], qr_h[hb], kc_h[hb], kr_h[hb], a_hd[hb]
            c0 = h // 2
            r0 = (h % 2) * 64
            c4 = h // 4
            r4 = (h % 4) * 32
            for t, scr, rr, nrow in ((qch, qd_scr, r0, 64), (kch, kc_scr, r0, 64)):
                lo = prog.add("sd", lambda e, d=t[0], s2=scr[0][:][c0, rr:rr + nrow, :]: e.dma_start(d, s2),
                              tag=f"{pfx}ldh{sq}_{h}_{t[1].name}", key=t[1].name)
                prog.rd(lo, scr[1]); prog.wr(lo, t[1])
            for t, scr in ((qrh, qr_scr), (krh, kr_scr)):
                lo = prog.add("sd", lambda e, d=t[0], s2=scr[0][:][c4, r4:r4 + 32, :]: e.dma_start(d, s2),
                              tag=f"{pfx}ldh{sq}_{h}_{t[1].name}", key=t[1].name)
                prog.rd(lo, scr[1]); prog.wr(lo, t[1])
            if h % 2 == 0:
                vhp = v_hp[(h // 2) % 2]
                lo = prog.add("gd", lambda e, d=vhp[0].rearrange("p (t f) -> p t f", t=NQT),
                              s2=v_scr[0][:].rearrange("t p f -> p t f")[:, :, h * 64:(h + 2) * 64]: e.dma_start(d, s2),
                              tag=f"{pfx}ldvh{sq}_{h}", key=vhp[1].name)
                prog.rd(lo, v_scr[1]); prog.wr(lo, vhp[1])
            else:
                vhp = v_hp[(h // 2) % 2]
            vcol = (h % 2) * 64

            for qb in range(NQB):
                for t4 in range(4):
                    i = qb * 4 + t4
                    valid = (i + 1) * 128
                    nch = (valid + 511) // 512
                    for kb in range(nch):
                        w = min(512, valid - kb * 512)
                        mmgroup(prog, (sc_ps[0][:, kb * 512: kb * 512 + w], sc_bk[kb]),
                                [(qch[0][:, i * 128:(i + 1) * 128], kch[0][:, kb * 512: kb * 512 + w]),
                                 (qrh[0][:, i * 128:(i + 1) * 128], krh[0][:, kb * 512: kb * 512 + w])],
                                bufs_rd=[qch[1], kch[1], qrh[1], krh[1]],
                                tag=f"{pfx}sc{sq}_{h}_{i}_{kb}")
                        if kb == nch - 1:
                            o = prog.add("v", lambda e, i=i, valid=valid: e.tensor_add(
                                sc_ps[0][:, i * 128:valid], sc_ps[0][:, i * 128:valid], causal[0]),
                                tag=f"{pfx}mask{sq}_{h}_{i}")
                            prog.rd(o, causal[1], sc_bk[kb]); prog.wr(o, sc_bk[kb])
                        o = prog.add("v", lambda e, kb=kb, w=w: e.reduce_max(
                            cmax[0][:, kb:kb + 1], sc_ps[0][:, kb * 512: kb * 512 + w], axis=AX.X, negate=True),
                            tag=f"{pfx}cmax{sq}_{h}_{i}_{kb}")
                        prog.rd(o, sc_bk[kb]); prog.wr(o, cmax[1])
                    o = prog.add("v", lambda e, nch=nch: e.tensor_reduce(
                        nmax[0], cmax[0][:, 0:nch], axis=AX.X, op=ALU.min),
                        tag=f"{pfx}max{sq}_{h}_{i}")
                    prog.rd(o, cmax[1]); prog.wr(o, nmax[1])
                    o = prog.add("s", lambda e, t4=t4, valid=valid: e.activation(
                        P_sb[t4][0][:, 0:valid], sc_ps[0][:, 0:valid], AF.Exp,
                        bias=nmax[0], accum_out=rsum[0]),
                        tag=f"{pfx}exp{sq}_{h}_{i}")
                    prog.rd(o, nmax[1], *sc_bk[0:nch]); prog.wr(o, P_sb[t4][1], rsum[1])
                    o = prog.add("v", lambda e, t4=t4: e.reciprocal(rcp[t4][0], rsum[0]),
                                 tag=f"{pfx}rcp{sq}_{h}_{i}")
                    prog.rd(o, rsum[1]); prog.wr(o, rcp[t4][1])
                    o = prog.add("v", lambda e, t4=t4, valid=valid: e.tensor_scalar(
                        P_sb[t4][0][:, 0:valid], P_sb[t4][0][:, 0:valid], rcp[t4][0], None, ALU.mult),
                        tag=f"{pfx}pscale{sq}_{h}_{i}")
                    prog.rd(o, rcp[t4][1], P_sb[t4][1]); prog.wr(o, P_sb[t4][1])

                nj = (qb + 1) * 4
                for j in range(nj):
                    ptb = PT_sb[j % 2]
                    t4min = max(0, j - qb * 4)
                    for t4 in range(t4min, 4):
                        to = prog.add("t", lambda e, j=j, t4=t4: e.transpose(
                            pt_ps[0][:, t4 * 128:(t4 + 1) * 128],
                            P_sb[t4][0][:, j * 128:(j + 1) * 128], identbf[0]),
                            tag=f"{pfx}tr{sq}_{h}_{qb}_{j}_{t4}")
                        prog.rd(to, P_sb[t4][1], identbf[1])
                        prog.wr(to, pt_ps[1])
                    eo = prog.add("v" if j % 2 == 0 else "s",
                                  lambda e, t4min=t4min, ptb=ptb: (e.tensor_copy if hasattr(e, "tensor_copy") else e.copy)(
                                      ptb[0][:, t4min * 128:512], pt_ps[0][:, t4min * 128:512]),
                                  tag=f"{pfx}pte{sq}_{h}_{qb}_{j}")
                    prog.rd(eo, pt_ps[1]); prog.wr(eo, ptb[1])
                    mmgroup(prog, (pv_ps[0][0:64, t4min * 128:512], pv_ps[1]),
                            [(vhp[0][:, j * 128 + vcol: j * 128 + vcol + 64], ptb[0][:, t4min * 128:512])],
                            bufs_rd=[vhp[1], ptb[1]], tag=f"{pfx}pv{sq}_{h}_{qb}_{j}",
                            start=(j == 0), stop=(j == nj - 1))
                eo = prog.add("v", lambda e, qb=qb, ah=ah: e.tensor_copy(
                    ah[0][:, qb * 512:(qb + 1) * 512], pv_ps[0][0:64, :]),
                    tag=f"{pfx}pve{sq}_{h}_{qb}")
                prog.rd(eo, pv_ps[1]); prog.wr(eo, ah[1])

            so = prog.add("sd", lambda e, d=a_scr[0][:][c0, r0:r0 + 64, sq * TSEQ:(sq + 1) * TSEQ], s2=ah[0]: e.dma_start(d, s2),
                          tag=f"{pfx}sta{sq}_{h}", key=ah[1].name)
            prog.rd(so, ah[1]); prog.wr(so, a_scr[1])

    scope_end()
    emit_out_proj(nc, prog, pfx, a_scr, xin, xout, wd["W_o"], None, wsl, pp,
                  xbat, X8b)


def launch_attn_mla(x_shards, wdn):
    consts = host_consts()
    nc = bass.Bass("TRN2", target_bir_lowering=False)
    xin = nc.dram_tensor("xin", [DC, 128, TC], F32, kind="ExternalInput")
    xout = nc.dram_tensor("xout", [DC, 128, TC], F32, kind="ExternalOutput")
    shp = {"W_kv": [D, KVR], "g_kv": [KVR], "W_kdec": [KVR, D], "W_vdec": [KVR, D],
           "W_q": [D, QR], "W_qdec": [QR, D], "W_krope": [D, QR], "W_qrope": [QR, QR],
           "W_o": [D, D]}
    wd = {k: nc.dram_tensor(k, v, F32, kind="ExternalInput") for k, v in shp.items()}
    prog = Prog(nc, "ml")
    names = ["ones128", "identbf", "causal128", "ropecos", "ropesin", "ropeRT"]
    cst = load_consts(nc, prog, consts, names)
    build_attn_mla(nc, prog, xin, xout, wd, cst)
    prog.run()
    in_maps = []
    for c in range(NCORES):
        m = {"xin": to_fm(x_shards[c])}
        m.update(wdn)
        for k in names:
            m[k] = consts[k]
        in_maps.append(m)
    res = run_bass_kernel_spmd(nc, in_maps, core_ids=list(range(NCORES)))
    return [from_fm(r["xout"]) for r in res.results]



NEXP = 8
CAP = 9831
NTILES = 256


def build_moe_logits(nc, prog, xin, lg_out, wg_d, bg_d, cst, pfx="mg"):
    ones = cst["ones128"]
    onesf = cst["onesf"]
    pp = [psum(nc, f"{pfx}_pp{i}", [128, BN]) for i in range(2)]
    lg_ps = psum(nc, f"{pfx}_lgps", [8, BN])
    cs_ps = psum(nc, f"{pfx}_csps", [8, 1])
    lnsc = mk_ln_scratch(nc, pfx, psA=pp[0], psB=pp[1])
    xbat = [sb(nc, f"{pfx}_xb0", [128, DC * BN], F32)] * 2
    X8b = [[(xbat[i][0][:, c * BN:(c + 1) * BN], xbat[i][1]) for c in range(DC)]
           for i in range(2)]
    wgs = sb(nc, f"{pfx}_wgs", [128, DC * NEXP], F32)
    onescol = sb(nc, f"{pfx}_onescol", [128, 1], F32)
    ncs = sb(nc, f"{pfx}_ncs", [8, 1], F32)
    bgrow = sb(nc, f"{pfx}_bgrow", [1, NEXP], F32)
    bgcol = sb(nc, f"{pfx}_bgcol", [8, 1], F32)
    bg_ps = psum(nc, f"{pfx}_bgps", [8, 1])
    lg_sb = sb(nc, f"{pfx}_lgsb", [8, TC], F32)
    t2 = sb(nc, f"{pfx}_t2", [8, BN], F32)

    lo = prog.add("sd", lambda e: e.dma_start(
        wgs[0].rearrange("p (kc e) -> p kc e", e=NEXP),
        wg_d[:].rearrange("(kc p) e -> p kc e", p=128)), tag=f"{pfx}ldwg", key=wgs[1].name)
    prog.wr(lo, wgs[1])
    lo = prog.add("sd", lambda e: e.dma_start(bgrow[0], bg_d[:].rearrange("(o e) -> o e", o=1)),
                  tag=f"{pfx}ldbg", key=bgrow[1].name)
    prog.wr(lo, bgrow[1])
    o = prog.add("v", lambda e: e.memset(onescol[0], 1.0), tag=f"{pfx}onec")
    prog.wr(o, onescol[1])
    mmgroup(prog, cs_ps, [(wgs[0][:, kc * NEXP:(kc + 1) * NEXP], onescol[0]) for kc in range(DC)],
            bufs_rd=[wgs[1], onescol[1]], tag=f"{pfx}cs")
    o = prog.add("v", lambda e: e.tensor_scalar(ncs[0], cs_ps[0], -1.0, None, ALU.mult), tag=f"{pfx}ncs")
    prog.rd(o, cs_ps[1]); prog.wr(o, ncs[1])
    mmgroup(prog, bg_ps, [(bgrow[0], onescol[0][0:1, :])], bufs_rd=[bgrow[1], onescol[1]], tag=f"{pfx}bgT")
    o = prog.add("v", lambda e: e.tensor_copy(bgcol[0], bg_ps[0]), tag=f"{pfx}bgc")
    prog.rd(o, bg_ps[1]); prog.wr(o, bgcol[1])

    for b in range(NB):
        dbl = b % 2
        src = xin[:].rearrange("c p t -> p c t")[:, :, b * BN:(b + 1) * BN]
        op = prog.add("sd", lambda e, d=xbat[dbl][0].rearrange("p (c t) -> p c t", c=DC), s2=src: e.dma_start(d, s2),
                      tag=f"{pfx}ldx{b}", key=xbat[dbl][1].name)
        prog.wr(op, xbat[dbl][1])
        emit_ln(prog, X8b[dbl], None, lnsc, ones, norm=False)
        mmgroup(prog, lg_ps, [(wgs[0][:, kc * NEXP:(kc + 1) * NEXP], X8b[dbl][kc][0]) for kc in range(DC)],
                bufs_rd=[wgs[1], xbat[dbl][1]], tag=f"{pfx}raw{b}")
        o = prog.add("v", lambda e: e.tensor_mul(t2[0], lg_ps[0], lnsc["rstd"][0][0:8, :]), tag=f"{pfx}l1{b}")
        prog.rd(o, lg_ps[1], lnsc["rstd"][1]); prog.wr(o, t2[1])
        o = prog.add("v", lambda e: e.scalar_tensor_tensor(t2[0], lnsc["mr"][0][0:8, :], ncs[0], t2[0], ALU.mult, ALU.add),
                     tag=f"{pfx}l2{b}")
        prog.rd(o, lnsc["mr"][1], ncs[1], t2[1]); prog.wr(o, t2[1])
        o = prog.add("v", lambda e, b=b: e.tensor_scalar(lg_sb[0][:, b * BN:(b + 1) * BN], t2[0], bgcol[0], None, ALU.add),
                     tag=f"{pfx}l3{b}")
        prog.rd(o, t2[1], bgcol[1]); prog.wr(o, lg_sb[1])
    so = prog.add("sd", lambda e: e.dma_start(lg_out[:], lg_sb[0]), tag=f"{pfx}stlg", key=lg_sb[1].name)
    prog.rd(so, lg_sb[1])


def build_moe_gate(nc, prog, lgt_d, w_out, cst, pfx="gt"):
    ones = cst["ones128"]
    onesf = cst["onesf"]
    iota = cst["iota8t"]
    upst = cst["uppstrict"]
    BIG = 1e9

    pp = [psum(nc, f"{pfx}_pp{i}", [128, BN]) for i in range(2)]
    ct_ps = psum(nc, f"{pfx}_ctps", [1, BN])
    lg = [sb(nc, f"{pfx}_lg{i}", [128, BN], F32) for i in range(2)]
    OHa = [sb(nc, f"{pfx}_oh{k}", [128, 4 * BN], BF16) for k in range(2)]
    gv = [sb(nc, f"{pfx}_gv{k}", [128, NTILES], F32) for k in range(2)]
    m1 = sb(nc, f"{pfx}_m1", [128, 64], F32)
    m2 = sb(nc, f"{pfx}_m2", [128, 64], F32)
    idx = sb(nc, f"{pfx}_idx", [128, 64], F32)
    tt_ = sb(nc, f"{pfx}_tt", [128, BN], F32)
    msk = sb(nc, f"{pfx}_msk", [128, BN], F32)
    Pin = sb(nc, f"{pfx}_Pin", [128, 8 * BN], F32)
    crow = [sb(nc, f"{pfx}_crow{i}", [1, 8 * BN], F32) for i in range(2)]
    orow = sb(nc, f"{pfx}_orow", [1, 8 * BN], F32)
    pos = sb(nc, f"{pfx}_pos", [128, BN], F32)
    pat = sb(nc, f"{pfx}_pat", [128, 64], F32)
    keep = sb(nc, f"{pfx}_keep", [128, 64], F32)
    ck = sb(nc, f"{pfx}_ck", [128, 64], F32)
    w_sb = sb(nc, f"{pfx}_wsb", [128, 4 * BN], F32)

    def r3(ap, e=8):
        return ap.rearrange("p (g e) -> p g e", e=e)

    def bc(a, b):
        return bass.broadcast_tensor_aps(a, b)

    for bi in range(4):
        dbl = bi % 2
        lgb = lg[dbl]
        op = prog.add("sd", lambda e, d=lgb[0].rearrange("p (g e) -> p g e", e=8),
                      s2=lgt_d[:][:, bi * 64:(bi + 1) * 64, :]: e.dma_start(d, s2),
                      tag=f"{pfx}ldlg{bi}", key=lgb[1].name)
        prog.wr(op, lgb[1])
        o = prog.add("v", lambda e, lgb=lgb: e.tensor_reduce(m1[0].rearrange("p (g e) -> p g e", e=1), r3(lgb[0]), axis=AX.X, op=ALU.max), tag=f"{pfx}m1{bi}")
        prog.rd(o, lgb[1]); prog.wr(o, m1[1])
        x1, x2 = bc(r3(lgb[0]), r3(m1[0], 1))
        o = prog.add("v", lambda e, x1=x1, x2=x2: e.tensor_tensor(r3(tt_[0]), x1, x2, ALU.is_equal), tag=f"{pfx}eq{bi}")
        prog.rd(o, lgb[1], m1[1]); prog.wr(o, tt_[1])
        o = prog.add("v", lambda e: e.tensor_scalar(tt_[0], tt_[0], -BIG, BIG, ALU.mult, ALU.add), tag=f"{pfx}tb{bi}")
        prog.rd(o, tt_[1]); prog.wr(o, tt_[1])
        o = prog.add("v", lambda e: e.tensor_add(tt_[0], tt_[0], iota[0]), tag=f"{pfx}ti{bi}")
        prog.rd(o, tt_[1], iota[1]); prog.wr(o, tt_[1])
        o = prog.add("v", lambda e: e.tensor_reduce(idx[0].rearrange("p (g e) -> p g e", e=1), r3(tt_[0]), axis=AX.X, op=ALU.min), tag=f"{pfx}i1{bi}")
        prog.rd(o, tt_[1]); prog.wr(o, idx[1])
        x1, x2 = bc(r3(iota[0]), r3(idx[0], 1))
        o = prog.add("v", lambda e, bi=bi, x1=x1, x2=x2: e.tensor_tensor(
            r3(OHa[0][0][:, bi * BN:(bi + 1) * BN]), x1, x2, ALU.is_equal), tag=f"{pfx}oh1{bi}")
        prog.rd(o, iota[1]); prog.wr(o, OHa[0][1])
        o = prog.add("v", lambda e, bi=bi, lgb=lgb: e.scalar_tensor_tensor(
            msk[0], OHa[0][0][:, bi * BN:(bi + 1) * BN], -BIG, lgb[0], ALU.mult, ALU.add), tag=f"{pfx}mk{bi}")
        prog.rd(o, OHa[0][1], lgb[1]); prog.wr(o, msk[1])
        o = prog.add("v", lambda e: e.tensor_reduce(m2[0].rearrange("p (g e) -> p g e", e=1), r3(msk[0]), axis=AX.X, op=ALU.max), tag=f"{pfx}m2{bi}")
        prog.rd(o, msk[1]); prog.wr(o, m2[1])
        x1, x2 = bc(r3(msk[0]), r3(m2[0], 1))
        o = prog.add("v", lambda e, x1=x1, x2=x2: e.tensor_tensor(r3(tt_[0]), x1, x2, ALU.is_equal), tag=f"{pfx}eq2{bi}")
        prog.rd(o, msk[1], m2[1]); prog.wr(o, tt_[1])
        o = prog.add("v", lambda e: e.tensor_scalar(tt_[0], tt_[0], -BIG, BIG, ALU.mult, ALU.add), tag=f"{pfx}tb2{bi}")
        prog.rd(o, tt_[1]); prog.wr(o, tt_[1])
        o = prog.add("v", lambda e: e.tensor_add(tt_[0], tt_[0], iota[0]), tag=f"{pfx}ti2{bi}")
        prog.rd(o, tt_[1], iota[1]); prog.wr(o, tt_[1])
        o = prog.add("v", lambda e: e.tensor_reduce(idx[0].rearrange("p (g e) -> p g e", e=1), r3(tt_[0]), axis=AX.X, op=ALU.min), tag=f"{pfx}i2{bi}")
        prog.rd(o, tt_[1]); prog.wr(o, idx[1])
        x1, x2 = bc(r3(iota[0]), r3(idx[0], 1))
        o = prog.add("v", lambda e, bi=bi, x1=x1, x2=x2: e.tensor_tensor(
            r3(OHa[1][0][:, bi * BN:(bi + 1) * BN]), x1, x2, ALU.is_equal), tag=f"{pfx}oh2{bi}")
        prog.rd(o, iota[1]); prog.wr(o, OHa[1][1])
        o = prog.add("v", lambda e: e.tensor_sub(m1[0], m1[0], m2[0]), tag=f"{pfx}d12{bi}")
        prog.rd(o, m1[1], m2[1]); prog.wr(o, m1[1])
        o = prog.add("s", lambda e, bi=bi: e.activation(gv[0][0][:, bi * 64:(bi + 1) * 64], m1[0], AF.Sigmoid), tag=f"{pfx}sg{bi}")
        prog.rd(o, m1[1]); prog.wr(o, gv[0][1])
        o = prog.add("v", lambda e, bi=bi: e.tensor_scalar(
            gv[1][0][:, bi * 64:(bi + 1) * 64], gv[0][0][:, bi * 64:(bi + 1) * 64], -1.0, 1.0, ALU.mult, ALU.add), tag=f"{pfx}g2{bi}")
        prog.rd(o, gv[0][1]); prog.wr(o, gv[1][1])

    for k in range(2):
        for bi in range(4):
            sl = slice((k * 4 + bi) * BN, (k * 4 + bi + 1) * BN)
            mmgroup(prog, pp[bi % 2], [(upst[0], OHa[k][0][:, bi * BN:(bi + 1) * BN])],
                    bufs_rd=[upst[1], OHa[k][1]], tag=f"{pfx}pa{k}{bi}")
            o = prog.add("v", lambda e, sl=sl, pg=pp[bi % 2]: e.tensor_copy(Pin[0][:, sl], pg[0]), tag=f"{pfx}pae{k}{bi}")
            prog.rd(o, pp[bi % 2][1]); prog.wr(o, Pin[1])
            mmgroup(prog, ct_ps, [(ones[0][:, 0:1], OHa[k][0][:, bi * BN:(bi + 1) * BN])],
                    bufs_rd=[ones[1], OHa[k][1]], tag=f"{pfx}ct{k}{bi}")
            o = prog.add("v", lambda e, sl=sl: e.tensor_copy(crow[0][0][:, sl], ct_ps[0]), tag=f"{pfx}cte{k}{bi}")
            prog.rd(o, ct_ps[1]); prog.wr(o, crow[0][1])

    cur = 0
    sh = 8
    while sh < 8 * BN:
        nxt = 1 - cur
        o = prog.add("v", lambda e, cur=cur, nxt=nxt, sh=sh: e.tensor_copy(
            crow[nxt][0][:, 0:sh], crow[cur][0][:, 0:sh]), tag=f"{pfx}sch{sh}")
        prog.rd(o, crow[cur][1]); prog.wr(o, crow[nxt][1])
        o = prog.add("v", lambda e, cur=cur, nxt=nxt, sh=sh: e.tensor_add(
            crow[nxt][0][:, sh:], crow[cur][0][:, sh:], crow[cur][0][:, 0:8 * BN - sh]), tag=f"{pfx}sca{sh}")
        prog.rd(o, crow[cur][1]); prog.wr(o, crow[nxt][1])
        cur = nxt
        sh *= 2
    o = prog.add("v", lambda e: e.memset(orow[0][:, 0:8], 0.0), tag=f"{pfx}oz")
    prog.wr(o, orow[1])
    o = prog.add("v", lambda e, cur=cur: e.tensor_copy(orow[0][:, 8:], crow[cur][0][:, 0:8 * BN - 8]), tag=f"{pfx}osh")
    prog.rd(o, crow[cur][1]); prog.wr(o, orow[1])

    for k in range(2):
        for bi in range(4):
            sl = slice((k * 4 + bi) * BN, (k * 4 + bi + 1) * BN)
            mmgroup(prog, pp[0], [(onesf[0][0:1, :], orow[0][:, sl])],
                    bufs_rd=[onesf[1], orow[1]], tag=f"{pfx}ob{k}{bi}")
            o = prog.add("v", lambda e, sl=sl: e.tensor_add(pos[0], pp[0][0], Pin[0][:, sl]), tag=f"{pfx}pp{k}{bi}")
            prog.rd(o, pp[0][1], Pin[1]); prog.wr(o, pos[1])
            o = prog.add("v", lambda e, k=k, bi=bi: e.tensor_mul(tt_[0], pos[0], OHa[k][0][:, bi * BN:(bi + 1) * BN]), tag=f"{pfx}pm{k}{bi}")
            prog.rd(o, pos[1], OHa[k][1]); prog.wr(o, tt_[1])
            o = prog.add("v", lambda e: e.tensor_reduce(pat[0].rearrange("p (g e) -> p g e", e=1), r3(tt_[0]), axis=AX.X, op=ALU.add), tag=f"{pfx}pr{k}{bi}")
            prog.rd(o, tt_[1]); prog.wr(o, pat[1])
            o = prog.add("v", lambda e: e.tensor_scalar(keep[0], pat[0], CAP - 0.5, None, ALU.is_lt), tag=f"{pfx}kp{k}{bi}")
            prog.rd(o, pat[1]); prog.wr(o, keep[1])
            o = prog.add("v", lambda e, k=k, bi=bi: e.tensor_mul(ck[0], keep[0], gv[k][0][:, bi * 64:(bi + 1) * 64]), tag=f"{pfx}ckm{k}{bi}")
            prog.rd(o, keep[1], gv[k][1]); prog.wr(o, ck[1])
            x1, x2 = bc(r3(OHa[k][0][:, bi * BN:(bi + 1) * BN]), r3(ck[0], 1))
            if k == 0:
                o = prog.add("v", lambda e, bi=bi, x1=x1, x2=x2: e.tensor_tensor(
                    r3(w_sb[0][:, bi * BN:(bi + 1) * BN]), x1, x2, ALU.mult), tag=f"{pfx}w1{k}{bi}")
                prog.rd(o, OHa[k][1], ck[1]); prog.wr(o, w_sb[1])
            else:
                o = prog.add("v", lambda e, x1=x1, x2=x2: e.tensor_tensor(r3(tt_[0]), x1, x2, ALU.mult), tag=f"{pfx}w2a{k}{bi}")
                prog.rd(o, OHa[k][1], ck[1]); prog.wr(o, tt_[1])
                o = prog.add("v", lambda e, bi=bi: e.tensor_add(
                    w_sb[0][:, bi * BN:(bi + 1) * BN], w_sb[0][:, bi * BN:(bi + 1) * BN], tt_[0]), tag=f"{pfx}w2b{k}{bi}")
                prog.rd(o, tt_[1]); prog.wr(o, w_sb[1])

    so = prog.add("sd", lambda e: e.dma_start(w_out[:], w_sb[0].rearrange("p (g e) -> p g e", e=8)),
                  tag=f"{pfx}stw", key=w_sb[1].name)
    prog.rd(so, w_sb[1])


def build_moe_mlp(nc, prog, xin, xout, w_own_d, wT_d, w1_d, b1_d, w2_d, b2_d,
                  cst, pfx="mm", eps=1e-5):
    ones = cst["ones128"]
    onesf = cst["onesf"]
    MB = 1024

    w1bf = (dram(nc, f"{pfx}_w1bf", [NEXP, D, 4096], BF16), Buf(f"{pfx}_w1bf"))
    w2bf = (dram(nc, f"{pfx}_w2bf", [NEXP, 4096, D], BF16), Buf(f"{pfx}_w2bf"))
    emit_cast_to_dram(prog, w1_d, w1bf, f"{pfx}cw1")
    emit_cast_to_dram(prog, w2_d, w2bf, f"{pfx}cw2")

    pp = [psum(nc, f"{pfx}_pp{i}", [128, BN]) for i in range(4)]
    lnsc = mk_ln_scratch(nc, pfx, psA=pp[0], psB=pp[1])
    xbat = [sb(nc, f"{pfx}_xb0", [128, DC * MB], F32)] * 2
    XL = [sb(nc, f"{pfx}_xl{c}", [128, MB], BF16) for c in range(DC)]
    hh = [sb(nc, f"{pfx}_hh{m}", [128, MB], BF16) for m in range(16)]
    acc = [sb(nc, f"{pfx}_acc{c}", [128, MB], F32) for c in range(DC)]
    w1sl = [sb(nc, f"{pfx}_w1sl{i}", [128, D], BF16) for i in range(2)]
    w2sl = [sb(nc, f"{pfx}_w2sl0", [128, 4096], BF16)] * 2
    b1c = sb(nc, f"{pfx}_b1c", [128, NEXP * 32], F32)
    b1row = sb(nc, f"{pfx}_b1row", [1, 1024], F32)
    b2sb = sb(nc, f"{pfx}_b2sb", [8, D], BF16)
    wT8 = sb(nc, f"{pfx}_wT8", [8, MB], BF16)
    wrow = sb(nc, f"{pfx}_wrow", [1, MB], F32)
    wbc = sb(nc, f"{pfx}_wbc", [128, MB], F32)
    ones1 = sb(nc, f"{pfx}_ones1", [1, 1], F32)
    tmp = sb(nc, f"{pfx}_tmp", [128, BN], BF16)

    o = prog.add("v", lambda e: e.memset(ones1[0], 1.0), tag=f"{pfx}one1")
    prog.wr(o, ones1[1])
    lo = prog.add("gd", lambda e: e.dma_start(b2sb[0], b2_d[:]), tag=f"{pfx}ldb2", key=b2sb[1].name)
    prog.wr(lo, b2sb[1])

    for ee in range(NEXP):
        for q4 in range(4):
            lo = prog.add("sd", lambda e, ee=ee, q4=q4: e.dma_start(
                b1row[0], b1_d[:][ee, q4 * 1024:(q4 + 1) * 1024].rearrange("(o f) -> o f", o=1)),
                tag=f"{pfx}ldb1{ee}_{q4}", key=b1row[1].name)
            prog.wr(lo, b1row[1])
            for m8 in range(8):
                m = q4 * 8 + m8
                pg = pp[2 + m % 2]
                mmgroup(prog, (pg[0][:, 0:1], pg[1]), [(b1row[0][:, m8 * 128:(m8 + 1) * 128], ones1[0])],
                        bufs_rd=[b1row[1], ones1[1]], tag=f"{pfx}b1T{ee}_{m}")
                o = prog.add("v", lambda e, ee=ee, m=m, pg=pg: e.tensor_copy(b1c[0][:, ee * 32 + m: ee * 32 + m + 1], pg[0][:, 0:1]),
                             tag=f"{pfx}b1Te{ee}_{m}")
                prog.rd(o, pg[1]); prog.wr(o, b1c[1])

    for B4 in range(TC // MB):
        dbl = B4 % 2
        src = xin[:].rearrange("c p t -> p c t")[:, :, B4 * MB:(B4 + 1) * MB]
        op = prog.add("sd", lambda e, d=xbat[dbl][0].rearrange("p (c t) -> p c t", c=DC), s2=src: e.dma_start(d, s2),
                      tag=f"{pfx}ldx{B4}", key=xbat[dbl][1].name)
        prog.wr(op, xbat[dbl][1])
        lo = prog.add("gd", lambda e, B4=B4: e.dma_start(wT8[0], wT_d[:][:, B4 * MB:(B4 + 1) * MB]),
                      tag=f"{pfx}ldwT{B4}", key=wT8[1].name)
        prog.wr(lo, wT8[1])
        for hf in range(2):
            X8h = [(xbat[dbl][0][:, c * MB + hf * BN:(c * MB) + (hf + 1) * BN], xbat[dbl][1]) for c in range(DC)]
            XLh = [(XL[c][0][:, hf * BN:(hf + 1) * BN], XL[c][1]) for c in range(DC)]
            emit_ln(prog, X8h, XLh, lnsc, ones, eps=eps)
        for ee in range(NEXP):
            lo = prog.add("sd", lambda e, ee=ee, B4=B4: e.dma_start(
                wrow[0], wT_d[:][ee, B4 * MB:(B4 + 1) * MB].rearrange("(o t) -> o t", o=1)),
                tag=f"{pfx}ldwr{ee}_{B4}", key=wrow[1].name)
            prog.wr(lo, wrow[1])
            for hf in range(2):
                pg = pp[2 + hf % 2]
                mmgroup(prog, pg, [(onesf[0][0:1, :], wrow[0][:, hf * BN:(hf + 1) * BN])],
                        bufs_rd=[onesf[1], wrow[1]], tag=f"{pfx}wbc{ee}_{B4}_{hf}")
                o = prog.add("v", lambda e, hf=hf, pg=pg: e.tensor_copy(wbc[0][:, hf * BN:(hf + 1) * BN], pg[0]),
                             tag=f"{pfx}wbce{ee}_{B4}_{hf}")
                prog.rd(o, pg[1]); prog.wr(o, wbc[1])
            for m in range(32):
                slot = w1sl[m % 2]
                src2 = w1bf[0][:][ee].rearrange("(kc p) f -> p kc f", p=128)[:, :, m * 128:(m + 1) * 128]
                lo = prog.add("gd", lambda e, d=slot[0].rearrange("p (kc f) -> p kc f", f=128), s2=src2: e.dma_start(d, s2),
                              tag=f"{pfx}lw1{ee}_{B4}_{m}", key=slot[1].name)
                prog.rd(lo, w1bf[1]); prog.wr(lo, slot[1])
                for hf in range(2):
                    pg = pp[2 + (m * 2 + hf) % 2]
                    mmgroup(prog, pg,
                            [(slot[0][:, kc * 128:(kc + 1) * 128], XL[kc][0][:, hf * BN:(hf + 1) * BN]) for kc in range(DC)],
                            bufs_rd=[slot[1]] + [XL[kc][1] for kc in range(DC)],
                            tag=f"{pfx}w1{ee}_{B4}_{m}_{hf}")
                    o = prog.add("s", lambda e, ee=ee, m=m, pg=pg: e.activation(
                        tmp[0], pg[0], AF.Gelu, bias=b1c[0][:, ee * 32 + m: ee * 32 + m + 1]),
                        tag=f"{pfx}gel{ee}_{B4}_{m}_{hf}")
                    prog.rd(o, pg[1], b1c[1]); prog.wr(o, tmp[1])
                    o = prog.add("v", lambda e, m=m, hf=hf: e.tensor_mul(
                        hh[m][0][:, hf * BN:(hf + 1) * BN], tmp[0], wbc[0][:, hf * BN:(hf + 1) * BN]),
                        tag=f"{pfx}hw{ee}_{B4}_{m}_{hf}")
                    prog.rd(o, tmp[1], wbc[1]); prog.wr(o, hh[m][1])
            for m2 in range(DC):
                slot = w2sl[m2 % 2]
                src2 = w2bf[0][:][ee].rearrange("(kc p) f -> p kc f", p=128)[:, :, m2 * 128:(m2 + 1) * 128]
                lo = prog.add("gd", lambda e, d=slot[0].rearrange("p (kc f) -> p kc f", f=128), s2=src2: e.dma_start(d, s2),
                              tag=f"{pfx}lw2{ee}_{B4}_{m2}", key=slot[1].name)
                prog.rd(lo, w2bf[1]); prog.wr(lo, slot[1])
                for hf in range(2):
                    pg = pp[2 + (m2 * 2 + hf) % 2]
                    pairs = [(slot[0][:, kc * 128:(kc + 1) * 128], hh[kc][0][:, hf * BN:(hf + 1) * BN]) for kc in range(32)]
                    if ee == NEXP - 1:
                        pairs.append((b2sb[0][:, m2 * 128:(m2 + 1) * 128],
                                      wT8[0][:, hf * BN:(hf + 1) * BN]))
                    mmgroup(prog, pg, pairs,
                            bufs_rd=[slot[1], b2sb[1], wT8[1]] + [hh[kc][1] for kc in range(32)],
                            tag=f"{pfx}w2{ee}_{B4}_{m2}_{hf}")
                    if ee == 0:
                        o = prog.add("v", lambda e, m2=m2, hf=hf, pg=pg: e.tensor_copy(
                            acc[m2][0][:, hf * BN:(hf + 1) * BN], pg[0]), tag=f"{pfx}ac{ee}_{B4}_{m2}_{hf}")
                    else:
                        o = prog.add("v", lambda e, m2=m2, hf=hf, pg=pg: e.tensor_add(
                            acc[m2][0][:, hf * BN:(hf + 1) * BN], acc[m2][0][:, hf * BN:(hf + 1) * BN], pg[0]),
                            tag=f"{pfx}ac{ee}_{B4}_{m2}_{hf}")
                    prog.rd(o, pg[1], acc[m2][1]); prog.wr(o, acc[m2][1])
        for c in range(DC):
            o = prog.add("v", lambda e, c=c: e.tensor_add(
                acc[c][0], acc[c][0], xbat[dbl][0][:, c * MB:(c + 1) * MB]), tag=f"{pfx}res{B4}_{c}")
            prog.rd(o, acc[c][1], xbat[dbl][1]); prog.wr(o, acc[c][1])
        for c in range(DC):
            dstc = xout[:][c, :, B4 * MB:(B4 + 1) * MB]
            so = prog.add("sd", lambda e, d=dstc, s2=acc[c][0]: e.dma_start(d, s2),
                          tag=f"{pfx}st{B4}_{c}", key=acc[c][1].name)
            prog.rd(so, acc[c][1])
    return {"wgT": wgT, "wuT": wuT, "wdT": wdT}



def host_consts():
    c = {}
    c["ones128"] = np.ones((128, 128), dtype=BF16NP)
    c["onesf"] = np.ones((128, 128), dtype=np.float32)
    c["identbf"] = np.eye(128, dtype=BF16NP)
    mask = np.zeros((128, 128), dtype=np.float32)
    for p in range(128):
        mask[p, p + 1:] = -1e30
    c["causal128"] = mask
    RD = 32
    inv_freq = 1.0 / 10000.0 ** (np.arange(0, RD, 2, dtype=np.float32) / RD)
    t = np.arange(TSEQ, dtype=np.float32)
    freqs = np.outer(t, inv_freq)
    emb = np.concatenate([freqs, freqs], -1)
    c["ropecos"] = np.tile(np.cos(emb).T, (4, 1)).astype(BF16NP)
    c["ropesin"] = np.tile(np.sin(emb).T, (4, 1)).astype(BF16NP)
    R = np.zeros((RD, RD), dtype=np.float32)
    for i in range(16):
        R[i, i + 16] = -1.0
        R[i + 16, i] = 1.0
    RT = R.T
    blk = np.zeros((128, 128), dtype=np.float32)
    for b in range(4):
        blk[b * 32:(b + 1) * 32, b * 32:(b + 1) * 32] = RT
    c["ropeRT"] = blk.astype(BF16NP)
    c["iota8t"] = np.tile(np.arange(8, dtype=np.float32), (128, 64))
    c["identf"] = np.eye(128, dtype=np.float32)
    u = np.zeros((128, 128), dtype=np.float32)
    for i in range(128):
        u[i, i + 1:] = 1.0
    c["uppstrict"] = u.astype(BF16NP)
    return c


CONST_DT = {np.dtype(BF16NP): BF16, np.dtype(np.float32): F32,
            np.dtype(np.int32): I32}


def load_consts(nc, prog, consts, names):
    out = {}
    for k in names:
        v = consts[k]
        d = nc.dram_tensor(k, list(v.shape), CONST_DT[v.dtype], kind="ExternalInput")
        t = sb(nc, k + "_s", list(v.shape), CONST_DT[v.dtype])
        op = prog.add("sd", lambda e, t=t, d=d: e.dma_start(t[0], d[:]), tag=f"ldc_{k}", key=k + "_s")
        prog.wr(op, t[1])
        out[k] = t
    return out


def to_fm(x):
    t = x.shape[0]
    return np.ascontiguousarray(x.T.reshape(DC, 128, t))


def from_fm(xfm):
    return np.ascontiguousarray(np.asarray(xfm).reshape(D, -1).T)


def launch_ffn(x_shards, wg, wu, wd):
    consts = host_consts()
    nc = bass.Bass("TRN2", target_bir_lowering=False)
    xin = nc.dram_tensor("xin", [DC, 128, TC], F32, kind="ExternalInput")
    xout = nc.dram_tensor("xout", [DC, 128, TC], F32, kind="ExternalOutput")
    wg_d = nc.dram_tensor("wg", [HID, D], I32, kind="ExternalInput")
    wu_d = nc.dram_tensor("wu", [HID, D], I32, kind="ExternalInput")
    wd_d = nc.dram_tensor("wd", [D, HID], I32, kind="ExternalInput")
    prog = Prog(nc, "ffn")
    cst = load_consts(nc, prog, consts, ["ones128"])
    build_ffn(nc, prog, xin, xout, wg_d, wu_d, wd_d, cst)
    prog.run()
    in_maps = []
    for c in range(NCORES):
        m = {"xin": to_fm(x_shards[c]), "wg": wg, "wu": wu, "wd": wd,
             "ones128": consts["ones128"]}
        in_maps.append(m)
    res = run_bass_kernel_spmd(nc, in_maps, core_ids=list(range(NCORES)))
    return [from_fm(r["xout"]) for r in res.results]


def launch_attn_b(x_shards, wq, bq, wk, bk, wv, bv, wo, bo):
    consts = host_consts()
    nc = bass.Bass("TRN2", target_bir_lowering=False)
    xin = nc.dram_tensor("xin", [DC, 128, TC], F32, kind="ExternalInput")
    xout = nc.dram_tensor("xout", [DC, 128, TC], F32, kind="ExternalOutput")
    wd = {}
    for nm, arr in (("wq", wq), ("wk", wk), ("wv", wv), ("wo", wo)):
        wd[nm] = nc.dram_tensor(nm, [D, D], F32, kind="ExternalInput")
    for nm in ("bq", "bk", "bv", "bo"):
        wd[nm] = nc.dram_tensor(nm, [D], F32, kind="ExternalInput")
    prog = Prog(nc, "ab")
    cst = load_consts(nc, prog, consts, ["ones128", "onesf", "identbf", "causal128"])
    build_attn_b(nc, prog, xin, xout, wd, cst)
    prog.run()
    in_maps = []
    for c in range(NCORES):
        m = {"xin": to_fm(x_shards[c]), "wq": wq, "bq": bq, "wk": wk, "bk": bk,
             "wv": wv, "bv": bv, "wo": wo, "bo": bo}
        for k in ("ones128", "onesf", "identbf", "causal128"):
            m[k] = consts[k]
        in_maps.append(m)
    res = run_bass_kernel_spmd(nc, in_maps, core_ids=list(range(NCORES)))
    return [from_fm(r["xout"]) for r in res.results]


def moe_call(x_shards, Wg, bg, W1, b1, W2, b2):
    consts = host_consts()
    nc = bass.Bass("TRN2", target_bir_lowering=False)
    xin = nc.dram_tensor("xin", [DC, 128, TC], F32, kind="ExternalInput")
    lg_out = nc.dram_tensor("lgout", [8, TC], F32, kind="ExternalOutput")
    wg_d = nc.dram_tensor("Wg", [D, NEXP], F32, kind="ExternalInput")
    bg_d = nc.dram_tensor("bg", [NEXP], F32, kind="ExternalInput")
    prog = Prog(nc, "mg")
    cst = load_consts(nc, prog, consts, ["ones128", "onesf"])
    build_moe_logits(nc, prog, xin, lg_out, wg_d, bg_d, cst)
    prog.run()
    in_maps = [{"xin": to_fm(x_shards[c]), "Wg": Wg, "bg": bg,
                "ones128": consts["ones128"], "onesf": consts["onesf"]}
               for c in range(NCORES)]
    res = run_bass_kernel_spmd(nc, in_maps, core_ids=list(range(NCORES)))
    logits = np.concatenate([np.asarray(r["lgout"]).T for r in res.results], axis=0)

    lgt = np.ascontiguousarray(logits.reshape(NTILES, 128, NEXP).transpose(1, 0, 2))

    nc = bass.Bass("TRN2", target_bir_lowering=False)
    lgt_d = nc.dram_tensor("lgt", [128, NTILES, NEXP], F32, kind="ExternalInput")
    w_out = nc.dram_tensor("wout", [128, NTILES, NEXP], F32, kind="ExternalOutput")
    prog = Prog(nc, "gt")
    cst = load_consts(nc, prog, consts, ["ones128", "onesf", "iota8t", "uppstrict"])
    build_moe_gate(nc, prog, lgt_d, w_out, cst)
    prog.run()
    in_maps = [{"lgt": lgt, "ones128": consts["ones128"], "onesf": consts["onesf"],
                "iota8t": consts["iota8t"], "uppstrict": consts["uppstrict"]}
               for c in range(NCORES)]
    res = run_bass_kernel_spmd(nc, in_maps, core_ids=list(range(NCORES)))
    w_tiled = np.asarray(res.results[0]["wout"])
    w_all = np.ascontiguousarray(w_tiled.transpose(1, 0, 2).reshape(32768, NEXP))

    nc = bass.Bass("TRN2", target_bir_lowering=False)
    xin = nc.dram_tensor("xin", [DC, 128, TC], F32, kind="ExternalInput")
    xout = nc.dram_tensor("xout", [DC, 128, TC], F32, kind="ExternalOutput")
    w_own_d = nc.dram_tensor("wown", [128, 32, NEXP], F32, kind="ExternalInput")
    wT_d = nc.dram_tensor("wT", [NEXP, TC], F32, kind="ExternalInput")
    w1_d = nc.dram_tensor("W1", [NEXP, D, 4096], F32, kind="ExternalInput")
    b1_d = nc.dram_tensor("b1", [NEXP, 4096], F32, kind="ExternalInput")
    w2_d = nc.dram_tensor("W2", [NEXP, 4096, D], F32, kind="ExternalInput")
    b2_d = nc.dram_tensor("b2", [NEXP, D], F32, kind="ExternalInput")
    prog = Prog(nc, "mm")
    cst = load_consts(nc, prog, consts, ["ones128", "onesf"])
    build_moe_mlp(nc, prog, xin, xout, w_own_d, wT_d, w1_d, b1_d, w2_d, b2_d, cst)
    prog.run()
    in_maps = []
    for c in range(NCORES):
        w_own = w_all[c * TC:(c + 1) * TC]
        in_maps.append({
            "xin": to_fm(x_shards[c]),
            "wown": np.ascontiguousarray(w_own.reshape(32, 128, NEXP).transpose(1, 0, 2)),
            "wT": np.ascontiguousarray(w_own.T),
            "W1": W1, "b1": b1, "W2": W2, "b2": b2,
            "ones128": consts["ones128"], "onesf": consts["onesf"]})
    res = run_bass_kernel_spmd(nc, in_maps, core_ids=list(range(NCORES)))
    return [from_fm(r["xout"]) for r in res.results]



def build_moe_fused(nc, prog, xin, xout, wd, cst, pfx="mf", pre=None, eps=1e-5):
    ones = cst["ones128"]
    onesf = cst["onesf"]
    iota = cst["iota8t"]
    upst = cst["uppstrict"]
    identf = cst["identf"]
    BIG = 1e9
    MB = 1024
    NT = 32

    if pre is None:
        w1bf = (dram(nc, f"{pfx}_w1bf", [NEXP, D, 4096], BF16), Buf(f"{pfx}_w1bf"))
        w2bf = (dram(nc, f"{pfx}_w2bf", [NEXP, 4096, D], BF16), Buf(f"{pfx}_w2bf"))
        emit_cast_to_dram(prog, wd["W1"], w1bf, f"{pfx}cw1")
        emit_cast_to_dram(prog, wd["W2"], w2bf, f"{pfx}cw2")
    else:
        w1bf, w2bf = pre["w1bf"], pre["w2bf"]

    cc_in = (dram(nc, f"{pfx}_ccin", [1, 16], F32), Buf(f"{pfx}_ccin"))
    cc_out = (dram(nc, f"{pfx}_ccout", [8, 16], F32), Buf(f"{pfx}_ccout"))

    pp = [psum(nc, f"{pfx}_pp{i}", [128, BN]) for i in range(4)]
    lg_ps = psum(nc, f"{pfx}_lgps", [8, BN])
    rw_ps = psum(nc, f"{pfx}_rwps", [1, BN])

    lnsc = mk_ln_scratch(nc, pfx, psA=pp[0], psB=pp[1])
    xbat = [sb(nc, f"{pfx}_xb0", [128, DC * MB], F32)] * 2
    wgs = sb(nc, f"{pfx}_wgs", [128, DC * NEXP], F32)
    onescol = sb(nc, f"{pfx}_onescol", [128, 1], F32)
    ncs = sb(nc, f"{pfx}_ncs", [8, 1], F32)
    bgrow = sb(nc, f"{pfx}_bgrow", [1, NEXP], F32)
    bgcol = sb(nc, f"{pfx}_bgcol", [8, 1], F32)
    w_sb = sb(nc, f"{pfx}_wsb", [128, NT * 8], F32)
    b1c = sb(nc, f"{pfx}_b1c", [128, NEXP * 32], F32)
    b1row = sb(nc, f"{pfx}_b1row", [1, 1024], F32)
    b2c = sb(nc, f"{pfx}_b2c", [128, NEXP * DC], F32)
    ones1 = sb(nc, f"{pfx}_ones1", [1, 1], F32)
    scope_begin()
    lg_sb = sb(nc, f"{pfx}_lgsb", [8, TC], F32)
    t2 = sb(nc, f"{pfx}_t2", [8, BN], F32)
    lgt = sb(nc, f"{pfx}_lgt", [128, NT * 8], F32)
    OHa = [sb(nc, f"{pfx}_oh{k}", [128, NT * 8], BF16) for k in range(2)]
    gv = [sb(nc, f"{pfx}_gv{k}", [128, NT], F32) for k in range(2)]
    m1 = sb(nc, f"{pfx}_m1", [128, NT], F32)
    mx2 = sb(nc, f"{pfx}_m2", [128, NT], F32)
    idx = sb(nc, f"{pfx}_idx", [128, NT], F32)
    tt_ = sb(nc, f"{pfx}_tt", [128, NT * 8], F32)
    msk = sb(nc, f"{pfx}_msk", [128, NT * 8], F32)
    Pin = sb(nc, f"{pfx}_Pin", [128, 2 * NT * 8], F32)
    crow = [sb(nc, f"{pfx}_crow{i}", [1, 2 * NT * 8], F32) for i in range(2)]
    orow = sb(nc, f"{pfx}_orow", [1, 2 * NT * 8], F32)
    trow = sb(nc, f"{pfx}_trow", [1, 16], F32)
    G_sb = sb(nc, f"{pfx}_G", [8, 16], F32)
    cmask = sb(nc, f"{pfx}_cmask", [8, 2], F32)
    offr = sb(nc, f"{pfx}_offr", [1, 16], F32)
    pos = sb(nc, f"{pfx}_pos", [128, NT * 8], F32)
    pat = sb(nc, f"{pfx}_pat", [128, NT], F32)
    keep = sb(nc, f"{pfx}_keep", [128, NT], F32)
    ck = sb(nc, f"{pfx}_ck", [128, NT], F32)

    def r3(ap, e=8):
        return ap.rearrange("p (g e) -> p g e", e=e)

    def bc2(a, b):
        return bass.broadcast_tensor_aps(a, b)

    o = prog.add("v", lambda e: e.memset(ones1[0], 1.0), tag=f"{pfx}one1")
    prog.wr(o, ones1[1])
    o = prog.add("v", lambda e: e.memset(onescol[0], 1.0), tag=f"{pfx}onec")
    prog.wr(o, onescol[1])
    lo = prog.add("sd", lambda e: e.dma_start(cmask[0], wd["coremask"][:]),
                  tag=f"{pfx}ldcm", key=cmask[1].name)
    prog.wr(lo, cmask[1])
    lo = prog.add("sd", lambda e: e.dma_start(
        wgs[0].rearrange("p (kc e) -> p kc e", e=NEXP),
        wd["Wg"][:].rearrange("(kc p) e -> p kc e", p=128)), tag=f"{pfx}ldwg", key=wgs[1].name)
    prog.wr(lo, wgs[1])
    lo = prog.add("sd", lambda e: e.dma_start(bgrow[0], wd["bg"][:].rearrange("(o e) -> o e", o=1)),
                  tag=f"{pfx}ldbg", key=bgrow[1].name)
    prog.wr(lo, bgrow[1])
    cs_ps = (pp[2][0][0:8, 0:1], pp[2][1])
    mmgroup(prog, cs_ps, [(wgs[0][:, kc * NEXP:(kc + 1) * NEXP], onescol[0]) for kc in range(DC)],
            bufs_rd=[wgs[1], onescol[1]], tag=f"{pfx}cs")
    o = prog.add("v", lambda e: e.tensor_scalar(ncs[0], cs_ps[0], -1.0, None, ALU.mult), tag=f"{pfx}ncs")
    prog.rd(o, cs_ps[1]); prog.wr(o, ncs[1])
    bg_ps = (pp[3][0][0:8, 0:1], pp[3][1])
    mmgroup(prog, bg_ps, [(bgrow[0], ones1[0])], bufs_rd=[bgrow[1], ones1[1]], tag=f"{pfx}bgT")
    o = prog.add("v", lambda e: e.tensor_copy(bgcol[0], bg_ps[0]), tag=f"{pfx}bgc")
    prog.rd(o, bg_ps[1]); prog.wr(o, bgcol[1])
    for ee in range(NEXP):
        for q4 in range(4):
            lo = prog.add("sd", lambda e, ee=ee, q4=q4: e.dma_start(
                b1row[0], wd["b1"][:][ee, q4 * 1024:(q4 + 1) * 1024].rearrange("(o f) -> o f", o=1)),
                tag=f"{pfx}ldb1{ee}_{q4}", key=b1row[1].name)
            prog.wr(lo, b1row[1])
            for m8 in range(8):
                m = q4 * 8 + m8
                pg = pp[2 + m % 2]
                mmgroup(prog, (pg[0][:, 0:1], pg[1]), [(b1row[0][:, m8 * 128:(m8 + 1) * 128], ones1[0])],
                        bufs_rd=[b1row[1], ones1[1]], tag=f"{pfx}b1T{ee}_{m}")
                o = prog.add("v", lambda e, ee=ee, m=m, pg=pg: e.tensor_copy(b1c[0][:, ee * 32 + m: ee * 32 + m + 1], pg[0][:, 0:1]),
                             tag=f"{pfx}b1Te{ee}_{m}")
                prog.rd(o, pg[1]); prog.wr(o, b1c[1])
        lo = prog.add("sd", lambda e, ee=ee: e.dma_start(
            b1row[0][:, 0:1024], wd["b2"][:][ee].rearrange("(o f) -> o f", o=1)),
            tag=f"{pfx}ldb2{ee}", key=b1row[1].name)
        prog.wr(lo, b1row[1])
        for m8 in range(DC):
            pg = pp[2 + m8 % 2]
            mmgroup(prog, (pg[0][:, 0:1], pg[1]), [(b1row[0][:, m8 * 128:(m8 + 1) * 128], ones1[0])],
                    bufs_rd=[b1row[1], ones1[1]], tag=f"{pfx}b2T{ee}_{m8}")
            o = prog.add("v", lambda e, ee=ee, m8=m8, pg=pg: e.tensor_copy(b2c[0][:, ee * DC + m8: ee * DC + m8 + 1], pg[0][:, 0:1]),
                         tag=f"{pfx}b2Te{ee}_{m8}")
            prog.rd(o, pg[1]); prog.wr(o, b2c[1])

    X8b = [[(xbat[i][0][:, c * MB:(c + 1) * MB][:, 0:BN], xbat[i][1]) for c in range(DC)]
           for i in range(2)]
    for b in range(NB):
        dbl = 0
        hf = b % 2
        if hf == 0:
            src = xin[:].rearrange("c p t -> p c t")[:, :, (b // 2) * MB:(b // 2 + 1) * MB]
            op = prog.add("sd", lambda e, d=xbat[0][0].rearrange("p (c t) -> p c t", c=DC), s2=src: e.dma_start(d, s2),
                          tag=f"{pfx}ldx{b}", key=xbat[0][1].name)
            prog.wr(op, xbat[0][1])
        X8h = [(xbat[0][0][:, c * MB + hf * BN: c * MB + (hf + 1) * BN], xbat[0][1]) for c in range(DC)]
        emit_ln(prog, X8h, None, lnsc, ones, eps=eps, norm=False)
        mmgroup(prog, lg_ps, [(wgs[0][:, kc * NEXP:(kc + 1) * NEXP], X8h[kc][0]) for kc in range(DC)],
                bufs_rd=[wgs[1], xbat[0][1]], tag=f"{pfx}raw{b}")
        o = prog.add("v", lambda e: e.tensor_mul(t2[0], lg_ps[0], lnsc["rstd"][0][0:8, :]), tag=f"{pfx}l1{b}")
        prog.rd(o, lg_ps[1], lnsc["rstd"][1]); prog.wr(o, t2[1])
        o = prog.add("v", lambda e: e.scalar_tensor_tensor(t2[0], lnsc["mr"][0][0:8, :], ncs[0], t2[0], ALU.mult, ALU.add),
                     tag=f"{pfx}l2{b}")
        prog.rd(o, lnsc["mr"][1], ncs[1], t2[1]); prog.wr(o, t2[1])
        o = prog.add("v", lambda e, b=b: e.tensor_scalar(lg_sb[0][:, b * BN:(b + 1) * BN], t2[0], bgcol[0], None, ALU.add),
                     tag=f"{pfx}l3{b}")
        prog.rd(o, t2[1], bgcol[1]); prog.wr(o, lg_sb[1])

    for half in range(2):
        pg = pp[2 + half % 2]
        for t in range(16):
            gt = half * 16 + t
            to = prog.add("t", lambda e, gt=gt, t=t, pg=pg: e.transpose(
                pg[0][:, t * 8:(t + 1) * 8], lg_sb[0][:, gt * 128:(gt + 1) * 128],
                identf[0][0:8, 0:8]), tag=f"{pfx}lgT{gt}")
            prog.rd(to, lg_sb[1], identf[1]); prog.wr(to, pg[1])
        o = prog.add("v", lambda e, half=half, pg=pg: e.tensor_copy(
            lgt[0][:, half * 128:(half + 1) * 128], pg[0][:, 0:128]), tag=f"{pfx}lgTe{half}")
        prog.rd(o, pg[1]); prog.wr(o, lgt[1])

    def rt(ap, e=8):
        return ap.rearrange("p (g e) -> p g e", e=e)

    o = prog.add("v", lambda e: e.tensor_reduce(rt(m1[0], 1), rt(lgt[0]), axis=AX.X, op=ALU.max), tag=f"{pfx}m1")
    prog.rd(o, lgt[1]); prog.wr(o, m1[1])
    x1, x2 = bc2(rt(lgt[0]), rt(m1[0], 1))
    o = prog.add("v", lambda e, x1=x1, x2=x2: e.tensor_tensor(rt(tt_[0]), x1, x2, ALU.is_equal), tag=f"{pfx}eq")
    prog.rd(o, lgt[1], m1[1]); prog.wr(o, tt_[1])
    o = prog.add("v", lambda e: e.tensor_scalar(tt_[0], tt_[0], -BIG, BIG, ALU.mult, ALU.add), tag=f"{pfx}tb")
    prog.rd(o, tt_[1]); prog.wr(o, tt_[1])
    o = prog.add("v", lambda e: e.tensor_add(tt_[0], tt_[0], iota[0][:, 0:NT * 8]), tag=f"{pfx}ti")
    prog.rd(o, tt_[1], iota[1]); prog.wr(o, tt_[1])
    o = prog.add("v", lambda e: e.tensor_reduce(rt(idx[0], 1), rt(tt_[0]), axis=AX.X, op=ALU.min), tag=f"{pfx}i1")
    prog.rd(o, tt_[1]); prog.wr(o, idx[1])
    x1, x2 = bc2(rt(iota[0][:, 0:NT * 8]), rt(idx[0], 1))
    o = prog.add("v", lambda e, x1=x1, x2=x2: e.tensor_tensor(rt(OHa[0][0]), x1, x2, ALU.is_equal), tag=f"{pfx}oh1")
    prog.rd(o, iota[1], idx[1]); prog.wr(o, OHa[0][1])
    o = prog.add("v", lambda e: e.scalar_tensor_tensor(msk[0], OHa[0][0], -BIG, lgt[0], ALU.mult, ALU.add), tag=f"{pfx}mk")
    prog.rd(o, OHa[0][1], lgt[1]); prog.wr(o, msk[1])
    o = prog.add("v", lambda e: e.tensor_reduce(rt(mx2[0], 1), rt(msk[0]), axis=AX.X, op=ALU.max), tag=f"{pfx}m2")
    prog.rd(o, msk[1]); prog.wr(o, mx2[1])
    x1, x2 = bc2(rt(msk[0]), rt(mx2[0], 1))
    o = prog.add("v", lambda e, x1=x1, x2=x2: e.tensor_tensor(rt(tt_[0]), x1, x2, ALU.is_equal), tag=f"{pfx}eq2")
    prog.rd(o, msk[1], mx2[1]); prog.wr(o, tt_[1])
    o = prog.add("v", lambda e: e.tensor_scalar(tt_[0], tt_[0], -BIG, BIG, ALU.mult, ALU.add), tag=f"{pfx}tb2")
    prog.rd(o, tt_[1]); prog.wr(o, tt_[1])
    o = prog.add("v", lambda e: e.tensor_add(tt_[0], tt_[0], iota[0][:, 0:NT * 8]), tag=f"{pfx}ti2")
    prog.rd(o, tt_[1], iota[1]); prog.wr(o, tt_[1])
    o = prog.add("v", lambda e: e.tensor_reduce(rt(idx[0], 1), rt(tt_[0]), axis=AX.X, op=ALU.min), tag=f"{pfx}i2")
    prog.rd(o, tt_[1]); prog.wr(o, idx[1])
    x1, x2 = bc2(rt(iota[0][:, 0:NT * 8]), rt(idx[0], 1))
    o = prog.add("v", lambda e, x1=x1, x2=x2: e.tensor_tensor(rt(OHa[1][0]), x1, x2, ALU.is_equal), tag=f"{pfx}oh2")
    prog.rd(o, iota[1], idx[1]); prog.wr(o, OHa[1][1])
    o = prog.add("v", lambda e: e.tensor_sub(m1[0], m1[0], mx2[0]), tag=f"{pfx}d12")
    prog.rd(o, m1[1], mx2[1]); prog.wr(o, m1[1])
    o = prog.add("s", lambda e: e.activation(gv[0][0], m1[0], AF.Sigmoid), tag=f"{pfx}sg")
    prog.rd(o, m1[1]); prog.wr(o, gv[0][1])
    o = prog.add("v", lambda e: e.tensor_scalar(gv[1][0], gv[0][0], -1.0, 1.0, ALU.mult, ALU.add), tag=f"{pfx}g2")
    prog.rd(o, gv[0][1]); prog.wr(o, gv[1][1])

    for k in range(2):
        pg = pp[2 + k % 2]
        mmgroup(prog, (pg[0][:, 0:NT * 8], pg[1]), [(upst[0], OHa[k][0])],
                bufs_rd=[upst[1], OHa[k][1]], tag=f"{pfx}pa{k}")
        o = prog.add("v", lambda e, k=k, pg=pg: e.tensor_copy(Pin[0][:, k * NT * 8:(k + 1) * NT * 8], pg[0][:, 0:NT * 8]),
                     tag=f"{pfx}pae{k}")
        prog.rd(o, pg[1]); prog.wr(o, Pin[1])
        mmgroup(prog, (rw_ps[0][:, 0:NT * 8], rw_ps[1]), [(ones[0][:, 0:1], OHa[k][0])],
                bufs_rd=[ones[1], OHa[k][1]], tag=f"{pfx}ct{k}")
        o = prog.add("v", lambda e, k=k: e.tensor_copy(crow[0][0][:, k * NT * 8:(k + 1) * NT * 8], rw_ps[0][:, 0:NT * 8]),
                     tag=f"{pfx}cte{k}")
        prog.rd(o, rw_ps[1]); prog.wr(o, crow[0][1])

    cur = 0
    sh = 8
    while sh < NT * 8:
        nxt = 1 - cur
        for k in range(2):
            base = k * NT * 8
            o = prog.add("v", lambda e, cur=cur, nxt=nxt, sh=sh, base=base: e.tensor_copy(
                crow[nxt][0][:, base:base + sh], crow[cur][0][:, base:base + sh]), tag=f"{pfx}sch{sh}_{k}")
            prog.rd(o, crow[cur][1]); prog.wr(o, crow[nxt][1])
            o = prog.add("v", lambda e, cur=cur, nxt=nxt, sh=sh, base=base: e.tensor_add(
                crow[nxt][0][:, base + sh:base + NT * 8], crow[cur][0][:, base + sh:base + NT * 8],
                crow[cur][0][:, base:base + NT * 8 - sh]), tag=f"{pfx}sca{sh}_{k}")
            prog.rd(o, crow[cur][1]); prog.wr(o, crow[nxt][1])
        cur = nxt
        sh *= 2
    for k in range(2):
        o = prog.add("v", lambda e, k=k, cur=cur: e.tensor_copy(
            trow[0][:, k * 8:(k + 1) * 8], crow[cur][0][:, k * NT * 8 + (NT - 1) * 8: (k + 1) * NT * 8]),
            tag=f"{pfx}tot{k}")
        prog.rd(o, crow[cur][1]); prog.wr(o, trow[1])
        o = prog.add("v", lambda e, k=k: e.memset(orow[0][:, k * NT * 8:k * NT * 8 + 8], 0.0), tag=f"{pfx}oz{k}")
        prog.wr(o, orow[1])
        o = prog.add("v", lambda e, k=k, cur=cur: e.tensor_copy(
            orow[0][:, k * NT * 8 + 8:(k + 1) * NT * 8], crow[cur][0][:, k * NT * 8:(k + 1) * NT * 8 - 8]),
            tag=f"{pfx}osh{k}")
        prog.rd(o, crow[cur][1]); prog.wr(o, orow[1])

    so = prog.add("sd", lambda e: e.dma_start(cc_in[0][:], trow[0]), tag=f"{pfx}stcc", key=cc_in[1].name)
    prog.rd(so, trow[1]); prog.wr(so, cc_in[1])
    cco = prog.add("p", lambda e: e.collective_compute(
        "AllGather", ALU.bypass, replica_groups=[[0, 1, 2, 3, 4, 5, 6, 7]],
        ins=[cc_in[0][:].opt()], outs=[cc_out[0][:].opt()]), tag=f"{pfx}cc")
    prog.rd(cco, cc_in[1]); prog.wr(cco, cc_out[1])
    lo = prog.add("sd", lambda e: e.dma_start(G_sb[0], cc_out[0][:]), tag=f"{pfx}ldG", key=G_sb[1].name)
    prog.rd(lo, cc_out[1]); prog.wr(lo, G_sb[1])

    mmgroup(prog, (rw_ps[0][0:1, 0:16], rw_ps[1]), [(cmask[0][:, 0:1], G_sb[0])],
            bufs_rd=[cmask[1], G_sb[1]], tag=f"{pfx}off1")
    mmgroup(prog, (rw_ps[0][0:1, 8:16], rw_ps[1]), [(cmask[0][:, 1:2], G_sb[0][:, 0:8])],
            bufs_rd=[cmask[1], G_sb[1]], tag=f"{pfx}off2", start=False)
    o = prog.add("v", lambda e: e.tensor_copy(offr[0], rw_ps[0][0:1, 0:16]), tag=f"{pfx}offe")
    prog.rd(o, rw_ps[1]); prog.wr(o, offr[1])

    for k in range(2):
        a3 = orow[0][:, k * NT * 8:(k + 1) * NT * 8].rearrange("p (g e) -> p g e", e=8)
        b3 = offr[0][:, k * 8:(k + 1) * 8].rearrange("p (g e) -> p g e", e=8)
        x1, x2 = bc2(a3, b3)
        o = prog.add("v", lambda e, x1=x1, x2=x2: e.tensor_tensor(x1, x1, x2, ALU.add), tag=f"{pfx}oadd{k}")
        prog.rd(o, offr[1], orow[1]); prog.wr(o, orow[1])

    for k in range(2):
        mmgroup(prog, (pp[2][0][:, 0:NT * 8], pp[2][1]), [(onesf[0][0:1, :], orow[0][:, k * NT * 8:(k + 1) * NT * 8])],
                bufs_rd=[onesf[1], orow[1]], tag=f"{pfx}ob{k}")
        o = prog.add("v", lambda e, k=k: e.tensor_add(pos[0], pp[2][0][:, 0:NT * 8], Pin[0][:, k * NT * 8:(k + 1) * NT * 8]),
                     tag=f"{pfx}pp{k}")
        prog.rd(o, pp[2][1], Pin[1]); prog.wr(o, pos[1])
        o = prog.add("v", lambda e, k=k: e.tensor_mul(tt_[0], pos[0], OHa[k][0]), tag=f"{pfx}pm{k}")
        prog.rd(o, pos[1], OHa[k][1]); prog.wr(o, tt_[1])
        o = prog.add("v", lambda e: e.tensor_reduce(rt(pat[0], 1), rt(tt_[0]), axis=AX.X, op=ALU.add), tag=f"{pfx}pr{k}")
        prog.rd(o, tt_[1]); prog.wr(o, pat[1])
        o = prog.add("v", lambda e: e.tensor_scalar(keep[0], pat[0], CAP - 0.5, None, ALU.is_lt), tag=f"{pfx}kp{k}")
        prog.rd(o, pat[1]); prog.wr(o, keep[1])
        o = prog.add("v", lambda e, k=k: e.tensor_mul(ck[0], keep[0], gv[k][0]), tag=f"{pfx}ckm{k}")
        prog.rd(o, keep[1], gv[k][1]); prog.wr(o, ck[1])
        x1, x2 = bc2(rt(OHa[k][0]), rt(ck[0], 1))
        if k == 0:
            o = prog.add("v", lambda e, x1=x1, x2=x2: e.tensor_tensor(rt(w_sb[0]), x1, x2, ALU.mult), tag=f"{pfx}w1{k}")
            prog.rd(o, OHa[k][1], ck[1]); prog.wr(o, w_sb[1])
        else:
            o = prog.add("v", lambda e, x1=x1, x2=x2: e.tensor_tensor(rt(tt_[0]), x1, x2, ALU.mult), tag=f"{pfx}w2a{k}")
            prog.rd(o, OHa[k][1], ck[1]); prog.wr(o, tt_[1])
            o = prog.add("v", lambda e: e.tensor_add(w_sb[0], w_sb[0], tt_[0]), tag=f"{pfx}w2b{k}")
            prog.rd(o, tt_[1]); prog.wr(o, w_sb[1])

    scope_end()
    XL = [sb(nc, f"{pfx}_xl{c}", [128, MB], BF16) for c in range(DC)]
    hh = [sb(nc, f"{pfx}_hh{m}", [128, MB], BF16) for m in range(16)]
    acc = [sb(nc, f"{pfx}_acc{c}", [128, MB], F32) for c in range(DC)]
    w1sl = [sb(nc, f"{pfx}_w1sl{i}", [128, D], BF16) for i in range(2)]
    w2sl = [sb(nc, f"{pfx}_w2sl0", [128, 4096], BF16)] * 2
    wrow = sb(nc, f"{pfx}_wrow", [1, MB], F32)
    wbc = sb(nc, f"{pfx}_wbc", [128, MB], F32)
    tmp = sb(nc, f"{pfx}_tmp", [128, BN], BF16)

    for B4 in range(TC // MB):
        src = xin[:].rearrange("c p t -> p c t")[:, :, B4 * MB:(B4 + 1) * MB]
        op = prog.add("sd", lambda e, d=xbat[0][0].rearrange("p (c t) -> p c t", c=DC), s2=src: e.dma_start(d, s2),
                      tag=f"{pfx}ldx2{B4}", key=xbat[0][1].name)
        prog.wr(op, xbat[0][1])
        for hf in range(2):
            X8h = [(xbat[0][0][:, c * MB + hf * BN: c * MB + (hf + 1) * BN], xbat[0][1]) for c in range(DC)]
            XLh = [(XL[c][0][:, hf * BN:(hf + 1) * BN], XL[c][1]) for c in range(DC)]
            emit_ln(prog, X8h, XLh, lnsc, ones, eps=eps)
        for ee in range(NEXP):
            for t8 in range(8):
                tile = B4 * 8 + t8
                pg = pp[2 + (t8 // 4) % 2]
                to = prog.add("t", lambda e, tile=tile, t8=t8, ee=ee, pg=pg: e.transpose(
                    pg[0][0:1, (t8 % 4) * 128:(t8 % 4) * 128 + 128],
                    w_sb[0][:, tile * 8 + ee: tile * 8 + ee + 1], identf[0]),
                    tag=f"{pfx}wT{ee}_{B4}_{t8}")
                prog.rd(to, w_sb[1], identf[1]); prog.wr(to, pg[1])
                if t8 % 4 == 3:
                    o = prog.add("v", lambda e, t8=t8, pg=pg: e.tensor_copy(
                        wrow[0][:, (t8 - 3) * 128:(t8 + 1) * 128], pg[0][0:1, 0:512]), tag=f"{pfx}wTe{ee}_{B4}_{t8}")
                    prog.rd(o, pg[1]); prog.wr(o, wrow[1])
            for hf in range(2):
                pg = pp[2 + hf % 2]
                mmgroup(prog, pg, [(onesf[0][0:1, :], wrow[0][:, hf * BN:(hf + 1) * BN])],
                        bufs_rd=[onesf[1], wrow[1]], tag=f"{pfx}wbc{ee}_{B4}_{hf}")
                o = prog.add("v", lambda e, hf=hf, pg=pg: e.tensor_copy(wbc[0][:, hf * BN:(hf + 1) * BN], pg[0]),
                             tag=f"{pfx}wbce{ee}_{B4}_{hf}")
                prog.rd(o, pg[1]); prog.wr(o, wbc[1])
            for grp in range(2):
                for m16 in range(16):
                    m = grp * 16 + m16
                    slot = w1sl[m % 2]
                    src2 = w1bf[0][:][ee].rearrange("(kc p) f -> p kc f", p=128)[:, :, m * 128:(m + 1) * 128]
                    lo = prog.add("gd", lambda e, d=slot[0].rearrange("p (kc f) -> p kc f", f=128), s2=src2: e.dma_start(d, s2),
                                  tag=f"{pfx}lw1{ee}_{B4}_{m}", key=slot[1].name)
                    prog.rd(lo, w1bf[1]); prog.wr(lo, slot[1])
                    for hf in range(2):
                        pg = pp[2 + (m * 2 + hf) % 2]
                        mmgroup(prog, pg,
                                [(slot[0][:, kc * 128:(kc + 1) * 128], XL[kc][0][:, hf * BN:(hf + 1) * BN]) for kc in range(DC)],
                                bufs_rd=[slot[1]] + [XL[kc][1] for kc in range(DC)],
                                tag=f"{pfx}w1m{ee}_{B4}_{m}_{hf}")
                        o = prog.add("s", lambda e, ee=ee, m=m, pg=pg: e.activation(
                            tmp[0], pg[0], AF.Gelu, bias=b1c[0][:, ee * 32 + m: ee * 32 + m + 1]),
                            tag=f"{pfx}gel{ee}_{B4}_{m}_{hf}")
                        prog.rd(o, pg[1], b1c[1]); prog.wr(o, tmp[1])
                        o = prog.add("v", lambda e, m16=m16, hf=hf: e.tensor_mul(
                            hh[m16][0][:, hf * BN:(hf + 1) * BN], tmp[0], wbc[0][:, hf * BN:(hf + 1) * BN]),
                            tag=f"{pfx}hw{ee}_{B4}_{m}_{hf}")
                        prog.rd(o, tmp[1], wbc[1]); prog.wr(o, hh[m16][1])
                for m2 in range(DC):
                    slot = w2sl[m2 % 2]
                    src2 = w2bf[0][:][ee].rearrange("(kc p) f -> p kc f", p=128)[
                        :, grp * 16:(grp + 1) * 16, m2 * 128:(m2 + 1) * 128]
                    lo = prog.add("gd", lambda e, d=slot[0][:, 0:2048].rearrange("p (kc f) -> p kc f", f=128), s2=src2: e.dma_start(d, s2),
                                  tag=f"{pfx}lw2{ee}_{B4}_{grp}_{m2}", key=slot[1].name)
                    prog.rd(lo, w2bf[1]); prog.wr(lo, slot[1])
                    for hf in range(2):
                        pg = pp[2 + (m2 * 2 + hf) % 2]
                        mmgroup(prog, pg,
                                [(slot[0][:, kc * 128:(kc + 1) * 128], hh[kc][0][:, hf * BN:(hf + 1) * BN]) for kc in range(16)],
                                bufs_rd=[slot[1]] + [hh[kc][1] for kc in range(16)],
                                tag=f"{pfx}w2m{ee}_{B4}_{grp}_{m2}_{hf}")
                        if ee == 0 and grp == 0:
                            o = prog.add("v", lambda e, m2=m2, hf=hf, pg=pg: e.tensor_copy(
                                acc[m2][0][:, hf * BN:(hf + 1) * BN], pg[0]), tag=f"{pfx}ac{ee}_{B4}_{grp}_{m2}_{hf}")
                        else:
                            o = prog.add("v", lambda e, m2=m2, hf=hf, pg=pg: e.tensor_add(
                                acc[m2][0][:, hf * BN:(hf + 1) * BN], acc[m2][0][:, hf * BN:(hf + 1) * BN], pg[0]),
                                tag=f"{pfx}ac{ee}_{B4}_{grp}_{m2}_{hf}")
                        prog.rd(o, pg[1], acc[m2][1]); prog.wr(o, acc[m2][1])
            for m2 in range(DC):
                o = prog.add("v", lambda e, ee=ee, m2=m2: e.scalar_tensor_tensor(
                    acc[m2][0], wbc[0], b2c[0][:, ee * DC + m2: ee * DC + m2 + 1], acc[m2][0], ALU.mult, ALU.add),
                    tag=f"{pfx}b2a{ee}_{B4}_{m2}")
                prog.rd(o, wbc[1], b2c[1], acc[m2][1]); prog.wr(o, acc[m2][1])
        for c in range(DC):
            o = prog.add("v", lambda e, c=c: e.tensor_add(
                acc[c][0], acc[c][0], xbat[0][0][:, c * MB:(c + 1) * MB]), tag=f"{pfx}res{B4}_{c}")
            prog.rd(o, acc[c][1], xbat[0][1]); prog.wr(o, acc[c][1])
        for c in range(DC):
            dstc = xout[:][c, :, B4 * MB:(B4 + 1) * MB]
            so = prog.add("sd", lambda e, d=dstc, s2=acc[c][0]: e.dma_start(d, s2),
                          tag=f"{pfx}st{B4}_{c}", key=acc[c][1].name)
            prog.rd(so, acc[c][1])
    return {"w1bf": w1bf, "w2bf": w2bf}



_CACHE = {}


def _get(name, buildfn):
    if name not in _CACHE:
        _CACHE[name] = buildfn()
    return _CACHE[name]


def _mk_simple(name, build, wspecs, cnames):
    def bf():
        consts = host_consts()
        nc = bass.Bass("TRN2", target_bir_lowering=False)
        xin = nc.dram_tensor("xin", [DC, 128, TC], F32, kind="ExternalInput")
        xout = nc.dram_tensor("xout", [DC, 128, TC], F32, kind="ExternalOutput")
        wd = {k: nc.dram_tensor(k, shp, dt, kind="ExternalInput")
              for k, shp, dt in wspecs}
        prog = Prog(nc, name)
        cst = load_consts(nc, prog, consts, cnames)
        build(nc, prog, xin, xout, wd, cst)
        prog.run()
        return nc, consts
    return bf


def _run(nc, consts, cnames, fm_shards, wmap):
    in_maps = []
    for c in range(NCORES):
        m = {"xin": fm_shards[c]}
        m.update(wmap)
        for k in cnames:
            m[k] = consts[k]
        in_maps.append(m)
    res = run_bass_kernel_spmd(nc, in_maps, core_ids=list(range(NCORES)))
    return [np.asarray(r["xout"]) for r in res.results]


def build_fused(nc):
    consts = host_consts()
    xio = [nc.dram_tensor(f"xs{i}", [DC, 128, TC], F32,
                          kind=("ExternalInput" if i == 0 else
                                "ExternalOutput" if i == 8 else "Internal"))
           for i in range(9)]
    wn = {}
    def W(name, shape, dt=F32):
        wn[name] = nc.dram_tensor(name, shape, dt, kind="ExternalInput")
        return wn[name]

    prog = Prog(nc, "fz")
    cn = ["ones128", "onesf", "identbf", "identf", "causal128", "ropecos",
          "ropesin", "ropeRT", "iota8t", "uppstrict"]
    cst = load_consts(nc, prog, consts, cn)

    wg_d = nc.dram_tensor("L0f_wg", [HID, D], I32, kind="ExternalInput")
    wu_d = nc.dram_tensor("L0f_wu", [HID, D], I32, kind="ExternalInput")
    wd_d = nc.dram_tensor("L0f_wd", [D, HID], I32, kind="ExternalInput")
    ffn_cast = ((dram(nc, "s1wg_bf", [HID, D], BF16), Buf("s1wg_bf")),
                (dram(nc, "s1wu_bf", [HID, D], BF16), Buf("s1wu_bf")),
                (dram(nc, "s1wd_bf", [D, HID], BF16), Buf("s1wd_bf")))
    emit_cast_to_dram(prog, wg_d, ffn_cast[0], "hcast_wg")
    emit_cast_to_dram(prog, wu_d, ffn_cast[1], "hcast_wu")
    emit_cast_to_dram(prog, wd_d, ffn_cast[2], "hcast_wd")
    w1_d = nc.dram_tensor("L1f_W1", [NEXP, D, 4096], F32, kind="ExternalInput")
    w2_d = nc.dram_tensor("L1f_W2", [NEXP, 4096, D], F32, kind="ExternalInput")
    moe_pre = {"w1bf": (dram(nc, "s5_w1bf", [NEXP, D, 4096], BF16), Buf("s5_w1bf")),
               "w2bf": (dram(nc, "s5_w2bf", [NEXP, 4096, D], BF16), Buf("s5_w2bf"))}

    scope_begin()
    wd = {k: W("L0aa_" + k, shp) for k, shp in
          [("W_kv", [D, KVR]), ("g_kv", [KVR]), ("W_kdec", [KVR, D]),
           ("W_vdec", [KVR, D]), ("W_q", [D, QR]), ("W_qdec", [QR, D]),
           ("W_krope", [D, QR]), ("W_qrope", [QR, QR]), ("W_o", [D, D])]}
    build_attn_mla(nc, prog, xio[0], xio[1], wd, cst, pfx="s0")
    scope_end()
    mc1 = emit_cast_to_dram(prog, w1_d, moe_pre["w1bf"], "hcast_w1")
    mc2 = emit_cast_to_dram(prog, w2_d, moe_pre["w2bf"], "hcast_w2")
    prog.fence("f0", exclude=(mc1, mc2))

    scope_begin()
    ffn_pre = build_ffn(nc, prog, xio[1], xio[2], wg_d, wu_d, wd_d, cst, pfx="s1",
                        precast=ffn_cast)
    scope_end(); prog.fence("f1")

    scope_begin()
    wd = {k: W("L0ab_" + k, shp) for k, shp in
          [("wq", [D, D]), ("wk", [D, D]), ("wv", [D, D]), ("wo", [D, D]), ("bo", [D])]}
    build_attn_lin(nc, prog, xio[2], xio[3], wd, cst, pfx="s2")
    scope_end(); prog.fence("f2")

    scope_begin()
    build_ffn(nc, prog, xio[3], xio[4], wg_d, wu_d, wd_d, cst, pre=ffn_pre, pfx="s3")
    scope_end(); prog.fence("f3")

    scope_begin()
    wd = {k: W("L1aa_" + k, shp) for k, shp in
          [("W_down", [D, D]), ("W_unify", [D, D]), ("b_unify", [D]),
           ("tokeys", [NH, 64, 64]), ("toqueries", [NH, 64, 64]), ("tovalues", [NH, 64, 64])]}
    build_attn_pa(nc, prog, xio[4], xio[5], wd, cst, pfx="s4")
    scope_end(); prog.fence("f4")

    scope_begin()
    wd = {"Wg": W("L1f_Wg", [D, NEXP]), "bg": W("L1f_bg", [NEXP]),
          "W1": w1_d, "b1": W("L1f_b1", [NEXP, 4096]),
          "W2": w2_d, "b2": W("L1f_b2", [NEXP, D]),
          "coremask": W("coremask", [8, 2])}
    build_moe_fused(nc, prog, xio[5], xio[6], wd, cst, pfx="s5", pre=moe_pre)
    scope_end(); prog.fence("f5")

    scope_begin()
    wd6 = {k: W("L1ab_" + k, shp) for k, shp in
           [("wq", [D, D]), ("bq", [D]), ("wk", [D, D]), ("bk", [D]),
            ("wv", [D, D]), ("bv", [D]), ("wo", [D, D]), ("bo", [D])]}
    build_attn_b(nc, prog, xio[6], xio[7], wd6, cst, pfx="s6")
    scope_end(); prog.fence("f6")

    scope_begin()
    build_moe_fused(nc, prog, xio[7], xio[8], wd, cst, pfx="s7", pre=moe_pre)
    scope_end()

    prog.run()
    return consts, cn


def kernel(**z):
    f32 = np.float32
    x = np.asarray(z["L0_attn_a_x"], dtype=f32).reshape(-1, D)

    def bf():
        nc = bass.Bass("TRN2", target_bir_lowering=False)
        consts, cn = build_fused(nc)
        return nc, consts, cn

    nc, consts, cn = _get("fused", bf)
    in_maps = []
    for c in range(NCORES):
        m = {"xs0": to_fm(x[c * TC:(c + 1) * TC])}
        for k in cn:
            m[k] = consts[k]
        for k in ["W_kv", "g_kv", "W_kdec", "W_vdec", "W_q", "W_qdec",
                  "W_krope", "W_qrope", "W_o"]:
            m["L0aa_" + k] = np.asarray(z["L0_attn_a_" + k])
        m["L0f_wg"] = np.asarray(z["L0_ffn_w_gate"])
        m["L0f_wu"] = np.asarray(z["L0_ffn_w_up"])
        m["L0f_wd"] = np.asarray(z["L0_ffn_w_down"])
        for kk, src in [("wq", "Wq"), ("wk", "Wk"), ("wv", "Wv"), ("wo", "Wo"), ("bo", "bo")]:
            m["L0ab_" + kk] = np.asarray(z["L0_attn_b_" + src])
        for k in ["W_down", "tokeys", "toqueries", "tovalues", "W_unify", "b_unify"]:
            m["L1aa_" + k] = np.asarray(z["L1_attn_a_" + k])
        for k in ["Wg", "bg", "W1", "b1", "W2", "b2"]:
            m["L1f_" + k] = np.asarray(z["L1_ffn_" + k])
        for kk, src in [("wq", "Wq"), ("bq", "bq"), ("wk", "Wk"), ("bk", "bk"),
                        ("wv", "Wv"), ("bv", "bv"), ("wo", "Wo"), ("bo", "bo")]:
            m["L1ab_" + kk] = np.asarray(z["L1_attn_b_" + src])
        cm = np.zeros((8, 2), np.float32)
        cm[:c, 0] = 1.0
        cm[:, 1] = 1.0
        m["coremask"] = cm
        in_maps.append(m)
    res = run_bass_kernel_spmd(nc, in_maps, core_ids=list(range(NCORES)))
    out = np.concatenate([from_fm(np.asarray(r["xs8"])) for r in res.results], axis=0)
    return out.reshape(16, TSEQ, D).astype(np.float32)


def kernel_unfused(**z):
    f32 = np.float32
    x = np.asarray(z["L0_attn_a_x"], dtype=f32).reshape(-1, D)
    fm = [to_fm(x[c * TC:(c + 1) * TC]) for c in range(NCORES)]

    cn = ["ones128", "identbf", "causal128", "ropecos", "ropesin", "ropeRT"]
    wspecs = [("W_kv", [D, KVR], F32), ("g_kv", [KVR], F32), ("W_kdec", [KVR, D], F32),
              ("W_vdec", [KVR, D], F32), ("W_q", [D, QR], F32), ("W_qdec", [QR, D], F32),
              ("W_krope", [D, QR], F32), ("W_qrope", [QR, QR], F32), ("W_o", [D, D], F32)]
    nc, consts = _get("s0", _mk_simple("ml", build_attn_mla, wspecs, cn))
    wmap = {k: np.asarray(z["L0_attn_a_" + k]) for k in
            ["W_kv", "g_kv", "W_kdec", "W_vdec", "W_q", "W_qdec", "W_krope", "W_qrope", "W_o"]}
    fm = _run(nc, consts, cn, fm, wmap)

    def mk_ffn():
        consts = host_consts()
        nc = bass.Bass("TRN2", target_bir_lowering=False)
        xin = nc.dram_tensor("xin", [DC, 128, TC], F32, kind="ExternalInput")
        xout = nc.dram_tensor("xout", [DC, 128, TC], F32, kind="ExternalOutput")
        wg_d = nc.dram_tensor("wg", [HID, D], I32, kind="ExternalInput")
        wu_d = nc.dram_tensor("wu", [HID, D], I32, kind="ExternalInput")
        wd_d = nc.dram_tensor("wd", [D, HID], I32, kind="ExternalInput")
        prog = Prog(nc, "ffn")
        cst = load_consts(nc, prog, consts, ["ones128"])
        build_ffn(nc, prog, xin, xout, wg_d, wu_d, wd_d, cst)
        prog.run()
        return nc, consts

    nc, consts = _get("ffn", mk_ffn)
    wmap = {"wg": np.asarray(z["L0_ffn_w_gate"]), "wu": np.asarray(z["L0_ffn_w_up"]),
            "wd": np.asarray(z["L0_ffn_w_down"])}
    fm = _run(nc, consts, ["ones128"], fm, wmap)

    cn = ["ones128", "onesf", "identbf"]
    wspecs = [("wq", [D, D], F32), ("wk", [D, D], F32), ("wv", [D, D], F32),
              ("wo", [D, D], F32), ("bo", [D], F32)]
    nc2, consts2 = _get("s2", _mk_simple("al", build_attn_lin, wspecs, cn))
    wmap = {"wq": np.asarray(z["L0_attn_b_Wq"]), "wk": np.asarray(z["L0_attn_b_Wk"]),
            "wv": np.asarray(z["L0_attn_b_Wv"]), "wo": np.asarray(z["L0_attn_b_Wo"]),
            "bo": np.asarray(z["L0_attn_b_bo"])}
    fm = _run(nc2, consts2, cn, fm, wmap)

    nc, consts = _get("ffn", mk_ffn)
    wmap = {"wg": np.asarray(z["L0_ffn_w_gate"]), "wu": np.asarray(z["L0_ffn_w_up"]),
            "wd": np.asarray(z["L0_ffn_w_down"])}
    fm = _run(nc, consts, ["ones128"], fm, wmap)

    cn = ["ones128", "identbf"]
    wspecs = [("W_down", [D, D], F32), ("W_unify", [D, D], F32), ("b_unify", [D], F32),
              ("tokeys", [NH, 64, 64], F32), ("toqueries", [NH, 64, 64], F32),
              ("tovalues", [NH, 64, 64], F32)]
    nc4, consts4 = _get("s4", _mk_simple("pa", build_attn_pa, wspecs, cn))
    wmap = {k: np.asarray(z["L1_attn_a_" + k]) for k in
            ["W_down", "tokeys", "toqueries", "tovalues", "W_unify", "b_unify"]}
    fm = _run(nc4, consts4, cn, fm, wmap)

    moe_w = (np.asarray(z["L1_ffn_Wg"]), np.asarray(z["L1_ffn_bg"]),
             np.asarray(z["L1_ffn_W1"]), np.asarray(z["L1_ffn_b1"]),
             np.asarray(z["L1_ffn_W2"]), np.asarray(z["L1_ffn_b2"]))
    fm = _moe_call_fm(fm, *moe_w)

    cn = ["ones128", "onesf", "identbf", "causal128"]
    wspecs = [("wq", [D, D], F32), ("bq", [D], F32), ("wk", [D, D], F32), ("bk", [D], F32),
              ("wv", [D, D], F32), ("bv", [D], F32), ("wo", [D, D], F32), ("bo", [D], F32)]
    nc6, consts6 = _get("s6", _mk_simple("ab", build_attn_b, wspecs, cn))
    wmap = {"wq": np.asarray(z["L1_attn_b_Wq"]), "bq": np.asarray(z["L1_attn_b_bq"]),
            "wk": np.asarray(z["L1_attn_b_Wk"]), "bk": np.asarray(z["L1_attn_b_bk"]),
            "wv": np.asarray(z["L1_attn_b_Wv"]), "bv": np.asarray(z["L1_attn_b_bv"]),
            "wo": np.asarray(z["L1_attn_b_Wo"]), "bo": np.asarray(z["L1_attn_b_bo"])}
    fm = _run(nc6, consts6, cn, fm, wmap)

    fm = _moe_call_fm(fm, *moe_w)

    out = np.concatenate([from_fm(f) for f in fm], axis=0)
    return out.reshape(16, TSEQ, D).astype(np.float32)


def _moe_call_fm(fm, Wg, bg, W1, b1, W2, b2):
    consts = host_consts()

    def mk_lg():
        nc = bass.Bass("TRN2", target_bir_lowering=False)
        xin = nc.dram_tensor("xin", [DC, 128, TC], F32, kind="ExternalInput")
        lg_out = nc.dram_tensor("lgout", [8, TC], F32, kind="ExternalOutput")
        wg_d = nc.dram_tensor("Wg", [D, NEXP], F32, kind="ExternalInput")
        bg_d = nc.dram_tensor("bg", [NEXP], F32, kind="ExternalInput")
        prog = Prog(nc, "mg")
        cst = load_consts(nc, prog, consts, ["ones128", "onesf"])
        build_moe_logits(nc, prog, xin, lg_out, wg_d, bg_d, cst)
        prog.run()
        return nc, consts

    nc, cs = _get("moelg", mk_lg)
    in_maps = [{"xin": fm[c], "Wg": Wg, "bg": bg, "ones128": cs["ones128"],
                "onesf": cs["onesf"]} for c in range(NCORES)]
    res = run_bass_kernel_spmd(nc, in_maps, core_ids=list(range(NCORES)))
    logits = np.concatenate([np.asarray(r["lgout"]).T for r in res.results], axis=0)
    lgt = np.ascontiguousarray(logits.reshape(NTILES, 128, NEXP).transpose(1, 0, 2))

    def mk_gt():
        nc = bass.Bass("TRN2", target_bir_lowering=False)
        lgt_d = nc.dram_tensor("lgt", [128, NTILES, NEXP], F32, kind="ExternalInput")
        w_out = nc.dram_tensor("wout", [128, NTILES, NEXP], F32, kind="ExternalOutput")
        prog = Prog(nc, "gt")
        cst = load_consts(nc, prog, consts, ["ones128", "onesf", "iota8t", "uppstrict"])
        build_moe_gate(nc, prog, lgt_d, w_out, cst)
        prog.run()
        return nc, consts

    nc, cs = _get("moegt", mk_gt)
    in_maps = [{"lgt": lgt, "ones128": cs["ones128"], "onesf": cs["onesf"],
                "iota8t": cs["iota8t"], "uppstrict": cs["uppstrict"]}
               for c in range(NCORES)]
    res = run_bass_kernel_spmd(nc, in_maps, core_ids=list(range(NCORES)))
    w_tiled = np.asarray(res.results[0]["wout"])
    w_all = np.ascontiguousarray(w_tiled.transpose(1, 0, 2).reshape(32768, NEXP))

    def mk_mlp():
        nc = bass.Bass("TRN2", target_bir_lowering=False)
        xin = nc.dram_tensor("xin", [DC, 128, TC], F32, kind="ExternalInput")
        xout = nc.dram_tensor("xout", [DC, 128, TC], F32, kind="ExternalOutput")
        w_own_d = nc.dram_tensor("wown", [128, 32, NEXP], F32, kind="ExternalInput")
        wT_d = nc.dram_tensor("wT", [NEXP, TC], F32, kind="ExternalInput")
        w1_d = nc.dram_tensor("W1", [NEXP, D, 4096], F32, kind="ExternalInput")
        b1_d = nc.dram_tensor("b1", [NEXP, 4096], F32, kind="ExternalInput")
        w2_d = nc.dram_tensor("W2", [NEXP, 4096, D], F32, kind="ExternalInput")
        b2_d = nc.dram_tensor("b2", [NEXP, D], F32, kind="ExternalInput")
        prog = Prog(nc, "mm")
        cst = load_consts(nc, prog, consts, ["ones128", "onesf"])
        build_moe_mlp(nc, prog, xin, xout, w_own_d, wT_d, w1_d, b1_d, w2_d, b2_d, cst)
        prog.run()
        return nc, consts

    nc, cs = _get("moemlp", mk_mlp)
    in_maps = []
    for c in range(NCORES):
        w_own = w_all[c * TC:(c + 1) * TC]
        in_maps.append({
            "xin": fm[c],
            "wown": np.ascontiguousarray(w_own.reshape(32, 128, NEXP).transpose(1, 0, 2)),
            "wT": np.ascontiguousarray(w_own.T),
            "W1": W1, "b1": b1, "W2": W2, "b2": b2,
            "ones128": cs["ones128"], "onesf": cs["onesf"]})
    res = run_bass_kernel_spmd(nc, in_maps, core_ids=list(range(NCORES)))
    return [np.asarray(r["xout"]) for r in res.results]


# version 09
# speedup vs baseline: 1.0185x; 1.0185x over previous
import numpy as np
import ml_dtypes
import concourse.bass as bass
import concourse.mybir as mybir
from concourse.bass_utils import run_bass_kernel_spmd

F32 = mybir.dt.float32
BF16 = mybir.dt.bfloat16
I32 = mybir.dt.int32
AF = mybir.ActivationFunctionType
ALU = mybir.AluOpType
AX = mybir.AxisListType

D = 1024
DC = D // 128
TC = 4096
TSEQ = 2048
NB = TC // 512
BN = 512
NCORES = 8

BF16NP = ml_dtypes.bfloat16


ENGS = ("t", "v", "s", "p", "sd", "cd", "gd")
SEQMAP = {"tensor": ("t",), "vector": ("v",), "scalar": ("s", "cd"),
          "gpsimd": ("p", "gd"), "sync": ("sd",)}


class Op:
    __slots__ = ("eng", "fn", "after", "target", "n_inc", "tag", "idx", "key")

    def __init__(self, eng, fn, after, n_inc, tag, idx, key):
        self.eng = eng
        self.fn = fn
        self.after = list(after)
        self.n_inc = n_inc
        self.target = None
        self.tag = tag
        self.idx = idx
        self.key = key


class Buf:
    __slots__ = ("name", "last_writer", "readers")

    def __init__(self, name):
        self.name = name
        self.last_writer = None
        self.readers = []


class Prog:

    def __init__(self, nc, name="prog"):
        self.nc = nc
        self.name = name
        self.ops = []
        self.sems = {e: nc.alloc_semaphore(f"{name}_{e}") for e in ("t", "v", "s", "p")}
        self.counts = {e: 0 for e in ("t", "v", "s", "p")}

    def _key_sem(self, key):
        km = getattr(self, "_keymap", None)
        if km is None:
            km = self._keymap = {}
        if key in km:
            return km[key]
        slot = f"dmapool{len(km)}"
        km[key] = slot
        if slot not in self.sems:
            self.sems[slot] = self.nc.alloc_semaphore(f"{self.name}_{slot}")
            self.counts[slot] = 0
        return slot

    def add(self, eng, fn, after=(), tag="", key=None):
        after = [a for a in after if a is not None]
        if eng in ("sd", "cd", "gd"):
            assert key is not None, f"DMA op needs a sem key: {tag}"
            key = self._key_sem(key)
            n_inc = 16
        else:
            key = eng
            n_inc = 1
        if getattr(self, "_fence", None) is not None:
            after.append(self._fence)
        op = Op(eng, fn, after, n_inc, tag, len(self.ops), key)
        self.ops.append(op)
        return op

    def fence(self, tag="fence"):
        last = {}
        for op in self.ops:
            last[op.key] = op
        f = self.add("v", lambda e: e.engine_nop(), after=list(last.values()),
                     tag=tag)
        self._fence = f
        self._keymap = {}
        return f

    def rd(self, op, *bufs):
        for b in bufs:
            if b is None:
                continue
            if b.last_writer is not None and b.last_writer is not op:
                op.after.append(b.last_writer)
            b.readers.append(op)
        return op

    def wr(self, op, *bufs):
        for b in bufs:
            if b is None:
                continue
            if b.last_writer is not None and b.last_writer is not op:
                op.after.append(b.last_writer)
            op.after.extend(r for r in b.readers if r is not op)
            b.last_writer = op
            b.readers = []
        return op

    def run(self):
        for op in self.ops:
            self.counts[op.key] += op.n_inc
            op.target = self.counts[op.key]
        nc = self.nc
        seq_of = {}
        for seq, engs in SEQMAP.items():
            for e in engs:
                seq_of[e] = seq
        seq_ops = {seq: [o for o in self.ops if seq_of[o.eng] == seq]
                   for seq in SEQMAP}
        COMPUTE = ("t", "v", "s", "p")

        def body(seqname):
            def _(eng):
                hw = {}
                for op in seq_ops[seqname]:
                    waits = {}
                    for dep in op.after:
                        if (dep.eng == op.eng and dep.eng == "t"
                                and dep.idx < op.idx):
                            continue
                        if dep.target > hw.get(dep.key, 0):
                            waits[dep.key] = max(waits.get(dep.key, 0),
                                                 dep.target)
                    for k2 in sorted(waits):
                        eng.wait_ge(self.sems[k2], waits[k2])
                        hw[k2] = waits[k2]
                    last = op.fn(eng)
                    assert last is not None, f"op fn returned None: {op.tag}"
                    last.then_inc(self.sems[op.key], op.n_inc)
            return _

        with nc.Block() as blk:
            if seq_ops["tensor"]:
                blk.tensor(body("tensor"))
            if seq_ops["vector"]:
                blk.vector(body("vector"))
            if seq_ops["scalar"]:
                blk.scalar(body("scalar"))
            if seq_ops["gpsimd"]:
                blk.gpsimd(body("gpsimd"))
            if seq_ops["sync"]:
                blk.sync(body("sync"))

        targets = dict(self.counts)
        with nc.Block() as blk2:
            @blk2.sync
            def _(sy):
                for k, tv in targets.items():
                    if tv:
                        sy.wait_ge(self.sems[k], tv)


def mmgroup(prog, psum_items, pairs, after=(), bufs_rd=(), bufs_wr=(),
            tag="mm", start=True, stop=True):
    n = len(pairs)
    psum_ap = psum_items[0]

    def fn(t):
        ins = None
        for j, (lh, rh) in enumerate(pairs):
            ins = t.matmul(psum_ap, lh, rh,
                           start=(start and j == 0), stop=(stop and j == n - 1))
        return ins

    op = prog.add("t", fn, after, tag)
    prog.rd(op, *bufs_rd)
    prog.wr(op, psum_items[1], *bufs_wr)
    return op



_SCOPES = []


def scope_begin():
    _SCOPES.append([])


def scope_end():
    for g in reversed(_SCOPES.pop()):
        g.__exit__(None, None, None)


def sb(nc, name, shape, dtype, **kw):
    if _SCOPES:
        g = nc.sbuf_tensor(name, shape, dtype, **kw)
        h = g.__enter__()
        _SCOPES[-1].append(g)
    else:
        h = nc.alloc_sbuf_tensor(name, shape, dtype, **kw)
    return (h[:], Buf(name), h)


def psum(nc, name, shape, dtype=F32):
    if _SCOPES:
        g = nc.psum_tensor(name, shape, dtype)
        h = g.__enter__()
        _SCOPES[-1].append(g)
    else:
        h = nc.alloc_psum_tensor(name, shape, dtype)
    return (h[:], Buf(name), h)


def dram(nc, name, shape, dtype=F32, kind="Internal"):
    h = nc.dram_tensor(name, shape, dtype, kind=kind)
    return h



def mk_ln_scratch(nc, tag="", psA=None, psB=None):
    sc = {}
    sc["xb"] = [sb(nc, f"lnxb{tag}{c}", [128, BN], BF16) for c in range(DC)]
    sc["sq"] = [sb(nc, f"lnsq{tag}{c}", [128, BN], BF16) for c in range(DC)]
    sc["psA"] = psA if psA is not None else psum(nc, f"lnpsA{tag}", [128, BN])
    sc["psB"] = psB if psB is not None else psum(nc, f"lnpsB{tag}", [128, BN])
    for nm in ("mean", "m2", "var", "rstd", "mr"):
        sc[nm] = sb(nc, f"ln{nm}{tag}", [128, BN], F32)
    sc["tmp"] = [sb(nc, f"lntmp{tag}{i}", [128, BN], F32) for i in range(2)]
    return sc


def emit_ln(prog, X8b, XLb, sc, ones, eps=1e-5, stats=None, norm=True):
    xb, sq = sc["xb"], sc["sq"]
    rstd = sc["rstd"] if stats is None else stats["rstd"]
    mr = sc["mr"] if stats is None else stats["mr"]
    for c in range(DC):
        op = prog.add("v", lambda e, c=c: e.tensor_copy(xb[c][0], X8b[c][0]),
                      tag=f"lncast{c}")
        prog.rd(op, X8b[c][1]); prog.wr(op, xb[c][1])
        sop = prog.add("s", lambda e, c=c: e.activation(sq[c][0], X8b[c][0], AF.Square),
                       tag=f"lnsq{c}")
        prog.rd(sop, X8b[c][1]); prog.wr(sop, sq[c][1])

    mmgroup(prog, sc["psA"], [(ones[0], xb[c][0]) for c in range(DC)],
            bufs_rd=[ones[1]] + [xb[c][1] for c in range(DC)], tag="ln_sum")
    mmgroup(prog, sc["psB"], [(ones[0], sq[c][0]) for c in range(DC)],
            bufs_rd=[ones[1]] + [sq[c][1] for c in range(DC)], tag="ln_ssq")

    mean, m2, var = sc["mean"], sc["m2"], sc["var"]
    o = prog.add("v", lambda e: e.tensor_scalar(mean[0], sc["psA"][0], 1.0 / D, None, ALU.mult), tag="ln_mean")
    prog.rd(o, sc["psA"][1]); prog.wr(o, mean[1])
    o = prog.add("v", lambda e: e.tensor_mul(m2[0], mean[0], mean[0]), tag="ln_m2")
    prog.rd(o, mean[1]); prog.wr(o, m2[1])
    o = prog.add("v", lambda e: e.scalar_tensor_tensor(var[0], sc["psB"][0], 1.0 / D, m2[0], ALU.mult, ALU.subtract), tag="ln_var")
    prog.rd(o, sc["psB"][1], m2[1]); prog.wr(o, var[1])
    o = prog.add("v", lambda e: e.tensor_scalar(var[0], var[0], eps, None, ALU.add), tag="ln_veps")
    prog.rd(o, var[1]); prog.wr(o, var[1])
    o = prog.add("v", lambda e: e.reciprocal(var[0], var[0]), tag="ln_rec")
    prog.rd(o, var[1]); prog.wr(o, var[1])
    o = prog.add("s", lambda e: e.activation(rstd[0], var[0], AF.Sqrt), tag="ln_rstd")
    prog.rd(o, var[1]); prog.wr(o, rstd[1])
    o = prog.add("v", lambda e: e.tensor_mul(mr[0], mean[0], rstd[0]), tag="ln_mr")
    prog.rd(o, mean[1], rstd[1]); prog.wr(o, mr[1])
    if stats is not None and "mean" in stats:
        mo = prog.add("v", lambda e: e.tensor_copy(stats["mean"][0], mean[0]), tag="ln_meancp")
        prog.rd(mo, mean[1]); prog.wr(mo, stats["mean"][1])

    if not norm:
        return
    for c in range(DC):
        eng = "v"
        tm = sc["tmp"][c % 2]
        o1 = prog.add(eng, lambda e, c=c, tm=tm: e.tensor_mul(tm[0], X8b[c][0], rstd[0]), tag=f"lnn1_{c}")
        prog.rd(o1, X8b[c][1], rstd[1]); prog.wr(o1, tm[1])
        o2 = prog.add(eng, lambda e, c=c, tm=tm: e.tensor_sub(XLb[c][0], tm[0], mr[0]), tag=f"lnn2_{c}")
        prog.rd(o2, tm[1], mr[1]); prog.wr(o2, XLb[c][1])



def emit_cast_to_dram(prog, src_d, dst, tag):
    op = prog.add("gd", lambda e: e.dma_start(dst[0][:], src_d[:]), tag=tag,
                  key=dst[1].name)
    prog.wr(op, dst[1])
    return op


def emit_transpose_stage(prog, src_bf, dst_T, rows, cols, stage, tag):
    for c in range(cols // 128):
        st = stage[c % 2]
        lo = prog.add("sd", lambda e, c=c, st=st: e.dma_start(
            st[0][:, 0:rows], src_bf[0][:][:, c * 128:(c + 1) * 128],
            transpose=True), tag=f"{tag}_t{c}", key=st[1].name)
        prog.rd(lo, src_bf[1])
        prog.wr(lo, st[1])
        so = prog.add("sd", lambda e, c=c, st=st: e.dma_start(
            dst_T[0][:][c], st[0][:, 0:rows]), tag=f"{tag}_s{c}", key=st[1].name)
        prog.rd(so, st[1])
        prog.wr(so, dst_T[1])



HID = 3072
HC = HID // 128


def build_ffn(nc, prog, xin, xout, wg_d, wu_d, wd_d, cst, pre=None, pfx="f"):
    ones = cst["ones128"]

    if pre is None:
        wg_bf = (dram(nc, f"{pfx}wg_bf", [HID, D], BF16), Buf(f"{pfx}wg_bf"))
        wu_bf = (dram(nc, f"{pfx}wu_bf", [HID, D], BF16), Buf(f"{pfx}wu_bf"))
        wd_bf = (dram(nc, f"{pfx}wd_bf", [D, HID], BF16), Buf(f"{pfx}wd_bf"))
        wgT = (dram(nc, f"{pfx}wgT", [DC, 128, HID], BF16), Buf(f"{pfx}wgT"))
        wuT = (dram(nc, f"{pfx}wuT", [DC, 128, HID], BF16), Buf(f"{pfx}wuT"))
        wdT = (dram(nc, f"{pfx}wdT", [HC, 128, D], BF16), Buf(f"{pfx}wdT"))
        emit_cast_to_dram(prog, wg_d, wg_bf, "cast_wg")
        emit_cast_to_dram(prog, wu_d, wu_bf, "cast_wu")
        emit_cast_to_dram(prog, wd_d, wd_bf, "cast_wd")
        stg = [sb(nc, f"{pfx}wstg{i}", [128, max(HID, D)], BF16,
                  is_transpose_dest=True) for i in range(2)]
        emit_transpose_stage(prog, wg_bf, wgT, HID, D, stg, "wgT")
        emit_transpose_stage(prog, wu_bf, wuT, HID, D, stg, "wuT")
        emit_transpose_stage(prog, wd_bf, wdT, D, HID, stg, "wdT")
    else:
        wgT, wuT, wdT = pre["wgT"], pre["wuT"], pre["wdT"]

    GW = 256
    NGS = HID // GW
    wg_sl = [sb(nc, f"{pfx}wgsl{i}", [128, DC * GW], BF16) for i in range(2)]
    wu_sl = [sb(nc, f"{pfx}wusl{i}", [128, DC * GW], BF16) for i in range(2)]
    DW = 256
    NDS = D // DW
    wd_sl = [sb(nc, f"{pfx}wdsl{i}", [128, HC * DW], BF16) for i in range(2)]

    lnsc = mk_ln_scratch(nc, pfx)
    xbat = [sb(nc, f"{pfx}xb{i}", [128, DC * BN], F32) for i in range(2)]
    X8b = [[(xbat[i][0][:, c * BN:(c + 1) * BN], xbat[i][1]) for c in range(DC)]
           for i in range(2)]
    XLb = [[sb(nc, f"{pfx}xl{c}_{i}", [128, BN], BF16) for c in range(DC)] for i in range(2)]
    sg = [sb(nc, f"{pfx}sg{a}", [128, BN], BF16) for a in range(2)]
    hh = [sb(nc, f"{pfx}hh{a}", [128, BN], BF16) for a in range(HC)]
    xob = [sb(nc, f"{pfx}xo{i}", [128, DC * BN], F32) for i in range(2)]
    xo = [[(xob[i][0][:, c * BN:(c + 1) * BN], xob[i][1]) for c in range(DC)]
          for i in range(2)]
    pp = [psum(nc, f"{pfx}ps{i}", [128, BN]) for i in range(4)]
    SCALE2 = 1.0 / (127.0 * 127.0)

    pi = 0
    for b in range(NB):
        dbl = b % 2
        src = xin[:].rearrange("c p t -> p c t")[:, :, b * BN:(b + 1) * BN]
        op = prog.add("sd", lambda e, d=xbat[dbl][0].rearrange("p (c t) -> p c t", c=DC), s=src: e.dma_start(d, s),
                      tag=f"ldx{b}", key=xbat[dbl][1].name)
        prog.wr(op, xbat[dbl][1])
        emit_ln(prog, X8b[dbl], XLb[dbl], lnsc, ones)

        for g in range(NGS):
            wsl = wg_sl[g % 2]
            usl = wu_sl[g % 2]
            src = wgT[0][:].rearrange("c p h -> p c h")[:, :, g * GW:(g + 1) * GW]
            lo = prog.add("cd", lambda e, d=wsl[0], s=src: e.dma_start(d, s), tag=f"ldwg{b}_{g}", key=wsl[1].name)
            prog.rd(lo, wgT[1]); prog.wr(lo, wsl[1])
            src = wuT[0][:].rearrange("c p h -> p c h")[:, :, g * GW:(g + 1) * GW]
            lo = prog.add("cd", lambda e, d=usl[0], s=src: e.dma_start(d, s), tag=f"ldwu{b}_{g}", key=usl[1].name)
            prog.rd(lo, wuT[1]); prog.wr(lo, usl[1])
            for ha in (2 * g, 2 * g + 1):
                off = (ha - 2 * g) * 128
                pg = pp[pi % 4]; pi += 1
                mmgroup(prog, pg,
                        [(wsl[0][:, c * GW + off: c * GW + off + 128], XLb[dbl][c][0]) for c in range(DC)],
                        bufs_rd=[wsl[1]] + [XLb[dbl][c][1] for c in range(DC)],
                        tag=f"gate{b}_{ha}")
                sgt = sg[ha % 2]
                o = prog.add("s", lambda e, d=sgt[0], s=pg[0]: e.activation(
                    d, s, AF.Silu, scale=1.0 / 127.0), tag=f"silu{b}_{ha}")
                prog.rd(o, pg[1]); prog.wr(o, sgt[1])
                pu = pp[pi % 4]; pi += 1
                mmgroup(prog, pu,
                        [(usl[0][:, c * GW + off: c * GW + off + 128], XLb[dbl][c][0]) for c in range(DC)],
                        bufs_rd=[usl[1]] + [XLb[dbl][c][1] for c in range(DC)],
                        tag=f"up{b}_{ha}")
                o = prog.add("v", lambda e, d=hh[ha][0], s=pu[0], g2=sgt[0]: e.tensor_mul(d, g2, s), tag=f"hmul{b}_{ha}")
                prog.rd(o, pu[1], sgt[1]); prog.wr(o, hh[ha][1])

        for ds in range(NDS):
            dsl = wd_sl[ds % 2]
            src = wdT[0][:].rearrange("a p d -> p a d")[:, :, ds * DW:(ds + 1) * DW]
            lo = prog.add("cd", lambda e, d=dsl[0], s=src: e.dma_start(d, s), tag=f"ldwd{b}_{ds}", key=dsl[1].name)
            prog.rd(lo, wdT[1]); prog.wr(lo, dsl[1])
            for c in (2 * ds, 2 * ds + 1):
                off = (c - 2 * ds) * 128
                pd = pp[pi % 4]; pi += 1
                mmgroup(prog, pd,
                        [(dsl[0][:, a * DW + off: a * DW + off + 128], hh[a][0]) for a in range(HC)],
                        bufs_rd=[dsl[1]] + [hh[a][1] for a in range(HC)],
                        tag=f"down{b}_{c}")
                o = prog.add("v", lambda e, d=xo[dbl][c][0], s=pd[0], x=X8b[dbl][c][0]:
                             e.scalar_tensor_tensor(d, s, SCALE2, x, ALU.mult, ALU.add), tag=f"res{b}_{c}")
                prog.rd(o, pd[1], X8b[dbl][c][1]); prog.wr(o, xo[dbl][c][1])
        dst = xout[:].rearrange("c p t -> p c t")[:, :, b * BN:(b + 1) * BN]
        so = prog.add("sd", lambda e, d=dst, s=xob[dbl][0].rearrange("p (c t) -> p c t", c=DC): e.dma_start(d, s),
                      tag=f"st{b}", key=xob[dbl][1].name)
        prog.rd(so, xob[dbl][1])



NH = 16
HD = 64
NQT = TSEQ // 128
NQB = TSEQ // 512


def bias_cols(nc, prog, pfx, name, b_dram, pp, dim=D):
    nch = dim // 128
    row = sb(nc, f"{pfx}_{name}row", [1, dim], F32)
    op = prog.add("sd", lambda e: e.dma_start(row[0], b_dram[:].rearrange("(o f) -> o f", o=1)),
                  tag=f"{pfx}ldb_{name}", key=row[1].name)
    prog.wr(op, row[1])
    cols = sb(nc, f"{pfx}_{name}cols", [128, nch], F32)
    ones1 = sb(nc, f"{pfx}_{name}one", [1, 1], F32)
    o = prog.add("v", lambda e: e.memset(ones1[0], 1.0), tag=f"{pfx}ones1_{name}")
    prog.wr(o, ones1[1])
    for c in range(nch):
        ps_ = pp[c % 2]
        mo = mmgroup(prog, (ps_[0][:, 0:1], ps_[1]), [(row[0][:, c * 128:(c + 1) * 128], ones1[0])],
                     bufs_rd=[row[1], ones1[1]], tag=f"{pfx}bT_{name}{c}")
        eo = prog.add("v", lambda e, c=c, ps_=ps_: e.tensor_copy(cols[0][:, c:c + 1], ps_[0][:, 0:1]),
                      tag=f"{pfx}bTe_{name}{c}")
        prog.rd(eo, ps_[1]); prog.wr(eo, cols[1])
    return cols


def bias_bc(nc, prog, pfx, name, b_dram, onesf, pp):
    row = sb(nc, f"{pfx}_{name}row", [1, D], F32)
    op = prog.add("sd", lambda e: e.dma_start(row[0], b_dram[:].rearrange("(o f) -> o f", o=1)),
                  tag=f"{pfx}ldbr_{name}", key=row[1].name)
    prog.wr(op, row[1])
    bc = sb(nc, f"{pfx}_{name}bc", [128, D], F32)
    for half in range(2):
        ps_ = pp[half % 2]
        mo = mmgroup(prog, ps_, [(onesf[0][0:1, :], row[0][:, half * 512:(half + 1) * 512])],
                     bufs_rd=[row[1]], tag=f"{pfx}bc_{name}{half}")
        eo = prog.add("v", lambda e, half=half, ps_=ps_: e.tensor_copy(bc[0][:, half * 512:(half + 1) * 512], ps_[0]),
                      tag=f"{pfx}bce_{name}{half}")
        prog.rd(eo, ps_[1]); prog.wr(eo, bc[1])
    return bc


def load_wslice(nc, prog, slot, w_dram, wbuf, m, tag):
    src = w_dram[:].rearrange("(kc p) f -> p kc f", p=128)[:, :, m * 128:(m + 1) * 128]
    lo = prog.add("gd", lambda e: e.dma_start(slot[0], src), tag=tag, key=slot[1].name)
    if wbuf is not None:
        prog.rd(lo, wbuf)
    prog.wr(lo, slot[1])
    return lo


def emit_out_proj(nc, prog, pfx, a_scr, xin, xout, wo_d, bo_c, wsl, pp,
                  xbat, X8b, scale=None):
    ab_buf = [sb(nc, f"{pfx}_abb0", [128, DC * BN], BF16)] * 2
    xob = [sb(nc, f"{pfx}_xo0", [128, DC * BN], F32)] * 2
    for b in range(NB):
        dbl = b % 2
        src = a_scr[0][:].rearrange("c p t -> p c t")[:, :, b * BN:(b + 1) * BN]
        op = prog.add("sd", lambda e, d=ab_buf[dbl][0].rearrange("p (c t) -> p c t", c=DC), s2=src: e.dma_start(d, s2),
                      tag=f"{pfx}lda{b}", key=ab_buf[dbl][1].name)
        prog.rd(op, a_scr[1]); prog.wr(op, ab_buf[dbl][1])
        src = xin[:].rearrange("c p t -> p c t")[:, :, b * BN:(b + 1) * BN]
        op = prog.add("sd", lambda e, d=xbat[dbl][0].rearrange("p (c t) -> p c t", c=DC), s2=src: e.dma_start(d, s2),
                      tag=f"{pfx}ldx2{b}", key=xbat[dbl][1].name)
        prog.wr(op, xbat[dbl][1])
        for m in range(DC):
            slot = wsl[m % 4]
            load_wslice(nc, prog, slot, wo_d, None, m, f"{pfx}lwo{b}{m}")
            pg = pp[m % 2]
            mmgroup(prog, pg,
                    [(slot[0][:, kc * 128:(kc + 1) * 128], ab_buf[dbl][0][:, kc * BN:(kc + 1) * BN]) for kc in range(DC)],
                    bufs_rd=[slot[1], ab_buf[dbl][1]],
                    tag=f"{pfx}wo{b}_{m}")
            if bo_c is not None:
                o = prog.add("v", lambda e, d=xob[dbl][0][:, m * BN:(m + 1) * BN], p2=pg, m=m, x=X8b[dbl][m][0]:
                             e.scalar_tensor_tensor(d, p2[0], bo_c[0][:, m:m + 1], x, ALU.add, ALU.add),
                             tag=f"{pfx}res{b}_{m}")
                prog.rd(o, pg[1], bo_c[1], xbat[dbl][1]); prog.wr(o, xob[dbl][1])
            else:
                sc2 = 1.0 if scale is None else scale
                o = prog.add("v", lambda e, d=xob[dbl][0][:, m * BN:(m + 1) * BN], p2=pg, m=m, x=X8b[dbl][m][0], sc2=sc2:
                             e.scalar_tensor_tensor(d, p2[0], sc2, x, ALU.mult, ALU.add),
                             tag=f"{pfx}res{b}_{m}")
                prog.rd(o, pg[1], xbat[dbl][1]); prog.wr(o, xob[dbl][1])
        dst = xout[:].rearrange("c p t -> p c t")[:, :, b * BN:(b + 1) * BN]
        so = prog.add("sd", lambda e, d=dst, s2=xob[dbl][0].rearrange("p (c t) -> p c t", c=DC): e.dma_start(d, s2),
                      tag=f"{pfx}st{b}", key=xob[dbl][1].name)
        prog.rd(so, xob[dbl][1])


def build_attn_b(nc, prog, xin, xout, wd, cst, pfx="ab", eps=1e-5):
    ones = cst["ones128"]
    onesf = cst["onesf"]
    identbf = cst["identbf"]
    causal = cst["causal128"]

    q_scr = (dram(nc, f"{pfx}_qscr", [DC, 128, TSEQ], BF16), Buf(f"{pfx}_qscr"))
    k_scr = (dram(nc, f"{pfx}_kscr", [DC, 128, TSEQ], BF16), Buf(f"{pfx}_kscr"))
    v_scr = (dram(nc, f"{pfx}_vscr", [NQT, 128, D], BF16), Buf(f"{pfx}_vscr"))
    a_scr = (dram(nc, f"{pfx}_ascr", [DC, 128, TC], BF16), Buf(f"{pfx}_ascr"))

    pp = [psum(nc, f"{pfx}_pp{i}", [128, BN]) for i in range(2)]
    sc_ps = psum(nc, f"{pfx}_scps", [128, 2048])
    sc_bk = [Buf(f"{pfx}_scb{i}") for i in range(4)]
    pt_ps = psum(nc, f"{pfx}_ptps", [128, BN], BF16)
    pv_ps = psum(nc, f"{pfx}_pvps", [128, BN])

    lnsc = mk_ln_scratch(nc, pfx, psA=pp[0], psB=pp[1])

    xbat = [sb(nc, f"{pfx}_xb0", [128, DC * BN], F32)] * 2
    X8b = [[(xbat[i][0][:, c * BN:(c + 1) * BN], xbat[i][1]) for c in range(DC)]
           for i in range(2)]
    XLb = [[sb(nc, f"{pfx}_xl{c}_{i}", [128, BN], BF16) for c in range(DC)] for i in range(2)]
    wsl = [sb(nc, f"{pfx}_wsl{i}", [128, D], BF16) for i in range(4)]
    wv_t = sb(nc, f"{pfx}_wvt", [128, DC * D], BF16)
    qb_buf = [sb(nc, f"{pfx}_qbb0", [128, DC * BN], BF16)] * 2
    kb_buf = [sb(nc, f"{pfx}_kbb0", [128, DC * BN], BF16)] * 2
    vb_buf = [sb(nc, f"{pfx}_vbb0", [128, 4 * D], BF16)] * 2

    bq_c = bias_cols(nc, prog, pfx, "bq", wd["bq"], pp)
    bk_c = bias_cols(nc, prog, pfx, "bk", wd["bk"], pp)
    bo_c = bias_cols(nc, prog, pfx, "bo", wd["bo"], pp)
    bv_b = bias_bc(nc, prog, pfx, "bv", wd["bv"], onesf, pp)

    lo = prog.add("gd", lambda e: e.dma_start(
        wv_t[0], wd["wv"][:].rearrange("(kc p) f -> p kc f", p=128)),
        tag=f"{pfx}ldwv", key=wv_t[1].name)
    prog.wr(lo, wv_t[1])

    SCALE_Q = 1.0 / np.sqrt(HD)

    for s in range(2):
        for nb in range(NQB):
            dbl = nb % 2
            tok0 = s * TSEQ + nb * BN
            src = xin[:].rearrange("c p t -> p c t")[:, :, tok0:tok0 + BN]
            op = prog.add("sd", lambda e, d=xbat[dbl][0].rearrange("p (c t) -> p c t", c=DC), s2=src: e.dma_start(d, s2),
                          tag=f"{pfx}ldx{s}_{nb}", key=xbat[dbl][1].name)
            prog.wr(op, xbat[dbl][1])
            emit_ln(prog, X8b[dbl], XLb[dbl], lnsc, ones, eps=eps)

            for which, wname, bias, scl, obuf in (
                    (0, "wq", bq_c, SCALE_Q, qb_buf), (1, "wk", bk_c, 1.0, kb_buf)):
                for m in range(DC):
                    slot = wsl[(which * DC + m) % 4]
                    load_wslice(nc, prog, slot, wd[wname], None, m, f"{pfx}lw{wname}{s}{nb}{m}")
                    pg = pp[m % 2]
                    mmgroup(prog, pg,
                            [(slot[0][:, kc * 128:(kc + 1) * 128], XLb[dbl][kc][0]) for kc in range(DC)],
                            bufs_rd=[slot[1]] + [XLb[dbl][kc][1] for kc in range(DC)],
                            tag=f"{pfx}p{wname}{s}_{nb}_{m}")
                    dst = obuf[dbl][0][:, m * BN:(m + 1) * BN]
                    o = prog.add("v", lambda e, d=dst, p2=pg, b3=bias, m=m, scl=scl:
                                 e.tensor_scalar(d, p2[0], b3[0][:, m:m + 1], scl, ALU.add, ALU.mult),
                                 tag=f"{pfx}e{wname}{s}_{nb}_{m}")
                    prog.rd(o, pg[1], bias[1]); prog.wr(o, obuf[dbl][1])
            so = prog.add("sd", lambda e, d=q_scr[0][:].rearrange("c p t -> p c t")[:, :, nb * BN:(nb + 1) * BN],
                          s2=qb_buf[dbl][0].rearrange("p (c t) -> p c t", c=DC): e.dma_start(d, s2),
                          tag=f"{pfx}stq{s}_{nb}", key=qb_buf[dbl][1].name)
            prog.rd(so, qb_buf[dbl][1]); prog.wr(so, q_scr[1])
            so = prog.add("sd", lambda e, d=k_scr[0][:].rearrange("c p t -> p c t")[:, :, nb * BN:(nb + 1) * BN],
                          s2=kb_buf[dbl][0].rearrange("p (c t) -> p c t", c=DC): e.dma_start(d, s2),
                          tag=f"{pfx}stk{s}_{nb}", key=kb_buf[dbl][1].name)
            prog.rd(so, kb_buf[dbl][1]); prog.wr(so, k_scr[1])

            for tt in range(4):
                for fh in range(2):
                    pg = pp[(tt * 2 + fh) % 2]
                    mmgroup(prog, pg,
                            [(XLb[dbl][kc][0][:, tt * 128:(tt + 1) * 128],
                              wv_t[0][:, kc * D + fh * 512: kc * D + (fh + 1) * 512]) for kc in range(DC)],
                            bufs_rd=[wv_t[1]] + [XLb[dbl][kc][1] for kc in range(DC)],
                            tag=f"{pfx}pv{s}_{nb}_{tt}_{fh}")
                    dst = vb_buf[dbl][0][:, tt * D + fh * 512: tt * D + (fh + 1) * 512]
                    o = prog.add("v", lambda e, d=dst, p2=pg, fh=fh:
                                 e.tensor_add(d, p2[0], bv_b[0][:, fh * 512:(fh + 1) * 512]),
                                 tag=f"{pfx}ev{s}_{nb}_{tt}_{fh}")
                    prog.rd(o, pg[1], bv_b[1]); prog.wr(o, vb_buf[dbl][1])
            so = prog.add("sd", lambda e, d=v_scr[0][:].rearrange("t p f -> p t f")[:, nb * 4:(nb + 1) * 4, :],
                          s2=vb_buf[dbl][0].rearrange("p (t f) -> p t f", t=4): e.dma_start(d, s2),
                          tag=f"{pfx}stv{s}_{nb}", key=vb_buf[dbl][1].name)
            prog.rd(so, vb_buf[dbl][1]); prog.wr(so, v_scr[1])

        if s == 0:
            scope_begin()
            q_h = [sb(nc, f"{pfx}_qh{i}", [64, TSEQ], BF16) for i in range(2)]
            k_h = [sb(nc, f"{pfx}_kh{i}", [64, TSEQ], BF16) for i in range(2)]
            v_hp = [sb(nc, f"{pfx}_vhp{i}", [128, NQT * 128], BF16) for i in range(2)]
            P_sb = [sb(nc, f"{pfx}_P{i}", [128, 2048], BF16) for i in range(4)]
            PT_sb = [sb(nc, f"{pfx}_PT{i}", [128, BN], BF16) for i in range(2)]
            a_hd = [sb(nc, f"{pfx}_ah{i}", [64, TSEQ], BF16) for i in range(2)]
            nmax = sb(nc, f"{pfx}_nmax", [128, 1], F32)
            cmax = sb(nc, f"{pfx}_cmax", [128, 4], F32)
            rsum = sb(nc, f"{pfx}_rsum", [128, 1], F32)
            rcp = [sb(nc, f"{pfx}_rcp{i}", [128, 1], F32) for i in range(4)]
            build_attn_b.attn_bufs = (q_h, k_h, v_hp, P_sb, PT_sb, a_hd, nmax, rsum, rcp, cmax)
        else:
            q_h, k_h, v_hp, P_sb, PT_sb, a_hd, nmax, rsum, rcp, cmax = build_attn_b.attn_bufs

        for h in range(NH):
            hb = h % 2
            qh, kh, ah = q_h[hb], k_h[hb], a_hd[hb]
            c0 = h // 2
            r0 = (h % 2) * 64
            lo = prog.add("cd", lambda e, d=qh[0], s2=q_scr[0][:][c0, r0:r0 + 64, :]: e.dma_start(d, s2),
                          tag=f"{pfx}ldqh{s}_{h}", key=qh[1].name)
            prog.rd(lo, q_scr[1]); prog.wr(lo, qh[1])
            lo = prog.add("cd", lambda e, d=kh[0], s2=k_scr[0][:][c0, r0:r0 + 64, :]: e.dma_start(d, s2),
                          tag=f"{pfx}ldkh{s}_{h}", key=kh[1].name)
            prog.rd(lo, k_scr[1]); prog.wr(lo, kh[1])
            if h % 2 == 0:
                vhp = v_hp[(h // 2) % 2]
                lo = prog.add("gd", lambda e, d=vhp[0].rearrange("p (t f) -> p t f", t=NQT),
                              s2=v_scr[0][:].rearrange("t p f -> p t f")[:, :, h * 64:(h + 2) * 64]: e.dma_start(d, s2),
                              tag=f"{pfx}ldvh{s}_{h}", key=vhp[1].name)
                prog.rd(lo, v_scr[1]); prog.wr(lo, vhp[1])
            else:
                vhp = v_hp[(h // 2) % 2]
            vcol = (h % 2) * 64

            for qb in range(NQB):
                for t4 in range(4):
                    i = qb * 4 + t4
                    valid = (i + 1) * 128
                    nch = (valid + 511) // 512
                    for kb in range(nch):
                        w = min(512, valid - kb * 512)
                        mmgroup(prog, (sc_ps[0][:, kb * 512: kb * 512 + w], sc_bk[kb]),
                                [(qh[0][:, i * 128:(i + 1) * 128], kh[0][:, kb * 512: kb * 512 + w])],
                                bufs_rd=[qh[1], kh[1]], tag=f"{pfx}sc{s}_{h}_{i}_{kb}")
                        if kb == nch - 1:
                            o = prog.add("v", lambda e, i=i, valid=valid: e.tensor_add(
                                sc_ps[0][:, i * 128:valid], sc_ps[0][:, i * 128:valid], causal[0]),
                                tag=f"{pfx}mask{s}_{h}_{i}")
                            prog.rd(o, causal[1], sc_bk[kb]); prog.wr(o, sc_bk[kb])
                        o = prog.add("v", lambda e, kb=kb, w=w: e.reduce_max(
                            cmax[0][:, kb:kb + 1], sc_ps[0][:, kb * 512: kb * 512 + w], axis=AX.X, negate=True),
                            tag=f"{pfx}cmax{s}_{h}_{i}_{kb}")
                        prog.rd(o, sc_bk[kb]); prog.wr(o, cmax[1])
                    o = prog.add("v", lambda e, nch=nch: e.tensor_reduce(
                        nmax[0], cmax[0][:, 0:nch], axis=AX.X, op=ALU.min),
                        tag=f"{pfx}max{s}_{h}_{i}")
                    prog.rd(o, cmax[1]); prog.wr(o, nmax[1])
                    o = prog.add("s", lambda e, t4=t4, valid=valid: e.activation(
                        P_sb[t4][0][:, 0:valid], sc_ps[0][:, 0:valid], AF.Exp,
                        bias=nmax[0], accum_out=rsum[0]),
                        tag=f"{pfx}exp{s}_{h}_{i}")
                    prog.rd(o, nmax[1], *sc_bk[0:nch]); prog.wr(o, P_sb[t4][1], rsum[1])
                    o = prog.add("v", lambda e, t4=t4: e.reciprocal(rcp[t4][0], rsum[0]),
                                 tag=f"{pfx}rcp{s}_{h}_{i}")
                    prog.rd(o, rsum[1]); prog.wr(o, rcp[t4][1])
                    o = prog.add("v", lambda e, t4=t4, valid=valid: e.tensor_scalar(
                        P_sb[t4][0][:, 0:valid], P_sb[t4][0][:, 0:valid], rcp[t4][0], None, ALU.mult),
                        tag=f"{pfx}pscale{s}_{h}_{i}")
                    prog.rd(o, rcp[t4][1], P_sb[t4][1]); prog.wr(o, P_sb[t4][1])

                nj = (qb + 1) * 4
                for j in range(nj):
                    ptb = PT_sb[j % 2]
                    t4min = max(0, j - qb * 4)
                    for t4 in range(t4min, 4):
                        to = prog.add("t", lambda e, j=j, t4=t4: e.transpose(
                            pt_ps[0][:, t4 * 128:(t4 + 1) * 128],
                            P_sb[t4][0][:, j * 128:(j + 1) * 128], identbf[0]),
                            tag=f"{pfx}tr{s}_{h}_{qb}_{j}_{t4}")
                        prog.rd(to, P_sb[t4][1], identbf[1])
                        prog.wr(to, pt_ps[1])
                    eo = prog.add("v" if j % 2 == 0 else "s",
                                  lambda e, t4min=t4min, ptb=ptb: (e.tensor_copy if hasattr(e, "tensor_copy") else e.copy)(
                                      ptb[0][:, t4min * 128:512], pt_ps[0][:, t4min * 128:512]),
                                  tag=f"{pfx}pte{s}_{h}_{qb}_{j}")
                    prog.rd(eo, pt_ps[1]); prog.wr(eo, ptb[1])
                    mmgroup(prog, (pv_ps[0][0:64, t4min * 128:512], pv_ps[1]),
                            [(vhp[0][:, j * 128 + vcol: j * 128 + vcol + 64], ptb[0][:, t4min * 128:512])],
                            bufs_rd=[vhp[1], ptb[1]], tag=f"{pfx}pv{s}_{h}_{qb}_{j}",
                            start=(j == 0), stop=(j == nj - 1))
                eo = prog.add("v", lambda e, qb=qb, ah=ah: e.tensor_copy(
                    ah[0][:, qb * 512:(qb + 1) * 512], pv_ps[0][0:64, :]),
                    tag=f"{pfx}pve{s}_{h}_{qb}")
                prog.rd(eo, pv_ps[1]); prog.wr(eo, ah[1])

            so = prog.add("sd", lambda e, d=a_scr[0][:][c0, r0:r0 + 64, s * TSEQ:(s + 1) * TSEQ], s2=ah[0]: e.dma_start(d, s2),
                          tag=f"{pfx}sta{s}_{h}", key=ah[1].name)
            prog.rd(so, ah[1]); prog.wr(so, a_scr[1])

    scope_end()
    emit_out_proj(nc, prog, pfx, a_scr, xin, xout, wd["wo"], bo_c, wsl, pp,
                  xbat, X8b)



def emit_elu1(prog, pfx, dst, pg, tmp1, tmp2, tag):
    o1 = prog.add("v", lambda e: e.tensor_scalar(tmp1[0], pg[0], 0.0, None, ALU.min), tag=f"{pfx}el1{tag}")
    prog.rd(o1, pg[1]); prog.wr(o1, tmp1[1])
    o2 = prog.add("s", lambda e: e.activation(tmp2[0], tmp1[0], AF.Exp), tag=f"{pfx}el2{tag}")
    prog.rd(o2, tmp1[1]); prog.wr(o2, tmp2[1])
    o3 = prog.add("v", lambda e: e.scalar_tensor_tensor(dst[0], pg[0], 0.0, tmp2[0], ALU.max, ALU.add), tag=f"{pfx}el3{tag}")
    prog.rd(o3, pg[1], tmp2[1]); prog.wr(o3, dst[1])
    return o3


def build_attn_lin(nc, prog, xin, xout, wd, cst, pfx="al", eps_attn=1e-6, stages="abcdeotu"):
    ones = cst["ones128"]
    identbf = cst["identbf"]

    q_scr = (dram(nc, f"{pfx}_qscr", [DC, 128, TSEQ], BF16), Buf(f"{pfx}_qscr"))
    a_scr = (dram(nc, f"{pfx}_ascr", [DC, 128, TC], BF16), Buf(f"{pfx}_ascr"))

    pp = [psum(nc, f"{pfx}_pp{i}", [128, BN]) for i in range(2)]
    kv_ps = psum(nc, f"{pfx}_kvps", [128, BN])
    ks_ps = [psum(nc, f"{pfx}_ksps{i}", [1, BN]) for i in range(2)]
    z_ps = psum(nc, f"{pfx}_zps", [128, 16])
    ot_ps = psum(nc, f"{pfx}_otps", [128, BN])
    at_ps = psum(nc, f"{pfx}_atps", [128, BN], BF16)

    lnsc = mk_ln_scratch(nc, pfx, psA=pp[0], psB=pp[1])
    xbat = [sb(nc, f"{pfx}_xb0", [128, DC * BN], F32)] * 2
    X8b = [[(xbat[i][0][:, c * BN:(c + 1) * BN], xbat[i][1]) for c in range(DC)]
           for i in range(2)]
    XLb = [[sb(nc, f"{pfx}_xl{c}_{i}", [128, BN], BF16) for c in range(DC)] for i in range(2)]
    wsl = [sb(nc, f"{pfx}_wsl{i}", [128, D], BF16) for i in range(4)]
    wk_t = sb(nc, f"{pfx}_wkt", [128, DC * D], BF16)
    wv_t = sb(nc, f"{pfx}_wvt", [128, DC * D], BF16)
    qb_buf = [sb(nc, f"{pfx}_qbb0", [128, DC * BN], BF16)] * 2
    ktile = [sb(nc, f"{pfx}_kt{i}", [128, D], BF16) for i in range(2)]
    vtile = [sb(nc, f"{pfx}_vt{i}", [128, D], BF16) for i in range(2)]
    tmp1 = sb(nc, f"{pfx}_tmp1", [128, BN], F32)
    tmp2 = sb(nc, f"{pfx}_tmp2", [128, BN], F32)
    kv_eo = [sb(nc, f"{pfx}_kvsb{i}", [64, BN], BF16) for i in range(2)]
    kv_acc = sb(nc, f"{pfx}_kvacc", [128, BN], F32)
    ksrow = sb(nc, f"{pfx}_ksrow", [1, D], BF16)
    KS_eo = [sb(nc, f"{pfx}_KS{i}", [64, 16], BF16) for i in range(2)]
    zc = sb(nc, f"{pfx}_zc", [128, 16], F32)
    otok = sb(nc, f"{pfx}_otok", [128, D], BF16)
    ones1b = sb(nc, f"{pfx}_ones1b", [1, 1], BF16)
    qld_eo = [sb(nc, f"{pfx}_qld{i}", [64, DC * BN], BF16) for i in range(2)]
    abat = [sb(nc, f"{pfx}_ab0", [128, DC * BN], BF16)] * 2

    bo_c = bias_cols(nc, prog, pfx, "bo", wd["bo"], pp)
    o = prog.add("v", lambda e: e.memset(ones1b[0], 1.0), tag=f"{pfx}one1b")
    prog.wr(o, ones1b[1])

    for nm, t in (("wk", wk_t), ("wv", wv_t)):
        lo = prog.add("gd", lambda e, t=t, nm=nm: e.dma_start(
            t[0], wd[nm][:].rearrange("(kc p) f -> p kc f", p=128)),
            tag=f"{pfx}ld{nm}", key=t[1].name)
        prog.wr(lo, t[1])

    for sq in range(2):
        if "b" not in stages:
            break
        for nb in range(NQB):
            dbl = nb % 2
            tok0 = sq * TSEQ + nb * BN
            src = xin[:].rearrange("c p t -> p c t")[:, :, tok0:tok0 + BN]
            op = prog.add("sd", lambda e, d=xbat[dbl][0].rearrange("p (c t) -> p c t", c=DC), s2=src: e.dma_start(d, s2),
                          tag=f"{pfx}ldx{sq}_{nb}", key=xbat[dbl][1].name)
            prog.wr(op, xbat[dbl][1])
            emit_ln(prog, X8b[dbl], XLb[dbl], lnsc, ones)

            for m in range(DC):
                slot = wsl[m % 4]
                load_wslice(nc, prog, slot, wd["wq"], None, m, f"{pfx}lwq{sq}{nb}{m}")
                pg = pp[m % 2]
                mmgroup(prog, pg,
                        [(slot[0][:, kc * 128:(kc + 1) * 128], XLb[dbl][kc][0]) for kc in range(DC)],
                        bufs_rd=[slot[1]] + [XLb[dbl][kc][1] for kc in range(DC)],
                        tag=f"{pfx}pq{sq}_{nb}_{m}")
                emit_elu1(prog, pfx, (qb_buf[dbl][0][:, m * BN:(m + 1) * BN], qb_buf[dbl][1]),
                          pg, tmp1, tmp2, f"q{sq}_{nb}_{m}")
            so = prog.add("sd", lambda e, d=q_scr[0][:].rearrange("c p t -> p c t")[:, :, nb * BN:(nb + 1) * BN],
                          s2=qb_buf[dbl][0].rearrange("p (c t) -> p c t", c=DC): e.dma_start(d, s2),
                          tag=f"{pfx}stq{sq}_{nb}", key=qb_buf[dbl][1].name)
            prog.rd(so, qb_buf[dbl][1]); prog.wr(so, q_scr[1])

            for tt in range(4):
                gt = nb * 4 + tt
                kt, vt = ktile[tt % 2], vtile[tt % 2]
                for fh in range(2):
                    pg = pp[fh % 2]
                    mmgroup(prog, pg,
                            [(XLb[dbl][kc][0][:, tt * 128:(tt + 1) * 128],
                              wk_t[0][:, kc * D + fh * 512: kc * D + (fh + 1) * 512]) for kc in range(DC)],
                            bufs_rd=[wk_t[1]] + [XLb[dbl][kc][1] for kc in range(DC)],
                            tag=f"{pfx}pk{sq}_{gt}_{fh}")
                    emit_elu1(prog, pfx, (kt[0][:, fh * 512:(fh + 1) * 512], kt[1]),
                              pg, tmp1, tmp2, f"k{sq}_{gt}_{fh}")
                for fh in range(2):
                    pg = pp[fh % 2]
                    mmgroup(prog, pg,
                            [(XLb[dbl][kc][0][:, tt * 128:(tt + 1) * 128],
                              wv_t[0][:, kc * D + fh * 512: kc * D + (fh + 1) * 512]) for kc in range(DC)],
                            bufs_rd=[wv_t[1]] + [XLb[dbl][kc][1] for kc in range(DC)],
                            tag=f"{pfx}pv{sq}_{gt}_{fh}")
                    o = prog.add("v", lambda e, d=vt[0][:, fh * 512:(fh + 1) * 512], p2=pg: e.tensor_copy(d, p2[0]),
                                 tag=f"{pfx}ev{sq}_{gt}_{fh}")
                    prog.rd(o, pg[1]); prog.wr(o, vt[1])
                for h in range(NH):
                    r0 = (h % 2) * 64
                    c0 = (h // 2) * 64
                    mmgroup(prog, (kv_ps[0][r0:r0 + 64, c0:c0 + 64], kv_ps[1]),
                            [(kt[0][:, h * 64:(h + 1) * 64], vt[0][:, h * 64:(h + 1) * 64])],
                            bufs_rd=[kt[1], vt[1]], tag=f"{pfx}kv{sq}_{gt}_{h}")
                if gt == 0:
                    o = prog.add("v", lambda e: e.tensor_copy(kv_acc[0], kv_ps[0]), tag=f"{pfx}kva{sq}_{gt}")
                else:
                    o = prog.add("v", lambda e: e.tensor_add(kv_acc[0], kv_acc[0], kv_ps[0]), tag=f"{pfx}kva{sq}_{gt}")
                prog.rd(o, kv_ps[1], kv_acc[1]); prog.wr(o, kv_acc[1])
                for fh in range(2):
                    mmgroup(prog, ks_ps[fh],
                            [(ones[0][:, 0:1], kt[0][:, fh * 512:(fh + 1) * 512])],
                            bufs_rd=[ones[1], kt[1]], tag=f"{pfx}ks{sq}_{gt}_{fh}",
                            start=(gt == 0), stop=(gt == NQT - 1))

        if "c" not in stages:
            continue
        for i in range(2):
            o = prog.add("v", lambda e, i=i: e.tensor_copy(kv_eo[i][0], kv_acc[0][i * 64:(i + 1) * 64, :]),
                         tag=f"{pfx}kve{sq}_{i}")
            prog.rd(o, kv_acc[1]); prog.wr(o, kv_eo[i][1])
        for fh in range(2):
            o = prog.add("v", lambda e, fh=fh: e.tensor_copy(ksrow[0][:, fh * 512:(fh + 1) * 512], ks_ps[fh][0]),
                         tag=f"{pfx}kse{sq}_{fh}")
            prog.rd(o, ks_ps[fh][1]); prog.wr(o, ksrow[1])
        for i in range(2):
            o = prog.add("v", lambda e, i=i: e.memset(KS_eo[i][0], 0.0), tag=f"{pfx}KS0{sq}_{i}")
            prog.wr(o, KS_eo[i][1])
        for h in range(NH):
            c = h // 2
            r0 = (h % 2) * 64
            pg = pp[h % 2]
            mmgroup(prog, (pg[0][0:64, 0:1], pg[1]),
                    [(ksrow[0][:, c * 128 + r0: c * 128 + r0 + 64], ones1b[0])],
                    bufs_rd=[ksrow[1], ones1b[1]], tag=f"{pfx}ksT{sq}{h}")
            o = prog.add("v", lambda e, h=h, pg=pg: e.tensor_copy(
                KS_eo[h % 2][0][:, h:h + 1], pg[0][0:64, 0:1]),
                tag=f"{pfx}ksTe{sq}{h}")
            prog.rd(o, pg[1]); prog.wr(o, KS_eo[h % 2][1])

        if "d" not in stages:
            continue
        for nb in range(NQB):
            dbl = nb % 2
            for i in range(2):
                src = q_scr[0][:].rearrange("c p t -> p c t")[i * 64:(i + 1) * 64, :, nb * BN:(nb + 1) * BN]
                op = prog.add("sd", lambda e, d=qld_eo[i][0].rearrange("p (c t) -> p c t", c=DC), s2=src: e.dma_start(d, s2),
                              tag=f"{pfx}ldq{sq}_{nb}_{i}", key=qld_eo[i][1].name)
                prog.rd(op, q_scr[1]); prog.wr(op, qld_eo[i][1])
            for tt in range(4):
                for c in range(DC):
                    mmgroup(prog, (z_ps[0][:, 2 * c:2 * c + 2], z_ps[1]),
                            [(qld_eo[i][0][:, c * BN + tt * 128: c * BN + (tt + 1) * 128],
                              KS_eo[i][0][:, 2 * c:2 * c + 2]) for i in range(2)],
                            bufs_rd=[qld_eo[0][1], qld_eo[1][1], KS_eo[0][1], KS_eo[1][1]],
                            tag=f"{pfx}z{sq}_{nb}_{tt}_{c}")
                o = prog.add("v", lambda e: e.tensor_scalar(zc[0], z_ps[0], eps_attn, None, ALU.add),
                             tag=f"{pfx}zeps{sq}_{nb}_{tt}")
                prog.rd(o, z_ps[1]); prog.wr(o, zc[1])
                o = prog.add("v", lambda e: e.reciprocal(zc[0], zc[0]), tag=f"{pfx}zrec{sq}_{nb}_{tt}")
                prog.rd(o, zc[1]); prog.wr(o, zc[1])
                for half in range(2 if "o" in stages else 0):
                    for h in range(half * 8, half * 8 + 8):
                        i = h % 2
                        c0 = h // 2
                        mmgroup(prog, (ot_ps[0][:, (h % 8) * 64:(h % 8 + 1) * 64], ot_ps[1]),
                                [(qld_eo[i][0][:, c0 * BN + tt * 128: c0 * BN + (tt + 1) * 128],
                                  kv_eo[i][0][:, c0 * 64:(c0 + 1) * 64])],
                                bufs_rd=[qld_eo[i][1], kv_eo[i][1]], tag=f"{pfx}o{sq}_{nb}_{tt}_{h}")
                    for h in range(half * 8, half * 8 + 8):
                        if "u" in stages:
                            o = prog.add("v", lambda e, h=h: e.tensor_scalar(
                                otok[0][:, h * 64:(h + 1) * 64], ot_ps[0][:, (h % 8) * 64:(h % 8 + 1) * 64],
                                zc[0][:, h:h + 1], None, ALU.mult),
                                tag=f"{pfx}od{sq}_{nb}_{tt}_{h}")
                        else:
                            o = prog.add("v", lambda e, h=h: e.tensor_copy(
                                otok[0][:, h * 64:(h + 1) * 64], ot_ps[0][:, (h % 8) * 64:(h % 8 + 1) * 64]),
                                tag=f"{pfx}od{sq}_{nb}_{tt}_{h}")
                        prog.rd(o, ot_ps[1], zc[1]); prog.wr(o, otok[1])
                for c in range(DC if "t" in stages else 0):
                    sl = at_ps[0][:, (c % 4) * 128:(c % 4 + 1) * 128]
                    to = prog.add("t", lambda e, c=c, sl=sl: e.transpose(
                        sl, otok[0][:, c * 128:(c + 1) * 128], identbf[0]),
                        tag=f"{pfx}tr{sq}_{nb}_{tt}_{c}")
                    prog.rd(to, otok[1], identbf[1]); prog.wr(to, at_ps[1])
                    eo = prog.add("v" if c % 2 == 0 else "s",
                                  lambda e, c=c, sl=sl, dbl=dbl, tt=tt:
                                  (e.tensor_copy if hasattr(e, "tensor_copy") else e.copy)(
                                      abat[dbl][0][:, c * BN + tt * 128: c * BN + (tt + 1) * 128], sl),
                                  tag=f"{pfx}tre{sq}_{nb}_{tt}_{c}")
                    prog.rd(eo, at_ps[1]); prog.wr(eo, abat[dbl][1])
            so = prog.add("sd", lambda e, d=a_scr[0][:].rearrange("c p t -> p c t")[:, :, sq * TSEQ + nb * BN: sq * TSEQ + (nb + 1) * BN],
                          s2=abat[dbl][0].rearrange("p (c t) -> p c t", c=DC): e.dma_start(d, s2),
                          tag=f"{pfx}sta{sq}_{nb}", key=abat[dbl][1].name)
            prog.rd(so, abat[dbl][1]); prog.wr(so, a_scr[1])

    if "e" in stages:
        emit_out_proj(nc, prog, pfx, a_scr, xin, xout, wd["wo"], bo_c, wsl, pp,
                      xbat, X8b)


def launch_attn_lin(x_shards, wq, wk, wv, wo, bo):
    consts = host_consts()
    nc = bass.Bass("TRN2", target_bir_lowering=False)
    xin = nc.dram_tensor("xin", [DC, 128, TC], F32, kind="ExternalInput")
    xout = nc.dram_tensor("xout", [DC, 128, TC], F32, kind="ExternalOutput")
    wd = {}
    for nm in ("wq", "wk", "wv", "wo"):
        wd[nm] = nc.dram_tensor(nm, [D, D], F32, kind="ExternalInput")
    wd["bo"] = nc.dram_tensor("bo", [D], F32, kind="ExternalInput")
    prog = Prog(nc, "al")
    cst = load_consts(nc, prog, consts, ["ones128", "onesf", "identbf"])
    build_attn_lin(nc, prog, xin, xout, wd, cst)
    prog.run()
    in_maps = []
    for c in range(NCORES):
        m = {"xin": to_fm(x_shards[c]), "wq": wq, "wk": wk, "wv": wv, "wo": wo, "bo": bo,
             "ones128": consts["ones128"], "onesf": consts["onesf"],
             "identbf": consts["identbf"]}
        in_maps.append(m)
    res = run_bass_kernel_spmd(nc, in_maps, core_ids=list(range(NCORES)))
    return [from_fm(r["xout"]) for r in res.results]



def build_attn_pa(nc, prog, xin, xout, wd, cst, pfx="pa", eps=1e-5, stages="abcd"):
    ones = cst["ones128"]
    identbf = cst["identbf"]
    SCL = float(64.0 ** -0.25)

    xd_scr = (dram(nc, f"{pfx}_xdscr", [DC, 128, TSEQ], BF16), Buf(f"{pfx}_xdscr"))
    a_scr = (dram(nc, f"{pfx}_ascr", [DC, 128, TC], BF16), Buf(f"{pfx}_ascr"))

    pp = [psum(nc, f"{pfx}_pp{i}", [128, BN]) for i in range(2)]
    sc_ps = psum(nc, f"{pfx}_scps", [128, 2048])
    sc_bk = [Buf(f"{pfx}_scb{i}") for i in range(4)]
    pt_ps = psum(nc, f"{pfx}_ptps", [128, BN], BF16)
    pv_ps = psum(nc, f"{pfx}_pvps", [128, BN])

    lnsc = mk_ln_scratch(nc, pfx, psA=pp[0], psB=pp[1])
    xbat = [sb(nc, f"{pfx}_xb0", [128, DC * BN], F32)] * 2
    X8b = [[(xbat[i][0][:, c * BN:(c + 1) * BN], xbat[i][1]) for c in range(DC)]
           for i in range(2)]
    XLb = [[sb(nc, f"{pfx}_xl{c}_{i}", [128, BN], BF16) for c in range(DC)] for i in range(2)]
    wsl = [sb(nc, f"{pfx}_wsl{i}", [128, D], BF16) for i in range(4)]
    xdb = [sb(nc, f"{pfx}_xdb0", [128, DC * BN], BF16)] * 2

    tok_raw = sb(nc, f"{pfx}_tokraw", [64, NH * 64], BF16)
    tokT = sb(nc, f"{pfx}_tokT", [64, NH * 64], BF16)
    tqT = sb(nc, f"{pfx}_tqT", [64, NH * 64], BF16)
    tvT = sb(nc, f"{pfx}_tvT", [64, NH * 64], BF16)

    bo_c = bias_cols(nc, prog, pfx, "bu", wd["b_unify"], pp)

    for nm, dstT in (("tokeys", tokT), ("toqueries", tqT), ("tovalues", tvT)):
        lo = prog.add("gd", lambda e, nm=nm: e.dma_start(
            tok_raw[0].rearrange("p (h k) -> p h k", h=NH),
            wd[nm][:].rearrange("h i k -> i h k")),
            tag=f"{pfx}ld{nm}", key=tok_raw[1].name)
        prog.wr(lo, tok_raw[1])
        for h in range(NH):
            sl = pt_ps[0][0:64, (h % 8) * 64:(h % 8 + 1) * 64]
            to = prog.add("t", lambda e, h=h, sl=sl: e.transpose(
                sl, tok_raw[0][:, h * 64:(h + 1) * 64], identbf[0][0:64, 0:64]),
                tag=f"{pfx}tT{nm}{h}")
            prog.rd(to, tok_raw[1], identbf[1]); prog.wr(to, pt_ps[1])
            eo = prog.add("v", lambda e, h=h, sl=sl, dstT=dstT: e.tensor_copy(
                dstT[0][:, h * 64:(h + 1) * 64], sl), tag=f"{pfx}tTe{nm}{h}")
            prog.rd(eo, pt_ps[1]); prog.wr(eo, dstT[1])

    for sq in range(2):
        if "b" not in stages:
            break
        for nb in range(NQB):
            dbl = nb % 2
            tok0 = sq * TSEQ + nb * BN
            src = xin[:].rearrange("c p t -> p c t")[:, :, tok0:tok0 + BN]
            op = prog.add("sd", lambda e, d=xbat[dbl][0].rearrange("p (c t) -> p c t", c=DC), s2=src: e.dma_start(d, s2),
                          tag=f"{pfx}ldx{sq}_{nb}", key=xbat[dbl][1].name)
            prog.wr(op, xbat[dbl][1])
            emit_ln(prog, X8b[dbl], XLb[dbl], lnsc, ones)
            for m in range(DC):
                slot = wsl[m % 4]
                load_wslice(nc, prog, slot, wd["W_down"], None, m, f"{pfx}lwd{sq}{nb}{m}")
                pg = pp[m % 2]
                mmgroup(prog, pg,
                        [(slot[0][:, kc * 128:(kc + 1) * 128], XLb[dbl][kc][0]) for kc in range(DC)],
                        bufs_rd=[slot[1]] + [XLb[dbl][kc][1] for kc in range(DC)],
                        tag=f"{pfx}pd{sq}_{nb}_{m}")
                o = prog.add("v", lambda e, d=xdb[dbl][0][:, m * BN:(m + 1) * BN], p2=pg: e.tensor_copy(d, p2[0]),
                             tag=f"{pfx}ed{sq}_{nb}_{m}")
                prog.rd(o, pg[1]); prog.wr(o, xdb[dbl][1])
            so = prog.add("sd", lambda e, d=xd_scr[0][:].rearrange("c p t -> p c t")[:, :, nb * BN:(nb + 1) * BN],
                          s2=xdb[dbl][0].rearrange("p (c t) -> p c t", c=DC): e.dma_start(d, s2),
                          tag=f"{pfx}stxd{sq}_{nb}", key=xdb[dbl][1].name)
            prog.rd(so, xdb[dbl][1]); prog.wr(so, xd_scr[1])

        if "c" not in stages:
            continue
        if sq == 0:
            scope_begin()
            xd_h = [sb(nc, f"{pfx}_xdh{i}", [64, TSEQ], BF16) for i in range(2)]
            q_h = [sb(nc, f"{pfx}_qh{i}", [64, TSEQ], BF16) for i in range(2)]
            k_h = [sb(nc, f"{pfx}_kh{i}", [64, TSEQ], BF16) for i in range(2)]
            v_h = [sb(nc, f"{pfx}_vh{i}", [128, NQT * 64], BF16) for i in range(2)]
            P_sb = [sb(nc, f"{pfx}_P{i}", [128, 2048], BF16) for i in range(4)]
            PT_sb = [sb(nc, f"{pfx}_PT{i}", [128, BN], BF16) for i in range(2)]
            a_hd = [sb(nc, f"{pfx}_ah{i}", [64, TSEQ], BF16) for i in range(2)]
            nmax = sb(nc, f"{pfx}_nmax", [128, 1], F32)
            cmax = sb(nc, f"{pfx}_cmax", [128, 4], F32)
            rsum = sb(nc, f"{pfx}_rsum", [128, 1], F32)
            rcp = [sb(nc, f"{pfx}_rcp{i}", [128, 1], F32) for i in range(4)]
            build_attn_pa.bufs = (xd_h, q_h, k_h, v_h, P_sb, PT_sb, a_hd, nmax, rsum, rcp, cmax)
        else:
            xd_h, q_h, k_h, v_h, P_sb, PT_sb, a_hd, nmax, rsum, rcp, cmax = build_attn_pa.bufs

        for h in range(NH):
            hb = h % 2
            xh, qh, kh, vh, ah = xd_h[hb], q_h[hb], k_h[hb], v_h[hb], a_hd[hb]
            c0 = h // 2
            r0 = (h % 2) * 64
            lo = prog.add("cd", lambda e, d=xh[0], s2=xd_scr[0][:][c0, r0:r0 + 64, :]: e.dma_start(d, s2),
                          tag=f"{pfx}ldxh{sq}_{h}", key=xh[1].name)
            prog.rd(lo, xd_scr[1]); prog.wr(lo, xh[1])
            for kb in range(4):
                pg = pp[kb % 2]
                mmgroup(prog, (pg[0][0:64, :], pg[1]),
                        [(tokT[0][:, h * 64:(h + 1) * 64], xh[0][:, kb * 512:(kb + 1) * 512])],
                        bufs_rd=[tokT[1], xh[1]], tag=f"{pfx}kk{sq}_{h}_{kb}")
                o = prog.add("v", lambda e, d=kh[0][:, kb * 512:(kb + 1) * 512], p2=pg: e.tensor_scalar(
                    d, p2[0][0:64, :], SCL, None, ALU.mult), tag=f"{pfx}ekk{sq}_{h}_{kb}")
                prog.rd(o, pg[1]); prog.wr(o, kh[1])
                pg = pp[(kb + 1) % 2]
                mmgroup(prog, (pg[0][0:64, :], pg[1]),
                        [(tqT[0][:, h * 64:(h + 1) * 64], xh[0][:, kb * 512:(kb + 1) * 512])],
                        bufs_rd=[tqT[1], xh[1]], tag=f"{pfx}qq{sq}_{h}_{kb}")
                o = prog.add("v", lambda e, d=qh[0][:, kb * 512:(kb + 1) * 512], p2=pg: e.tensor_scalar(
                    d, p2[0][0:64, :], SCL, None, ALU.mult), tag=f"{pfx}eqq{sq}_{h}_{kb}")
                prog.rd(o, pg[1]); prog.wr(o, qh[1])
            for half in range(2):
                pg = pp[half % 2]
                for tt in range(8):
                    gt = half * 8 + tt
                    mmgroup(prog, (pg[0][:, tt * 64:(tt + 1) * 64], pg[1]),
                            [(xh[0][:, gt * 128:(gt + 1) * 128], tvT[0][:, h * 64:(h + 1) * 64])],
                            bufs_rd=[xh[1], tvT[1]], tag=f"{pfx}vv{sq}_{h}_{gt}")
                o = prog.add("v", lambda e, d=vh[0][:, half * 512:(half + 1) * 512], p2=pg: e.tensor_copy(d, p2[0]),
                             tag=f"{pfx}evv{sq}_{h}_{half}")
                prog.rd(o, pg[1]); prog.wr(o, vh[1])

            for qb in range(NQB):
                for t4 in range(4):
                    i = qb * 4 + t4
                    for kb in range(4):
                        mmgroup(prog, (sc_ps[0][:, kb * 512:(kb + 1) * 512], sc_bk[kb]),
                                [(qh[0][:, i * 128:(i + 1) * 128], kh[0][:, kb * 512:(kb + 1) * 512])],
                                bufs_rd=[qh[1], kh[1]], tag=f"{pfx}sc{sq}_{h}_{i}_{kb}")
                        o = prog.add("v", lambda e, kb=kb: e.reduce_max(
                            cmax[0][:, kb:kb + 1], sc_ps[0][:, kb * 512:(kb + 1) * 512], axis=AX.X, negate=True),
                            tag=f"{pfx}cmax{sq}_{h}_{i}_{kb}")
                        prog.rd(o, sc_bk[kb]); prog.wr(o, cmax[1])
                    o = prog.add("v", lambda e: e.tensor_reduce(
                        nmax[0], cmax[0][:, 0:4], axis=AX.X, op=ALU.min),
                        tag=f"{pfx}max{sq}_{h}_{i}")
                    prog.rd(o, cmax[1]); prog.wr(o, nmax[1])
                    o = prog.add("s", lambda e, t4=t4: e.activation(
                        P_sb[t4][0][:], sc_ps[0][:], AF.Exp,
                        bias=nmax[0], accum_out=rsum[0]),
                        tag=f"{pfx}exp{sq}_{h}_{i}")
                    prog.rd(o, nmax[1], *sc_bk); prog.wr(o, P_sb[t4][1], rsum[1])
                    o = prog.add("v", lambda e, t4=t4: e.reciprocal(rcp[t4][0], rsum[0]),
                                 tag=f"{pfx}rcp{sq}_{h}_{i}")
                    prog.rd(o, rsum[1]); prog.wr(o, rcp[t4][1])
                    o = prog.add("v", lambda e, t4=t4: e.tensor_scalar(
                        P_sb[t4][0][:], P_sb[t4][0][:], rcp[t4][0], None, ALU.mult),
                        tag=f"{pfx}pscale{sq}_{h}_{i}")
                    prog.rd(o, rcp[t4][1], P_sb[t4][1]); prog.wr(o, P_sb[t4][1])

                for j in range(NQT):
                    ptb = PT_sb[j % 2]
                    for t4 in range(4):
                        to = prog.add("t", lambda e, j=j, t4=t4: e.transpose(
                            pt_ps[0][:, t4 * 128:(t4 + 1) * 128],
                            P_sb[t4][0][:, j * 128:(j + 1) * 128], identbf[0]),
                            tag=f"{pfx}tr{sq}_{h}_{qb}_{j}_{t4}")
                        prog.rd(to, P_sb[t4][1], identbf[1])
                        prog.wr(to, pt_ps[1])
                    eo = prog.add("v" if j % 2 == 0 else "s",
                                  lambda e, ptb=ptb: (e.tensor_copy if hasattr(e, "tensor_copy") else e.copy)(
                                      ptb[0][:], pt_ps[0][:]),
                                  tag=f"{pfx}pte{sq}_{h}_{qb}_{j}")
                    prog.rd(eo, pt_ps[1]); prog.wr(eo, ptb[1])
                    mmgroup(prog, (pv_ps[0][0:64, :], pv_ps[1]),
                            [(vh[0][:, j * 64:(j + 1) * 64], ptb[0][:])],
                            bufs_rd=[vh[1], ptb[1]], tag=f"{pfx}pv{sq}_{h}_{qb}_{j}",
                            start=(j == 0), stop=(j == NQT - 1))
                eo = prog.add("v", lambda e, qb=qb, ah=ah: e.tensor_copy(
                    ah[0][:, qb * 512:(qb + 1) * 512], pv_ps[0][0:64, :]),
                    tag=f"{pfx}pve{sq}_{h}_{qb}")
                prog.rd(eo, pv_ps[1]); prog.wr(eo, ah[1])

            so = prog.add("sd", lambda e, d=a_scr[0][:][c0, r0:r0 + 64, sq * TSEQ:(sq + 1) * TSEQ], s2=ah[0]: e.dma_start(d, s2),
                          tag=f"{pfx}sta{sq}_{h}", key=ah[1].name)
            prog.rd(so, ah[1]); prog.wr(so, a_scr[1])

    if "c" in stages:
        scope_end()
    if "d" in stages:
        emit_out_proj(nc, prog, pfx, a_scr, xin, xout, wd["W_unify"], bo_c, wsl, pp,
                      xbat, X8b)


def launch_attn_pa(x_shards, W_down, tokeys, toqueries, tovalues, W_unify, b_unify):
    consts = host_consts()
    nc = bass.Bass("TRN2", target_bir_lowering=False)
    xin = nc.dram_tensor("xin", [DC, 128, TC], F32, kind="ExternalInput")
    xout = nc.dram_tensor("xout", [DC, 128, TC], F32, kind="ExternalOutput")
    wd = {"W_down": nc.dram_tensor("W_down", [D, D], F32, kind="ExternalInput"),
          "W_unify": nc.dram_tensor("W_unify", [D, D], F32, kind="ExternalInput"),
          "b_unify": nc.dram_tensor("b_unify", [D], F32, kind="ExternalInput"),
          "tokeys": nc.dram_tensor("tokeys", [NH, 64, 64], F32, kind="ExternalInput"),
          "toqueries": nc.dram_tensor("toqueries", [NH, 64, 64], F32, kind="ExternalInput"),
          "tovalues": nc.dram_tensor("tovalues", [NH, 64, 64], F32, kind="ExternalInput")}
    prog = Prog(nc, "pa")
    cst = load_consts(nc, prog, consts, ["ones128", "identbf"])
    build_attn_pa(nc, prog, xin, xout, wd, cst)
    prog.run()
    in_maps = []
    for c in range(NCORES):
        m = {"xin": to_fm(x_shards[c]), "W_down": W_down, "W_unify": W_unify,
             "b_unify": b_unify, "tokeys": tokeys, "toqueries": toqueries,
             "tovalues": tovalues, "ones128": consts["ones128"],
             "identbf": consts["identbf"]}
        in_maps.append(m)
    res = run_bass_kernel_spmd(nc, in_maps, core_ids=list(range(NCORES)))
    return [from_fm(r["xout"]) for r in res.results]



KVR = 256
QR = 512
RD = 32


def build_attn_mla(nc, prog, xin, xout, wd, cst, pfx="ml", eps=1e-5):
    eps_rms = 1e-6
    ones = cst["ones128"]
    identbf = cst["identbf"]
    causal = cst["causal128"]
    rcos = cst["ropecos"]
    rsin = cst["ropesin"]
    rRT = cst["ropeRT"]
    SCLQ = float(1.0 / np.sqrt(96.0))

    qc_scr = (dram(nc, f"{pfx}_qcscr", [4, 128, TSEQ], BF16), Buf(f"{pfx}_qcscr"))
    qd_scr = (dram(nc, f"{pfx}_qdscr", [DC, 128, TSEQ], BF16), Buf(f"{pfx}_qdscr"))
    qr_scr = (dram(nc, f"{pfx}_qrscr", [4, 128, TSEQ], BF16), Buf(f"{pfx}_qrscr"))
    kc_scr = (dram(nc, f"{pfx}_kcscr", [DC, 128, TSEQ], BF16), Buf(f"{pfx}_kcscr"))
    kr_scr = (dram(nc, f"{pfx}_krscr", [4, 128, TSEQ], BF16), Buf(f"{pfx}_krscr"))
    v_scr = (dram(nc, f"{pfx}_vscr", [NQT, 128, D], BF16), Buf(f"{pfx}_vscr"))
    a_scr = (dram(nc, f"{pfx}_ascr", [DC, 128, TC], BF16), Buf(f"{pfx}_ascr"))

    pp = [psum(nc, f"{pfx}_pp{i}", [128, BN]) for i in range(2)]
    sc_ps = psum(nc, f"{pfx}_scps", [128, 2048])
    sc_bk = [Buf(f"{pfx}_scb{i}") for i in range(4)]
    pt_ps = psum(nc, f"{pfx}_ptps", [128, BN], BF16)
    pv_ps = psum(nc, f"{pfx}_pvps", [128, BN])

    lnsc = mk_ln_scratch(nc, pfx, psA=pp[0], psB=pp[1])
    xbat = [sb(nc, f"{pfx}_xb0", [128, DC * BN], F32)] * 2
    X8b = [[(xbat[i][0][:, c * BN:(c + 1) * BN], xbat[i][1]) for c in range(DC)]
           for i in range(2)]
    XLb = [[sb(nc, f"{pfx}_xl{c}_{i}", [128, BN], BF16) for c in range(DC)] for i in range(2)]
    wsl = [sb(nc, f"{pfx}_wsl{i}", [128, D], BF16) for i in range(4)]
    wvd_t = sb(nc, f"{pfx}_wvdt", [128, 2 * D], BF16)
    kvc_raw = sb(nc, f"{pfx}_kvcr", [128, 2 * BN], F32)
    kvc_sq = sb(nc, f"{pfx}_kvcsq", [128, 2 * BN], BF16)
    rmsbc = sb(nc, f"{pfx}_rmsbc", [128, BN], F32)
    kvn = [sb(nc, f"{pfx}_kvn{c2}", [128, BN], BF16) for c2 in range(2)]
    qcb = sb(nc, f"{pfx}_qcb", [128, 4 * BN], BF16)
    outb = sb(nc, f"{pfx}_outb", [128, DC * BN], BF16)
    vb = sb(nc, f"{pfx}_vb", [128, D], BF16)
    rt1 = sb(nc, f"{pfx}_rt1", [128, BN], F32)
    rt2 = sb(nc, f"{pfx}_rt2", [128, BN], F32)
    rraw = sb(nc, f"{pfx}_rraw", [128, BN], BF16)

    gkv_c = bias_cols(nc, prog, pfx, "gkv", wd["g_kv"], pp, dim=KVR)

    lo = prog.add("gd", lambda e: e.dma_start(
        wvd_t[0], wd["W_vdec"][:].rearrange("(kc p) f -> p kc f", p=128)),
        tag=f"{pfx}ldwvd", key=wvd_t[1].name)
    prog.wr(lo, wvd_t[1])

    def stream_w(w_key, K_, m, slot, tag):
        src = wd[w_key][:].rearrange("(kc p) f -> p kc f", p=128)[:, :, m * 128:(m + 1) * 128]
        lo = prog.add("gd", lambda e, d=slot[0][:, 0:K_]: e.dma_start(
            d.rearrange("p (kc f) -> p kc f", f=128), src), tag=tag, key=slot[1].name)
        prog.wr(lo, slot[1])
        return lo

    def emit_rope(dbl, m, pg, dst, dstbuf, tok0, scale, tag):
        o = prog.add("v", lambda e: e.tensor_copy(rraw[0], pg[0]), tag=f"{pfx}rr{tag}")
        prog.rd(o, pg[1]); prog.wr(o, rraw[1])
        mmgroup(prog, pg, [(rRT[0], rraw[0])], bufs_rd=[rRT[1], rraw[1]],
                tag=f"{pfx}rot{tag}")
        o = prog.add("v", lambda e: e.scalar_tensor_tensor(
            rt1[0], rraw[0], scale, rcos[0][:, tok0:tok0 + BN], ALU.mult, ALU.mult),
            tag=f"{pfx}rc{tag}")
        prog.rd(o, rraw[1], rcos[1]); prog.wr(o, rt1[1])
        o = prog.add("v", lambda e: e.scalar_tensor_tensor(
            rt2[0], pg[0], scale, rsin[0][:, tok0:tok0 + BN], ALU.mult, ALU.mult),
            tag=f"{pfx}rs{tag}")
        prog.rd(o, pg[1], rsin[1]); prog.wr(o, rt2[1])
        o = prog.add("v", lambda e, dst=dst: e.tensor_add(dst, rt1[0], rt2[0]), tag=f"{pfx}ra{tag}")
        prog.rd(o, rt1[1], rt2[1]); prog.wr(o, dstbuf)

    for sq in range(2):
        for nb in range(NQB):
            dbl = nb % 2
            tok0s = sq * TSEQ + nb * BN
            tok0 = nb * BN
            src = xin[:].rearrange("c p t -> p c t")[:, :, tok0s:tok0s + BN]
            op = prog.add("sd", lambda e, d=xbat[dbl][0].rearrange("p (c t) -> p c t", c=DC), s2=src: e.dma_start(d, s2),
                          tag=f"{pfx}ldx{sq}_{nb}", key=xbat[dbl][1].name)
            prog.wr(op, xbat[dbl][1])
            emit_ln(prog, X8b[dbl], XLb[dbl], lnsc, ones)

            for m in range(2):
                slot = wsl[m % 4]
                stream_w("W_kv", D, m, slot, f"{pfx}lkv{sq}{nb}{m}")
                pg = pp[m % 2]
                mmgroup(prog, pg,
                        [(slot[0][:, kc * 128:(kc + 1) * 128], XLb[dbl][kc][0]) for kc in range(DC)],
                        bufs_rd=[slot[1]] + [XLb[dbl][kc][1] for kc in range(DC)],
                        tag=f"{pfx}kvc{sq}_{nb}_{m}")
                o = prog.add("v", lambda e, m=m, pg=pg: e.tensor_copy(kvc_raw[0][:, m * BN:(m + 1) * BN], pg[0]),
                             tag=f"{pfx}ekvc{sq}_{nb}_{m}")
                prog.rd(o, pg[1]); prog.wr(o, kvc_raw[1])
                o = prog.add("s", lambda e, m=m: e.activation(kvc_sq[0][:, m * BN:(m + 1) * BN],
                                                              kvc_raw[0][:, m * BN:(m + 1) * BN], AF.Square),
                             tag=f"{pfx}sqkv{sq}_{nb}_{m}")
                prog.rd(o, kvc_raw[1]); prog.wr(o, kvc_sq[1])
            mmgroup(prog, pp[0], [(ones[0], kvc_sq[0][:, m2 * BN:(m2 + 1) * BN]) for m2 in range(2)],
                    bufs_rd=[ones[1], kvc_sq[1]], tag=f"{pfx}ssq{sq}_{nb}")
            o = prog.add("v", lambda e: e.tensor_scalar(rmsbc[0], pp[0][0], 1.0 / KVR, eps_rms, ALU.mult, ALU.add),
                         tag=f"{pfx}rms1{sq}_{nb}")
            prog.rd(o, pp[0][1]); prog.wr(o, rmsbc[1])
            o = prog.add("v", lambda e: e.reciprocal(rmsbc[0], rmsbc[0]), tag=f"{pfx}rms2{sq}_{nb}")
            prog.rd(o, rmsbc[1]); prog.wr(o, rmsbc[1])
            o = prog.add("s", lambda e: e.activation(rmsbc[0], rmsbc[0], AF.Sqrt), tag=f"{pfx}rms3{sq}_{nb}")
            prog.rd(o, rmsbc[1]); prog.wr(o, rmsbc[1])
            for m in range(2):
                o = prog.add("v", lambda e, m=m: e.tensor_mul(rt1[0], kvc_raw[0][:, m * BN:(m + 1) * BN], rmsbc[0]),
                             tag=f"{pfx}kvn1{sq}_{nb}_{m}")
                prog.rd(o, kvc_raw[1], rmsbc[1]); prog.wr(o, rt1[1])
                o = prog.add("v", lambda e, m=m: e.tensor_scalar(kvn[m][0], rt1[0], gkv_c[0][:, m:m + 1], None, ALU.mult),
                             tag=f"{pfx}kvn2{sq}_{nb}_{m}")
                prog.rd(o, rt1[1], gkv_c[1]); prog.wr(o, kvn[m][1])

            for m in range(4):
                slot = wsl[m % 4]
                stream_w("W_q", D, m, slot, f"{pfx}lwq{sq}{nb}{m}")
                pg = pp[m % 2]
                mmgroup(prog, pg,
                        [(slot[0][:, kc * 128:(kc + 1) * 128], XLb[dbl][kc][0]) for kc in range(DC)],
                        bufs_rd=[slot[1]] + [XLb[dbl][kc][1] for kc in range(DC)],
                        tag=f"{pfx}qc{sq}_{nb}_{m}")
                o = prog.add("v", lambda e, m=m, pg=pg: e.tensor_copy(qcb[0][:, m * BN:(m + 1) * BN], pg[0]),
                             tag=f"{pfx}eqc{sq}_{nb}_{m}")
                prog.rd(o, pg[1]); prog.wr(o, qcb[1])
            so = prog.add("sd", lambda e, d=qc_scr[0][:].rearrange("c p t -> p c t")[:, :, tok0:tok0 + BN],
                          s2=qcb[0].rearrange("p (c t) -> p c t", c=4): e.dma_start(d, s2),
                          tag=f"{pfx}stqc{sq}_{nb}", key=qcb[1].name)
            prog.rd(so, qcb[1]); prog.wr(so, qc_scr[1])

            for m in range(DC):
                slot = wsl[m % 4]
                stream_w("W_kdec", KVR, m, slot, f"{pfx}lkd{sq}{nb}{m}")
                pg = pp[m % 2]
                mmgroup(prog, pg,
                        [(slot[0][:, c2 * 128:(c2 + 1) * 128], kvn[c2][0]) for c2 in range(2)],
                        bufs_rd=[slot[1], kvn[0][1], kvn[1][1]],
                        tag=f"{pfx}kd{sq}_{nb}_{m}")
                o = prog.add("v", lambda e, m=m, pg=pg: e.tensor_copy(outb[0][:, m * BN:(m + 1) * BN], pg[0]),
                             tag=f"{pfx}ekd{sq}_{nb}_{m}")
                prog.rd(o, pg[1]); prog.wr(o, outb[1])
            so = prog.add("sd", lambda e, d=kc_scr[0][:].rearrange("c p t -> p c t")[:, :, tok0:tok0 + BN],
                          s2=outb[0].rearrange("p (c t) -> p c t", c=DC): e.dma_start(d, s2),
                          tag=f"{pfx}stkc{sq}_{nb}", key=outb[1].name)
            prog.rd(so, outb[1]); prog.wr(so, kc_scr[1])

            for tt in range(4):
                for fh in range(2):
                    pg = pp[fh % 2]
                    mmgroup(prog, pg,
                            [(kvn[c2][0][:, tt * 128:(tt + 1) * 128],
                              wvd_t[0][:, c2 * D + fh * 512: c2 * D + (fh + 1) * 512]) for c2 in range(2)],
                            bufs_rd=[wvd_t[1], kvn[0][1], kvn[1][1]],
                            tag=f"{pfx}pv{sq}_{nb}_{tt}_{fh}")
                    o = prog.add("v", lambda e, d=vb[0][:, fh * 512:(fh + 1) * 512], p2=pg: e.tensor_copy(d, p2[0]),
                                 tag=f"{pfx}ev{sq}_{nb}_{tt}_{fh}")
                    prog.rd(o, pg[1]); prog.wr(o, vb[1])
                so = prog.add("sd", lambda e, d=v_scr[0][:][nb * 4 + tt], s2=vb[0]: e.dma_start(d, s2),
                              tag=f"{pfx}stv{sq}_{nb}_{tt}", key=vb[1].name)
                prog.rd(so, vb[1]); prog.wr(so, v_scr[1])

            for m in range(4):
                slot = wsl[m % 4]
                stream_w("W_krope", D, m, slot, f"{pfx}lkr{sq}{nb}{m}")
                pg = pp[m % 2]
                mmgroup(prog, pg,
                        [(slot[0][:, kc * 128:(kc + 1) * 128], XLb[dbl][kc][0]) for kc in range(DC)],
                        bufs_rd=[slot[1]] + [XLb[dbl][kc][1] for kc in range(DC)],
                        tag=f"{pfx}kr{sq}_{nb}_{m}")
                emit_rope(dbl, m, pg, outb[0][:, m * BN:(m + 1) * BN], outb[1], tok0, 1.0, f"kr{sq}_{nb}_{m}")
            so = prog.add("sd", lambda e, d=kr_scr[0][:].rearrange("c p t -> p c t")[:, :, tok0:tok0 + BN],
                          s2=outb[0][:, 0:4 * BN].rearrange("p (c t) -> p c t", c=4): e.dma_start(d, s2),
                          tag=f"{pfx}stkr{sq}_{nb}", key=outb[1].name)
            prog.rd(so, outb[1]); prog.wr(so, kr_scr[1])

            for m in range(4):
                slot = wsl[m % 4]
                stream_w("W_qrope", QR, m, slot, f"{pfx}lqr{sq}{nb}{m}")
                pg = pp[m % 2]
                mmgroup(prog, pg,
                        [(slot[0][:, c4 * 128:(c4 + 1) * 128], qcb[0][:, c4 * BN:(c4 + 1) * BN]) for c4 in range(4)],
                        bufs_rd=[slot[1], qcb[1]],
                        tag=f"{pfx}qr{sq}_{nb}_{m}")
                emit_rope(dbl, m, pg, outb[0][:, m * BN:(m + 1) * BN], outb[1], tok0, SCLQ, f"qr{sq}_{nb}_{m}")
            so = prog.add("sd", lambda e, d=qr_scr[0][:].rearrange("c p t -> p c t")[:, :, tok0:tok0 + BN],
                          s2=outb[0][:, 0:4 * BN].rearrange("p (c t) -> p c t", c=4): e.dma_start(d, s2),
                          tag=f"{pfx}stqr{sq}_{nb}", key=outb[1].name)
            prog.rd(so, outb[1]); prog.wr(so, qr_scr[1])

            for m in range(DC):
                slot = wsl[m % 4]
                stream_w("W_qdec", QR, m, slot, f"{pfx}lqd{sq}{nb}{m}")
                pg = pp[m % 2]
                mmgroup(prog, pg,
                        [(slot[0][:, c4 * 128:(c4 + 1) * 128], qcb[0][:, c4 * BN:(c4 + 1) * BN]) for c4 in range(4)],
                        bufs_rd=[slot[1], qcb[1]],
                        tag=f"{pfx}qd{sq}_{nb}_{m}")
                o = prog.add("v", lambda e, m=m, pg=pg: e.tensor_scalar(
                    outb[0][:, m * BN:(m + 1) * BN], pg[0], SCLQ, None, ALU.mult),
                    tag=f"{pfx}eqd{sq}_{nb}_{m}")
                prog.rd(o, pg[1]); prog.wr(o, outb[1])
            so = prog.add("sd", lambda e, d=qd_scr[0][:].rearrange("c p t -> p c t")[:, :, tok0:tok0 + BN],
                          s2=outb[0].rearrange("p (c t) -> p c t", c=DC): e.dma_start(d, s2),
                          tag=f"{pfx}stqd{sq}_{nb}", key=outb[1].name)
            prog.rd(so, outb[1]); prog.wr(so, qd_scr[1])

        if sq == 0:
            scope_begin()
            qc_h = [sb(nc, f"{pfx}_qch{i}", [64, TSEQ], BF16) for i in range(2)]
            qr_h = [sb(nc, f"{pfx}_qrh{i}", [32, TSEQ], BF16) for i in range(2)]
            kc_h = [sb(nc, f"{pfx}_kch{i}", [64, TSEQ], BF16) for i in range(2)]
            kr_h = [sb(nc, f"{pfx}_krh{i}", [32, TSEQ], BF16) for i in range(2)]
            v_hp = [sb(nc, f"{pfx}_vhp{i}", [128, NQT * 128], BF16) for i in range(2)]
            P_sb = [sb(nc, f"{pfx}_P{i}", [128, 2048], BF16) for i in range(4)]
            PT_sb = [sb(nc, f"{pfx}_PT{i}", [128, BN], BF16) for i in range(2)]
            a_hd = [sb(nc, f"{pfx}_ah{i}", [64, TSEQ], BF16) for i in range(2)]
            nmax = sb(nc, f"{pfx}_nmax", [128, 1], F32)
            cmax = sb(nc, f"{pfx}_cmax", [128, 4], F32)
            rsum = sb(nc, f"{pfx}_rsum", [128, 1], F32)
            rcp = [sb(nc, f"{pfx}_rcp{i}", [128, 1], F32) for i in range(4)]
            build_attn_mla.bufs = (qc_h, qr_h, kc_h, kr_h, v_hp, P_sb, PT_sb, a_hd, nmax, rsum, rcp, cmax)
        else:
            qc_h, qr_h, kc_h, kr_h, v_hp, P_sb, PT_sb, a_hd, nmax, rsum, rcp, cmax = build_attn_mla.bufs

        for h in range(NH):
            hb = h % 2
            qch, qrh, kch, krh, ah = qc_h[hb], qr_h[hb], kc_h[hb], kr_h[hb], a_hd[hb]
            c0 = h // 2
            r0 = (h % 2) * 64
            c4 = h // 4
            r4 = (h % 4) * 32
            for t, scr, rr, nrow in ((qch, qd_scr, r0, 64), (kch, kc_scr, r0, 64)):
                lo = prog.add("cd", lambda e, d=t[0], s2=scr[0][:][c0, rr:rr + nrow, :]: e.dma_start(d, s2),
                              tag=f"{pfx}ldh{sq}_{h}_{t[1].name}", key=t[1].name)
                prog.rd(lo, scr[1]); prog.wr(lo, t[1])
            for t, scr in ((qrh, qr_scr), (krh, kr_scr)):
                lo = prog.add("cd", lambda e, d=t[0], s2=scr[0][:][c4, r4:r4 + 32, :]: e.dma_start(d, s2),
                              tag=f"{pfx}ldh{sq}_{h}_{t[1].name}", key=t[1].name)
                prog.rd(lo, scr[1]); prog.wr(lo, t[1])
            if h % 2 == 0:
                vhp = v_hp[(h // 2) % 2]
                lo = prog.add("gd", lambda e, d=vhp[0].rearrange("p (t f) -> p t f", t=NQT),
                              s2=v_scr[0][:].rearrange("t p f -> p t f")[:, :, h * 64:(h + 2) * 64]: e.dma_start(d, s2),
                              tag=f"{pfx}ldvh{sq}_{h}", key=vhp[1].name)
                prog.rd(lo, v_scr[1]); prog.wr(lo, vhp[1])
            else:
                vhp = v_hp[(h // 2) % 2]
            vcol = (h % 2) * 64

            for qb in range(NQB):
                for t4 in range(4):
                    i = qb * 4 + t4
                    valid = (i + 1) * 128
                    nch = (valid + 511) // 512
                    for kb in range(nch):
                        w = min(512, valid - kb * 512)
                        mmgroup(prog, (sc_ps[0][:, kb * 512: kb * 512 + w], sc_bk[kb]),
                                [(qch[0][:, i * 128:(i + 1) * 128], kch[0][:, kb * 512: kb * 512 + w]),
                                 (qrh[0][:, i * 128:(i + 1) * 128], krh[0][:, kb * 512: kb * 512 + w])],
                                bufs_rd=[qch[1], kch[1], qrh[1], krh[1]],
                                tag=f"{pfx}sc{sq}_{h}_{i}_{kb}")
                        if kb == nch - 1:
                            o = prog.add("v", lambda e, i=i, valid=valid: e.tensor_add(
                                sc_ps[0][:, i * 128:valid], sc_ps[0][:, i * 128:valid], causal[0]),
                                tag=f"{pfx}mask{sq}_{h}_{i}")
                            prog.rd(o, causal[1], sc_bk[kb]); prog.wr(o, sc_bk[kb])
                        o = prog.add("v", lambda e, kb=kb, w=w: e.reduce_max(
                            cmax[0][:, kb:kb + 1], sc_ps[0][:, kb * 512: kb * 512 + w], axis=AX.X, negate=True),
                            tag=f"{pfx}cmax{sq}_{h}_{i}_{kb}")
                        prog.rd(o, sc_bk[kb]); prog.wr(o, cmax[1])
                    o = prog.add("v", lambda e, nch=nch: e.tensor_reduce(
                        nmax[0], cmax[0][:, 0:nch], axis=AX.X, op=ALU.min),
                        tag=f"{pfx}max{sq}_{h}_{i}")
                    prog.rd(o, cmax[1]); prog.wr(o, nmax[1])
                    o = prog.add("s", lambda e, t4=t4, valid=valid: e.activation(
                        P_sb[t4][0][:, 0:valid], sc_ps[0][:, 0:valid], AF.Exp,
                        bias=nmax[0], accum_out=rsum[0]),
                        tag=f"{pfx}exp{sq}_{h}_{i}")
                    prog.rd(o, nmax[1], *sc_bk[0:nch]); prog.wr(o, P_sb[t4][1], rsum[1])
                    o = prog.add("v", lambda e, t4=t4: e.reciprocal(rcp[t4][0], rsum[0]),
                                 tag=f"{pfx}rcp{sq}_{h}_{i}")
                    prog.rd(o, rsum[1]); prog.wr(o, rcp[t4][1])
                    o = prog.add("v", lambda e, t4=t4, valid=valid: e.tensor_scalar(
                        P_sb[t4][0][:, 0:valid], P_sb[t4][0][:, 0:valid], rcp[t4][0], None, ALU.mult),
                        tag=f"{pfx}pscale{sq}_{h}_{i}")
                    prog.rd(o, rcp[t4][1], P_sb[t4][1]); prog.wr(o, P_sb[t4][1])

                nj = (qb + 1) * 4
                for j in range(nj):
                    ptb = PT_sb[j % 2]
                    t4min = max(0, j - qb * 4)
                    for t4 in range(t4min, 4):
                        to = prog.add("t", lambda e, j=j, t4=t4: e.transpose(
                            pt_ps[0][:, t4 * 128:(t4 + 1) * 128],
                            P_sb[t4][0][:, j * 128:(j + 1) * 128], identbf[0]),
                            tag=f"{pfx}tr{sq}_{h}_{qb}_{j}_{t4}")
                        prog.rd(to, P_sb[t4][1], identbf[1])
                        prog.wr(to, pt_ps[1])
                    eo = prog.add("v" if j % 2 == 0 else "s",
                                  lambda e, t4min=t4min, ptb=ptb: (e.tensor_copy if hasattr(e, "tensor_copy") else e.copy)(
                                      ptb[0][:, t4min * 128:512], pt_ps[0][:, t4min * 128:512]),
                                  tag=f"{pfx}pte{sq}_{h}_{qb}_{j}")
                    prog.rd(eo, pt_ps[1]); prog.wr(eo, ptb[1])
                    mmgroup(prog, (pv_ps[0][0:64, t4min * 128:512], pv_ps[1]),
                            [(vhp[0][:, j * 128 + vcol: j * 128 + vcol + 64], ptb[0][:, t4min * 128:512])],
                            bufs_rd=[vhp[1], ptb[1]], tag=f"{pfx}pv{sq}_{h}_{qb}_{j}",
                            start=(j == 0), stop=(j == nj - 1))
                eo = prog.add("v", lambda e, qb=qb, ah=ah: e.tensor_copy(
                    ah[0][:, qb * 512:(qb + 1) * 512], pv_ps[0][0:64, :]),
                    tag=f"{pfx}pve{sq}_{h}_{qb}")
                prog.rd(eo, pv_ps[1]); prog.wr(eo, ah[1])

            so = prog.add("sd", lambda e, d=a_scr[0][:][c0, r0:r0 + 64, sq * TSEQ:(sq + 1) * TSEQ], s2=ah[0]: e.dma_start(d, s2),
                          tag=f"{pfx}sta{sq}_{h}", key=ah[1].name)
            prog.rd(so, ah[1]); prog.wr(so, a_scr[1])

    scope_end()
    emit_out_proj(nc, prog, pfx, a_scr, xin, xout, wd["W_o"], None, wsl, pp,
                  xbat, X8b)


def launch_attn_mla(x_shards, wdn):
    consts = host_consts()
    nc = bass.Bass("TRN2", target_bir_lowering=False)
    xin = nc.dram_tensor("xin", [DC, 128, TC], F32, kind="ExternalInput")
    xout = nc.dram_tensor("xout", [DC, 128, TC], F32, kind="ExternalOutput")
    shp = {"W_kv": [D, KVR], "g_kv": [KVR], "W_kdec": [KVR, D], "W_vdec": [KVR, D],
           "W_q": [D, QR], "W_qdec": [QR, D], "W_krope": [D, QR], "W_qrope": [QR, QR],
           "W_o": [D, D]}
    wd = {k: nc.dram_tensor(k, v, F32, kind="ExternalInput") for k, v in shp.items()}
    prog = Prog(nc, "ml")
    names = ["ones128", "identbf", "causal128", "ropecos", "ropesin", "ropeRT"]
    cst = load_consts(nc, prog, consts, names)
    build_attn_mla(nc, prog, xin, xout, wd, cst)
    prog.run()
    in_maps = []
    for c in range(NCORES):
        m = {"xin": to_fm(x_shards[c])}
        m.update(wdn)
        for k in names:
            m[k] = consts[k]
        in_maps.append(m)
    res = run_bass_kernel_spmd(nc, in_maps, core_ids=list(range(NCORES)))
    return [from_fm(r["xout"]) for r in res.results]



NEXP = 8
CAP = 9831
NTILES = 256


def build_moe_logits(nc, prog, xin, lg_out, wg_d, bg_d, cst, pfx="mg"):
    ones = cst["ones128"]
    onesf = cst["onesf"]
    pp = [psum(nc, f"{pfx}_pp{i}", [128, BN]) for i in range(2)]
    lg_ps = psum(nc, f"{pfx}_lgps", [8, BN])
    cs_ps = psum(nc, f"{pfx}_csps", [8, 1])
    lnsc = mk_ln_scratch(nc, pfx, psA=pp[0], psB=pp[1])
    xbat = [sb(nc, f"{pfx}_xb0", [128, DC * BN], F32)] * 2
    X8b = [[(xbat[i][0][:, c * BN:(c + 1) * BN], xbat[i][1]) for c in range(DC)]
           for i in range(2)]
    wgs = sb(nc, f"{pfx}_wgs", [128, DC * NEXP], F32)
    onescol = sb(nc, f"{pfx}_onescol", [128, 1], F32)
    ncs = sb(nc, f"{pfx}_ncs", [8, 1], F32)
    bgrow = sb(nc, f"{pfx}_bgrow", [1, NEXP], F32)
    bgcol = sb(nc, f"{pfx}_bgcol", [8, 1], F32)
    bg_ps = psum(nc, f"{pfx}_bgps", [8, 1])
    lg_sb = sb(nc, f"{pfx}_lgsb", [8, TC], F32)
    t2 = sb(nc, f"{pfx}_t2", [8, BN], F32)

    lo = prog.add("sd", lambda e: e.dma_start(
        wgs[0].rearrange("p (kc e) -> p kc e", e=NEXP),
        wg_d[:].rearrange("(kc p) e -> p kc e", p=128)), tag=f"{pfx}ldwg", key=wgs[1].name)
    prog.wr(lo, wgs[1])
    lo = prog.add("sd", lambda e: e.dma_start(bgrow[0], bg_d[:].rearrange("(o e) -> o e", o=1)),
                  tag=f"{pfx}ldbg", key=bgrow[1].name)
    prog.wr(lo, bgrow[1])
    o = prog.add("v", lambda e: e.memset(onescol[0], 1.0), tag=f"{pfx}onec")
    prog.wr(o, onescol[1])
    mmgroup(prog, cs_ps, [(wgs[0][:, kc * NEXP:(kc + 1) * NEXP], onescol[0]) for kc in range(DC)],
            bufs_rd=[wgs[1], onescol[1]], tag=f"{pfx}cs")
    o = prog.add("v", lambda e: e.tensor_scalar(ncs[0], cs_ps[0], -1.0, None, ALU.mult), tag=f"{pfx}ncs")
    prog.rd(o, cs_ps[1]); prog.wr(o, ncs[1])
    mmgroup(prog, bg_ps, [(bgrow[0], onescol[0][0:1, :])], bufs_rd=[bgrow[1], onescol[1]], tag=f"{pfx}bgT")
    o = prog.add("v", lambda e: e.tensor_copy(bgcol[0], bg_ps[0]), tag=f"{pfx}bgc")
    prog.rd(o, bg_ps[1]); prog.wr(o, bgcol[1])

    for b in range(NB):
        dbl = b % 2
        src = xin[:].rearrange("c p t -> p c t")[:, :, b * BN:(b + 1) * BN]
        op = prog.add("sd", lambda e, d=xbat[dbl][0].rearrange("p (c t) -> p c t", c=DC), s2=src: e.dma_start(d, s2),
                      tag=f"{pfx}ldx{b}", key=xbat[dbl][1].name)
        prog.wr(op, xbat[dbl][1])
        emit_ln(prog, X8b[dbl], None, lnsc, ones, norm=False)
        mmgroup(prog, lg_ps, [(wgs[0][:, kc * NEXP:(kc + 1) * NEXP], X8b[dbl][kc][0]) for kc in range(DC)],
                bufs_rd=[wgs[1], xbat[dbl][1]], tag=f"{pfx}raw{b}")
        o = prog.add("v", lambda e: e.tensor_mul(t2[0], lg_ps[0], lnsc["rstd"][0][0:8, :]), tag=f"{pfx}l1{b}")
        prog.rd(o, lg_ps[1], lnsc["rstd"][1]); prog.wr(o, t2[1])
        o = prog.add("v", lambda e: e.scalar_tensor_tensor(t2[0], lnsc["mr"][0][0:8, :], ncs[0], t2[0], ALU.mult, ALU.add),
                     tag=f"{pfx}l2{b}")
        prog.rd(o, lnsc["mr"][1], ncs[1], t2[1]); prog.wr(o, t2[1])
        o = prog.add("v", lambda e, b=b: e.tensor_scalar(lg_sb[0][:, b * BN:(b + 1) * BN], t2[0], bgcol[0], None, ALU.add),
                     tag=f"{pfx}l3{b}")
        prog.rd(o, t2[1], bgcol[1]); prog.wr(o, lg_sb[1])
    so = prog.add("sd", lambda e: e.dma_start(lg_out[:], lg_sb[0]), tag=f"{pfx}stlg", key=lg_sb[1].name)
    prog.rd(so, lg_sb[1])


def build_moe_gate(nc, prog, lgt_d, w_out, cst, pfx="gt"):
    ones = cst["ones128"]
    onesf = cst["onesf"]
    iota = cst["iota8t"]
    upst = cst["uppstrict"]
    BIG = 1e9

    pp = [psum(nc, f"{pfx}_pp{i}", [128, BN]) for i in range(2)]
    ct_ps = psum(nc, f"{pfx}_ctps", [1, BN])
    lg = [sb(nc, f"{pfx}_lg{i}", [128, BN], F32) for i in range(2)]
    OHa = [sb(nc, f"{pfx}_oh{k}", [128, 4 * BN], BF16) for k in range(2)]
    gv = [sb(nc, f"{pfx}_gv{k}", [128, NTILES], F32) for k in range(2)]
    m1 = sb(nc, f"{pfx}_m1", [128, 64], F32)
    m2 = sb(nc, f"{pfx}_m2", [128, 64], F32)
    idx = sb(nc, f"{pfx}_idx", [128, 64], F32)
    tt_ = sb(nc, f"{pfx}_tt", [128, BN], F32)
    msk = sb(nc, f"{pfx}_msk", [128, BN], F32)
    Pin = sb(nc, f"{pfx}_Pin", [128, 8 * BN], F32)
    crow = [sb(nc, f"{pfx}_crow{i}", [1, 8 * BN], F32) for i in range(2)]
    orow = sb(nc, f"{pfx}_orow", [1, 8 * BN], F32)
    pos = sb(nc, f"{pfx}_pos", [128, BN], F32)
    pat = sb(nc, f"{pfx}_pat", [128, 64], F32)
    keep = sb(nc, f"{pfx}_keep", [128, 64], F32)
    ck = sb(nc, f"{pfx}_ck", [128, 64], F32)
    w_sb = sb(nc, f"{pfx}_wsb", [128, 4 * BN], F32)

    def r3(ap, e=8):
        return ap.rearrange("p (g e) -> p g e", e=e)

    def bc(a, b):
        return bass.broadcast_tensor_aps(a, b)

    for bi in range(4):
        dbl = bi % 2
        lgb = lg[dbl]
        op = prog.add("sd", lambda e, d=lgb[0].rearrange("p (g e) -> p g e", e=8),
                      s2=lgt_d[:][:, bi * 64:(bi + 1) * 64, :]: e.dma_start(d, s2),
                      tag=f"{pfx}ldlg{bi}", key=lgb[1].name)
        prog.wr(op, lgb[1])
        o = prog.add("v", lambda e, lgb=lgb: e.tensor_reduce(m1[0].rearrange("p (g e) -> p g e", e=1), r3(lgb[0]), axis=AX.X, op=ALU.max), tag=f"{pfx}m1{bi}")
        prog.rd(o, lgb[1]); prog.wr(o, m1[1])
        x1, x2 = bc(r3(lgb[0]), r3(m1[0], 1))
        o = prog.add("v", lambda e, x1=x1, x2=x2: e.tensor_tensor(r3(tt_[0]), x1, x2, ALU.is_equal), tag=f"{pfx}eq{bi}")
        prog.rd(o, lgb[1], m1[1]); prog.wr(o, tt_[1])
        o = prog.add("v", lambda e: e.tensor_scalar(tt_[0], tt_[0], -BIG, BIG, ALU.mult, ALU.add), tag=f"{pfx}tb{bi}")
        prog.rd(o, tt_[1]); prog.wr(o, tt_[1])
        o = prog.add("v", lambda e: e.tensor_add(tt_[0], tt_[0], iota[0]), tag=f"{pfx}ti{bi}")
        prog.rd(o, tt_[1], iota[1]); prog.wr(o, tt_[1])
        o = prog.add("v", lambda e: e.tensor_reduce(idx[0].rearrange("p (g e) -> p g e", e=1), r3(tt_[0]), axis=AX.X, op=ALU.min), tag=f"{pfx}i1{bi}")
        prog.rd(o, tt_[1]); prog.wr(o, idx[1])
        x1, x2 = bc(r3(iota[0]), r3(idx[0], 1))
        o = prog.add("v", lambda e, bi=bi, x1=x1, x2=x2: e.tensor_tensor(
            r3(OHa[0][0][:, bi * BN:(bi + 1) * BN]), x1, x2, ALU.is_equal), tag=f"{pfx}oh1{bi}")
        prog.rd(o, iota[1]); prog.wr(o, OHa[0][1])
        o = prog.add("v", lambda e, bi=bi, lgb=lgb: e.scalar_tensor_tensor(
            msk[0], OHa[0][0][:, bi * BN:(bi + 1) * BN], -BIG, lgb[0], ALU.mult, ALU.add), tag=f"{pfx}mk{bi}")
        prog.rd(o, OHa[0][1], lgb[1]); prog.wr(o, msk[1])
        o = prog.add("v", lambda e: e.tensor_reduce(m2[0].rearrange("p (g e) -> p g e", e=1), r3(msk[0]), axis=AX.X, op=ALU.max), tag=f"{pfx}m2{bi}")
        prog.rd(o, msk[1]); prog.wr(o, m2[1])
        x1, x2 = bc(r3(msk[0]), r3(m2[0], 1))
        o = prog.add("v", lambda e, x1=x1, x2=x2: e.tensor_tensor(r3(tt_[0]), x1, x2, ALU.is_equal), tag=f"{pfx}eq2{bi}")
        prog.rd(o, msk[1], m2[1]); prog.wr(o, tt_[1])
        o = prog.add("v", lambda e: e.tensor_scalar(tt_[0], tt_[0], -BIG, BIG, ALU.mult, ALU.add), tag=f"{pfx}tb2{bi}")
        prog.rd(o, tt_[1]); prog.wr(o, tt_[1])
        o = prog.add("v", lambda e: e.tensor_add(tt_[0], tt_[0], iota[0]), tag=f"{pfx}ti2{bi}")
        prog.rd(o, tt_[1], iota[1]); prog.wr(o, tt_[1])
        o = prog.add("v", lambda e: e.tensor_reduce(idx[0].rearrange("p (g e) -> p g e", e=1), r3(tt_[0]), axis=AX.X, op=ALU.min), tag=f"{pfx}i2{bi}")
        prog.rd(o, tt_[1]); prog.wr(o, idx[1])
        x1, x2 = bc(r3(iota[0]), r3(idx[0], 1))
        o = prog.add("v", lambda e, bi=bi, x1=x1, x2=x2: e.tensor_tensor(
            r3(OHa[1][0][:, bi * BN:(bi + 1) * BN]), x1, x2, ALU.is_equal), tag=f"{pfx}oh2{bi}")
        prog.rd(o, iota[1]); prog.wr(o, OHa[1][1])
        o = prog.add("v", lambda e: e.tensor_sub(m1[0], m1[0], m2[0]), tag=f"{pfx}d12{bi}")
        prog.rd(o, m1[1], m2[1]); prog.wr(o, m1[1])
        o = prog.add("s", lambda e, bi=bi: e.activation(gv[0][0][:, bi * 64:(bi + 1) * 64], m1[0], AF.Sigmoid), tag=f"{pfx}sg{bi}")
        prog.rd(o, m1[1]); prog.wr(o, gv[0][1])
        o = prog.add("v", lambda e, bi=bi: e.tensor_scalar(
            gv[1][0][:, bi * 64:(bi + 1) * 64], gv[0][0][:, bi * 64:(bi + 1) * 64], -1.0, 1.0, ALU.mult, ALU.add), tag=f"{pfx}g2{bi}")
        prog.rd(o, gv[0][1]); prog.wr(o, gv[1][1])

    for k in range(2):
        for bi in range(4):
            sl = slice((k * 4 + bi) * BN, (k * 4 + bi + 1) * BN)
            mmgroup(prog, pp[bi % 2], [(upst[0], OHa[k][0][:, bi * BN:(bi + 1) * BN])],
                    bufs_rd=[upst[1], OHa[k][1]], tag=f"{pfx}pa{k}{bi}")
            o = prog.add("v", lambda e, sl=sl, pg=pp[bi % 2]: e.tensor_copy(Pin[0][:, sl], pg[0]), tag=f"{pfx}pae{k}{bi}")
            prog.rd(o, pp[bi % 2][1]); prog.wr(o, Pin[1])
            mmgroup(prog, ct_ps, [(ones[0][:, 0:1], OHa[k][0][:, bi * BN:(bi + 1) * BN])],
                    bufs_rd=[ones[1], OHa[k][1]], tag=f"{pfx}ct{k}{bi}")
            o = prog.add("v", lambda e, sl=sl: e.tensor_copy(crow[0][0][:, sl], ct_ps[0]), tag=f"{pfx}cte{k}{bi}")
            prog.rd(o, ct_ps[1]); prog.wr(o, crow[0][1])

    cur = 0
    sh = 8
    while sh < 8 * BN:
        nxt = 1 - cur
        o = prog.add("v", lambda e, cur=cur, nxt=nxt, sh=sh: e.tensor_copy(
            crow[nxt][0][:, 0:sh], crow[cur][0][:, 0:sh]), tag=f"{pfx}sch{sh}")
        prog.rd(o, crow[cur][1]); prog.wr(o, crow[nxt][1])
        o = prog.add("v", lambda e, cur=cur, nxt=nxt, sh=sh: e.tensor_add(
            crow[nxt][0][:, sh:], crow[cur][0][:, sh:], crow[cur][0][:, 0:8 * BN - sh]), tag=f"{pfx}sca{sh}")
        prog.rd(o, crow[cur][1]); prog.wr(o, crow[nxt][1])
        cur = nxt
        sh *= 2
    o = prog.add("v", lambda e: e.memset(orow[0][:, 0:8], 0.0), tag=f"{pfx}oz")
    prog.wr(o, orow[1])
    o = prog.add("v", lambda e, cur=cur: e.tensor_copy(orow[0][:, 8:], crow[cur][0][:, 0:8 * BN - 8]), tag=f"{pfx}osh")
    prog.rd(o, crow[cur][1]); prog.wr(o, orow[1])

    for k in range(2):
        for bi in range(4):
            sl = slice((k * 4 + bi) * BN, (k * 4 + bi + 1) * BN)
            mmgroup(prog, pp[0], [(onesf[0][0:1, :], orow[0][:, sl])],
                    bufs_rd=[onesf[1], orow[1]], tag=f"{pfx}ob{k}{bi}")
            o = prog.add("v", lambda e, sl=sl: e.tensor_add(pos[0], pp[0][0], Pin[0][:, sl]), tag=f"{pfx}pp{k}{bi}")
            prog.rd(o, pp[0][1], Pin[1]); prog.wr(o, pos[1])
            o = prog.add("v", lambda e, k=k, bi=bi: e.tensor_mul(tt_[0], pos[0], OHa[k][0][:, bi * BN:(bi + 1) * BN]), tag=f"{pfx}pm{k}{bi}")
            prog.rd(o, pos[1], OHa[k][1]); prog.wr(o, tt_[1])
            o = prog.add("v", lambda e: e.tensor_reduce(pat[0].rearrange("p (g e) -> p g e", e=1), r3(tt_[0]), axis=AX.X, op=ALU.add), tag=f"{pfx}pr{k}{bi}")
            prog.rd(o, tt_[1]); prog.wr(o, pat[1])
            o = prog.add("v", lambda e: e.tensor_scalar(keep[0], pat[0], CAP - 0.5, None, ALU.is_lt), tag=f"{pfx}kp{k}{bi}")
            prog.rd(o, pat[1]); prog.wr(o, keep[1])
            o = prog.add("v", lambda e, k=k, bi=bi: e.tensor_mul(ck[0], keep[0], gv[k][0][:, bi * 64:(bi + 1) * 64]), tag=f"{pfx}ckm{k}{bi}")
            prog.rd(o, keep[1], gv[k][1]); prog.wr(o, ck[1])
            x1, x2 = bc(r3(OHa[k][0][:, bi * BN:(bi + 1) * BN]), r3(ck[0], 1))
            if k == 0:
                o = prog.add("v", lambda e, bi=bi, x1=x1, x2=x2: e.tensor_tensor(
                    r3(w_sb[0][:, bi * BN:(bi + 1) * BN]), x1, x2, ALU.mult), tag=f"{pfx}w1{k}{bi}")
                prog.rd(o, OHa[k][1], ck[1]); prog.wr(o, w_sb[1])
            else:
                o = prog.add("v", lambda e, x1=x1, x2=x2: e.tensor_tensor(r3(tt_[0]), x1, x2, ALU.mult), tag=f"{pfx}w2a{k}{bi}")
                prog.rd(o, OHa[k][1], ck[1]); prog.wr(o, tt_[1])
                o = prog.add("v", lambda e, bi=bi: e.tensor_add(
                    w_sb[0][:, bi * BN:(bi + 1) * BN], w_sb[0][:, bi * BN:(bi + 1) * BN], tt_[0]), tag=f"{pfx}w2b{k}{bi}")
                prog.rd(o, tt_[1]); prog.wr(o, w_sb[1])

    so = prog.add("sd", lambda e: e.dma_start(w_out[:], w_sb[0].rearrange("p (g e) -> p g e", e=8)),
                  tag=f"{pfx}stw", key=w_sb[1].name)
    prog.rd(so, w_sb[1])


def build_moe_mlp(nc, prog, xin, xout, w_own_d, wT_d, w1_d, b1_d, w2_d, b2_d,
                  cst, pfx="mm", eps=1e-5):
    ones = cst["ones128"]
    onesf = cst["onesf"]
    MB = 1024

    w1bf = (dram(nc, f"{pfx}_w1bf", [NEXP, D, 4096], BF16), Buf(f"{pfx}_w1bf"))
    w2bf = (dram(nc, f"{pfx}_w2bf", [NEXP, 4096, D], BF16), Buf(f"{pfx}_w2bf"))
    emit_cast_to_dram(prog, w1_d, w1bf, f"{pfx}cw1")
    emit_cast_to_dram(prog, w2_d, w2bf, f"{pfx}cw2")

    pp = [psum(nc, f"{pfx}_pp{i}", [128, BN]) for i in range(4)]
    lnsc = mk_ln_scratch(nc, pfx, psA=pp[0], psB=pp[1])
    xbat = [sb(nc, f"{pfx}_xb0", [128, DC * MB], F32)] * 2
    XL = [sb(nc, f"{pfx}_xl{c}", [128, MB], BF16) for c in range(DC)]
    hh = [sb(nc, f"{pfx}_hh{m}", [128, MB], BF16) for m in range(16)]
    acc = [sb(nc, f"{pfx}_acc{c}", [128, MB], F32) for c in range(DC)]
    w1sl = [sb(nc, f"{pfx}_w1sl{i}", [128, D], BF16) for i in range(2)]
    w2sl = [sb(nc, f"{pfx}_w2sl0", [128, 4096], BF16)] * 2
    b1c = sb(nc, f"{pfx}_b1c", [128, NEXP * 32], F32)
    b1row = sb(nc, f"{pfx}_b1row", [1, 1024], F32)
    b2sb = sb(nc, f"{pfx}_b2sb", [8, D], BF16)
    wT8 = sb(nc, f"{pfx}_wT8", [8, MB], BF16)
    wrow = sb(nc, f"{pfx}_wrow", [1, MB], F32)
    wbc = sb(nc, f"{pfx}_wbc", [128, MB], F32)
    ones1 = sb(nc, f"{pfx}_ones1", [1, 1], F32)
    tmp = sb(nc, f"{pfx}_tmp", [128, BN], BF16)

    o = prog.add("v", lambda e: e.memset(ones1[0], 1.0), tag=f"{pfx}one1")
    prog.wr(o, ones1[1])
    lo = prog.add("gd", lambda e: e.dma_start(b2sb[0], b2_d[:]), tag=f"{pfx}ldb2", key=b2sb[1].name)
    prog.wr(lo, b2sb[1])

    for ee in range(NEXP):
        for q4 in range(4):
            lo = prog.add("sd", lambda e, ee=ee, q4=q4: e.dma_start(
                b1row[0], b1_d[:][ee, q4 * 1024:(q4 + 1) * 1024].rearrange("(o f) -> o f", o=1)),
                tag=f"{pfx}ldb1{ee}_{q4}", key=b1row[1].name)
            prog.wr(lo, b1row[1])
            for m8 in range(8):
                m = q4 * 8 + m8
                pg = pp[2 + m % 2]
                mmgroup(prog, (pg[0][:, 0:1], pg[1]), [(b1row[0][:, m8 * 128:(m8 + 1) * 128], ones1[0])],
                        bufs_rd=[b1row[1], ones1[1]], tag=f"{pfx}b1T{ee}_{m}")
                o = prog.add("v", lambda e, ee=ee, m=m, pg=pg: e.tensor_copy(b1c[0][:, ee * 32 + m: ee * 32 + m + 1], pg[0][:, 0:1]),
                             tag=f"{pfx}b1Te{ee}_{m}")
                prog.rd(o, pg[1]); prog.wr(o, b1c[1])

    for B4 in range(TC // MB):
        dbl = B4 % 2
        src = xin[:].rearrange("c p t -> p c t")[:, :, B4 * MB:(B4 + 1) * MB]
        op = prog.add("sd", lambda e, d=xbat[dbl][0].rearrange("p (c t) -> p c t", c=DC), s2=src: e.dma_start(d, s2),
                      tag=f"{pfx}ldx{B4}", key=xbat[dbl][1].name)
        prog.wr(op, xbat[dbl][1])
        lo = prog.add("gd", lambda e, B4=B4: e.dma_start(wT8[0], wT_d[:][:, B4 * MB:(B4 + 1) * MB]),
                      tag=f"{pfx}ldwT{B4}", key=wT8[1].name)
        prog.wr(lo, wT8[1])
        for hf in range(2):
            X8h = [(xbat[dbl][0][:, c * MB + hf * BN:(c * MB) + (hf + 1) * BN], xbat[dbl][1]) for c in range(DC)]
            XLh = [(XL[c][0][:, hf * BN:(hf + 1) * BN], XL[c][1]) for c in range(DC)]
            emit_ln(prog, X8h, XLh, lnsc, ones, eps=eps)
        for ee in range(NEXP):
            lo = prog.add("sd", lambda e, ee=ee, B4=B4: e.dma_start(
                wrow[0], wT_d[:][ee, B4 * MB:(B4 + 1) * MB].rearrange("(o t) -> o t", o=1)),
                tag=f"{pfx}ldwr{ee}_{B4}", key=wrow[1].name)
            prog.wr(lo, wrow[1])
            for hf in range(2):
                pg = pp[2 + hf % 2]
                mmgroup(prog, pg, [(onesf[0][0:1, :], wrow[0][:, hf * BN:(hf + 1) * BN])],
                        bufs_rd=[onesf[1], wrow[1]], tag=f"{pfx}wbc{ee}_{B4}_{hf}")
                o = prog.add("v", lambda e, hf=hf, pg=pg: e.tensor_copy(wbc[0][:, hf * BN:(hf + 1) * BN], pg[0]),
                             tag=f"{pfx}wbce{ee}_{B4}_{hf}")
                prog.rd(o, pg[1]); prog.wr(o, wbc[1])
            for m in range(32):
                slot = w1sl[m % 2]
                src2 = w1bf[0][:][ee].rearrange("(kc p) f -> p kc f", p=128)[:, :, m * 128:(m + 1) * 128]
                lo = prog.add("gd", lambda e, d=slot[0].rearrange("p (kc f) -> p kc f", f=128), s2=src2: e.dma_start(d, s2),
                              tag=f"{pfx}lw1{ee}_{B4}_{m}", key=slot[1].name)
                prog.rd(lo, w1bf[1]); prog.wr(lo, slot[1])
                for hf in range(2):
                    pg = pp[2 + (m * 2 + hf) % 2]
                    mmgroup(prog, pg,
                            [(slot[0][:, kc * 128:(kc + 1) * 128], XL[kc][0][:, hf * BN:(hf + 1) * BN]) for kc in range(DC)],
                            bufs_rd=[slot[1]] + [XL[kc][1] for kc in range(DC)],
                            tag=f"{pfx}w1{ee}_{B4}_{m}_{hf}")
                    o = prog.add("s", lambda e, ee=ee, m=m, pg=pg: e.activation(
                        tmp[0], pg[0], AF.Gelu, bias=b1c[0][:, ee * 32 + m: ee * 32 + m + 1]),
                        tag=f"{pfx}gel{ee}_{B4}_{m}_{hf}")
                    prog.rd(o, pg[1], b1c[1]); prog.wr(o, tmp[1])
                    o = prog.add("v", lambda e, m=m, hf=hf: e.tensor_mul(
                        hh[m][0][:, hf * BN:(hf + 1) * BN], tmp[0], wbc[0][:, hf * BN:(hf + 1) * BN]),
                        tag=f"{pfx}hw{ee}_{B4}_{m}_{hf}")
                    prog.rd(o, tmp[1], wbc[1]); prog.wr(o, hh[m][1])
            for m2 in range(DC):
                slot = w2sl[m2 % 2]
                src2 = w2bf[0][:][ee].rearrange("(kc p) f -> p kc f", p=128)[:, :, m2 * 128:(m2 + 1) * 128]
                lo = prog.add("gd", lambda e, d=slot[0].rearrange("p (kc f) -> p kc f", f=128), s2=src2: e.dma_start(d, s2),
                              tag=f"{pfx}lw2{ee}_{B4}_{m2}", key=slot[1].name)
                prog.rd(lo, w2bf[1]); prog.wr(lo, slot[1])
                for hf in range(2):
                    pg = pp[2 + (m2 * 2 + hf) % 2]
                    pairs = [(slot[0][:, kc * 128:(kc + 1) * 128], hh[kc][0][:, hf * BN:(hf + 1) * BN]) for kc in range(32)]
                    if ee == NEXP - 1:
                        pairs.append((b2sb[0][:, m2 * 128:(m2 + 1) * 128],
                                      wT8[0][:, hf * BN:(hf + 1) * BN]))
                    mmgroup(prog, pg, pairs,
                            bufs_rd=[slot[1], b2sb[1], wT8[1]] + [hh[kc][1] for kc in range(32)],
                            tag=f"{pfx}w2{ee}_{B4}_{m2}_{hf}")
                    if ee == 0:
                        o = prog.add("v", lambda e, m2=m2, hf=hf, pg=pg: e.tensor_copy(
                            acc[m2][0][:, hf * BN:(hf + 1) * BN], pg[0]), tag=f"{pfx}ac{ee}_{B4}_{m2}_{hf}")
                    else:
                        o = prog.add("v", lambda e, m2=m2, hf=hf, pg=pg: e.tensor_add(
                            acc[m2][0][:, hf * BN:(hf + 1) * BN], acc[m2][0][:, hf * BN:(hf + 1) * BN], pg[0]),
                            tag=f"{pfx}ac{ee}_{B4}_{m2}_{hf}")
                    prog.rd(o, pg[1], acc[m2][1]); prog.wr(o, acc[m2][1])
        for c in range(DC):
            o = prog.add("v", lambda e, c=c: e.tensor_add(
                acc[c][0], acc[c][0], xbat[dbl][0][:, c * MB:(c + 1) * MB]), tag=f"{pfx}res{B4}_{c}")
            prog.rd(o, acc[c][1], xbat[dbl][1]); prog.wr(o, acc[c][1])
        for c in range(DC):
            dstc = xout[:][c, :, B4 * MB:(B4 + 1) * MB]
            so = prog.add("sd", lambda e, d=dstc, s2=acc[c][0]: e.dma_start(d, s2),
                          tag=f"{pfx}st{B4}_{c}", key=acc[c][1].name)
            prog.rd(so, acc[c][1])
    return {"wgT": wgT, "wuT": wuT, "wdT": wdT}



def host_consts():
    c = {}
    c["ones128"] = np.ones((128, 128), dtype=BF16NP)
    c["onesf"] = np.ones((128, 128), dtype=np.float32)
    c["identbf"] = np.eye(128, dtype=BF16NP)
    mask = np.zeros((128, 128), dtype=np.float32)
    for p in range(128):
        mask[p, p + 1:] = -1e30
    c["causal128"] = mask
    RD = 32
    inv_freq = 1.0 / 10000.0 ** (np.arange(0, RD, 2, dtype=np.float32) / RD)
    t = np.arange(TSEQ, dtype=np.float32)
    freqs = np.outer(t, inv_freq)
    emb = np.concatenate([freqs, freqs], -1)
    c["ropecos"] = np.tile(np.cos(emb).T, (4, 1)).astype(BF16NP)
    c["ropesin"] = np.tile(np.sin(emb).T, (4, 1)).astype(BF16NP)
    R = np.zeros((RD, RD), dtype=np.float32)
    for i in range(16):
        R[i, i + 16] = -1.0
        R[i + 16, i] = 1.0
    RT = R.T
    blk = np.zeros((128, 128), dtype=np.float32)
    for b in range(4):
        blk[b * 32:(b + 1) * 32, b * 32:(b + 1) * 32] = RT
    c["ropeRT"] = blk.astype(BF16NP)
    c["iota8t"] = np.tile(np.arange(8, dtype=np.float32), (128, 64))
    c["identf"] = np.eye(128, dtype=np.float32)
    u = np.zeros((128, 128), dtype=np.float32)
    for i in range(128):
        u[i, i + 1:] = 1.0
    c["uppstrict"] = u.astype(BF16NP)
    return c


CONST_DT = {np.dtype(BF16NP): BF16, np.dtype(np.float32): F32,
            np.dtype(np.int32): I32}


def load_consts(nc, prog, consts, names):
    out = {}
    for k in names:
        v = consts[k]
        d = nc.dram_tensor(k, list(v.shape), CONST_DT[v.dtype], kind="ExternalInput")
        t = sb(nc, k + "_s", list(v.shape), CONST_DT[v.dtype])
        op = prog.add("sd", lambda e, t=t, d=d: e.dma_start(t[0], d[:]), tag=f"ldc_{k}", key=k + "_s")
        prog.wr(op, t[1])
        out[k] = t
    return out


def to_fm(x):
    t = x.shape[0]
    return np.ascontiguousarray(x.T.reshape(DC, 128, t))


def from_fm(xfm):
    return np.ascontiguousarray(np.asarray(xfm).reshape(D, -1).T)


def launch_ffn(x_shards, wg, wu, wd):
    consts = host_consts()
    nc = bass.Bass("TRN2", target_bir_lowering=False)
    xin = nc.dram_tensor("xin", [DC, 128, TC], F32, kind="ExternalInput")
    xout = nc.dram_tensor("xout", [DC, 128, TC], F32, kind="ExternalOutput")
    wg_d = nc.dram_tensor("wg", [HID, D], I32, kind="ExternalInput")
    wu_d = nc.dram_tensor("wu", [HID, D], I32, kind="ExternalInput")
    wd_d = nc.dram_tensor("wd", [D, HID], I32, kind="ExternalInput")
    prog = Prog(nc, "ffn")
    cst = load_consts(nc, prog, consts, ["ones128"])
    build_ffn(nc, prog, xin, xout, wg_d, wu_d, wd_d, cst)
    prog.run()
    in_maps = []
    for c in range(NCORES):
        m = {"xin": to_fm(x_shards[c]), "wg": wg, "wu": wu, "wd": wd,
             "ones128": consts["ones128"]}
        in_maps.append(m)
    res = run_bass_kernel_spmd(nc, in_maps, core_ids=list(range(NCORES)))
    return [from_fm(r["xout"]) for r in res.results]


def launch_attn_b(x_shards, wq, bq, wk, bk, wv, bv, wo, bo):
    consts = host_consts()
    nc = bass.Bass("TRN2", target_bir_lowering=False)
    xin = nc.dram_tensor("xin", [DC, 128, TC], F32, kind="ExternalInput")
    xout = nc.dram_tensor("xout", [DC, 128, TC], F32, kind="ExternalOutput")
    wd = {}
    for nm, arr in (("wq", wq), ("wk", wk), ("wv", wv), ("wo", wo)):
        wd[nm] = nc.dram_tensor(nm, [D, D], F32, kind="ExternalInput")
    for nm in ("bq", "bk", "bv", "bo"):
        wd[nm] = nc.dram_tensor(nm, [D], F32, kind="ExternalInput")
    prog = Prog(nc, "ab")
    cst = load_consts(nc, prog, consts, ["ones128", "onesf", "identbf", "causal128"])
    build_attn_b(nc, prog, xin, xout, wd, cst)
    prog.run()
    in_maps = []
    for c in range(NCORES):
        m = {"xin": to_fm(x_shards[c]), "wq": wq, "bq": bq, "wk": wk, "bk": bk,
             "wv": wv, "bv": bv, "wo": wo, "bo": bo}
        for k in ("ones128", "onesf", "identbf", "causal128"):
            m[k] = consts[k]
        in_maps.append(m)
    res = run_bass_kernel_spmd(nc, in_maps, core_ids=list(range(NCORES)))
    return [from_fm(r["xout"]) for r in res.results]


def moe_call(x_shards, Wg, bg, W1, b1, W2, b2):
    consts = host_consts()
    nc = bass.Bass("TRN2", target_bir_lowering=False)
    xin = nc.dram_tensor("xin", [DC, 128, TC], F32, kind="ExternalInput")
    lg_out = nc.dram_tensor("lgout", [8, TC], F32, kind="ExternalOutput")
    wg_d = nc.dram_tensor("Wg", [D, NEXP], F32, kind="ExternalInput")
    bg_d = nc.dram_tensor("bg", [NEXP], F32, kind="ExternalInput")
    prog = Prog(nc, "mg")
    cst = load_consts(nc, prog, consts, ["ones128", "onesf"])
    build_moe_logits(nc, prog, xin, lg_out, wg_d, bg_d, cst)
    prog.run()
    in_maps = [{"xin": to_fm(x_shards[c]), "Wg": Wg, "bg": bg,
                "ones128": consts["ones128"], "onesf": consts["onesf"]}
               for c in range(NCORES)]
    res = run_bass_kernel_spmd(nc, in_maps, core_ids=list(range(NCORES)))
    logits = np.concatenate([np.asarray(r["lgout"]).T for r in res.results], axis=0)

    lgt = np.ascontiguousarray(logits.reshape(NTILES, 128, NEXP).transpose(1, 0, 2))

    nc = bass.Bass("TRN2", target_bir_lowering=False)
    lgt_d = nc.dram_tensor("lgt", [128, NTILES, NEXP], F32, kind="ExternalInput")
    w_out = nc.dram_tensor("wout", [128, NTILES, NEXP], F32, kind="ExternalOutput")
    prog = Prog(nc, "gt")
    cst = load_consts(nc, prog, consts, ["ones128", "onesf", "iota8t", "uppstrict"])
    build_moe_gate(nc, prog, lgt_d, w_out, cst)
    prog.run()
    in_maps = [{"lgt": lgt, "ones128": consts["ones128"], "onesf": consts["onesf"],
                "iota8t": consts["iota8t"], "uppstrict": consts["uppstrict"]}
               for c in range(NCORES)]
    res = run_bass_kernel_spmd(nc, in_maps, core_ids=list(range(NCORES)))
    w_tiled = np.asarray(res.results[0]["wout"])
    w_all = np.ascontiguousarray(w_tiled.transpose(1, 0, 2).reshape(32768, NEXP))

    nc = bass.Bass("TRN2", target_bir_lowering=False)
    xin = nc.dram_tensor("xin", [DC, 128, TC], F32, kind="ExternalInput")
    xout = nc.dram_tensor("xout", [DC, 128, TC], F32, kind="ExternalOutput")
    w_own_d = nc.dram_tensor("wown", [128, 32, NEXP], F32, kind="ExternalInput")
    wT_d = nc.dram_tensor("wT", [NEXP, TC], F32, kind="ExternalInput")
    w1_d = nc.dram_tensor("W1", [NEXP, D, 4096], F32, kind="ExternalInput")
    b1_d = nc.dram_tensor("b1", [NEXP, 4096], F32, kind="ExternalInput")
    w2_d = nc.dram_tensor("W2", [NEXP, 4096, D], F32, kind="ExternalInput")
    b2_d = nc.dram_tensor("b2", [NEXP, D], F32, kind="ExternalInput")
    prog = Prog(nc, "mm")
    cst = load_consts(nc, prog, consts, ["ones128", "onesf"])
    build_moe_mlp(nc, prog, xin, xout, w_own_d, wT_d, w1_d, b1_d, w2_d, b2_d, cst)
    prog.run()
    in_maps = []
    for c in range(NCORES):
        w_own = w_all[c * TC:(c + 1) * TC]
        in_maps.append({
            "xin": to_fm(x_shards[c]),
            "wown": np.ascontiguousarray(w_own.reshape(32, 128, NEXP).transpose(1, 0, 2)),
            "wT": np.ascontiguousarray(w_own.T),
            "W1": W1, "b1": b1, "W2": W2, "b2": b2,
            "ones128": consts["ones128"], "onesf": consts["onesf"]})
    res = run_bass_kernel_spmd(nc, in_maps, core_ids=list(range(NCORES)))
    return [from_fm(r["xout"]) for r in res.results]



def build_moe_fused(nc, prog, xin, xout, wd, cst, pfx="mf", pre=None, eps=1e-5):
    ones = cst["ones128"]
    onesf = cst["onesf"]
    iota = cst["iota8t"]
    upst = cst["uppstrict"]
    identf = cst["identf"]
    BIG = 1e9
    MB = 1024
    NT = 32

    if pre is None:
        w1bf = (dram(nc, f"{pfx}_w1bf", [NEXP, D, 4096], BF16), Buf(f"{pfx}_w1bf"))
        w2bf = (dram(nc, f"{pfx}_w2bf", [NEXP, 4096, D], BF16), Buf(f"{pfx}_w2bf"))
        emit_cast_to_dram(prog, wd["W1"], w1bf, f"{pfx}cw1")
        emit_cast_to_dram(prog, wd["W2"], w2bf, f"{pfx}cw2")
    else:
        w1bf, w2bf = pre["w1bf"], pre["w2bf"]

    cc_in = (dram(nc, f"{pfx}_ccin", [1, 16], F32), Buf(f"{pfx}_ccin"))
    cc_out = (dram(nc, f"{pfx}_ccout", [8, 16], F32), Buf(f"{pfx}_ccout"))

    pp = [psum(nc, f"{pfx}_pp{i}", [128, BN]) for i in range(4)]
    lg_ps = psum(nc, f"{pfx}_lgps", [8, BN])
    rw_ps = psum(nc, f"{pfx}_rwps", [1, BN])

    lnsc = mk_ln_scratch(nc, pfx, psA=pp[0], psB=pp[1])
    xbat = [sb(nc, f"{pfx}_xb0", [128, DC * MB], F32)] * 2
    wgs = sb(nc, f"{pfx}_wgs", [128, DC * NEXP], F32)
    onescol = sb(nc, f"{pfx}_onescol", [128, 1], F32)
    ncs = sb(nc, f"{pfx}_ncs", [8, 1], F32)
    bgrow = sb(nc, f"{pfx}_bgrow", [1, NEXP], F32)
    bgcol = sb(nc, f"{pfx}_bgcol", [8, 1], F32)
    w_sb = sb(nc, f"{pfx}_wsb", [128, NT * 8], F32)
    b1c = sb(nc, f"{pfx}_b1c", [128, NEXP * 32], F32)
    b1row = sb(nc, f"{pfx}_b1row", [1, 1024], F32)
    b2c = sb(nc, f"{pfx}_b2c", [128, NEXP * DC], F32)
    ones1 = sb(nc, f"{pfx}_ones1", [1, 1], F32)
    scope_begin()
    lg_sb = sb(nc, f"{pfx}_lgsb", [8, TC], F32)
    t2 = sb(nc, f"{pfx}_t2", [8, BN], F32)
    lgt = sb(nc, f"{pfx}_lgt", [128, NT * 8], F32)
    OHa = [sb(nc, f"{pfx}_oh{k}", [128, NT * 8], BF16) for k in range(2)]
    gv = [sb(nc, f"{pfx}_gv{k}", [128, NT], F32) for k in range(2)]
    m1 = sb(nc, f"{pfx}_m1", [128, NT], F32)
    mx2 = sb(nc, f"{pfx}_m2", [128, NT], F32)
    idx = sb(nc, f"{pfx}_idx", [128, NT], F32)
    tt_ = sb(nc, f"{pfx}_tt", [128, NT * 8], F32)
    msk = sb(nc, f"{pfx}_msk", [128, NT * 8], F32)
    Pin = sb(nc, f"{pfx}_Pin", [128, 2 * NT * 8], F32)
    crow = [sb(nc, f"{pfx}_crow{i}", [1, 2 * NT * 8], F32) for i in range(2)]
    orow = sb(nc, f"{pfx}_orow", [1, 2 * NT * 8], F32)
    trow = sb(nc, f"{pfx}_trow", [1, 16], F32)
    G_sb = sb(nc, f"{pfx}_G", [8, 16], F32)
    cmask = sb(nc, f"{pfx}_cmask", [8, 2], F32)
    offr = sb(nc, f"{pfx}_offr", [1, 16], F32)
    pos = sb(nc, f"{pfx}_pos", [128, NT * 8], F32)
    pat = sb(nc, f"{pfx}_pat", [128, NT], F32)
    keep = sb(nc, f"{pfx}_keep", [128, NT], F32)
    ck = sb(nc, f"{pfx}_ck", [128, NT], F32)

    def r3(ap, e=8):
        return ap.rearrange("p (g e) -> p g e", e=e)

    def bc2(a, b):
        return bass.broadcast_tensor_aps(a, b)

    o = prog.add("v", lambda e: e.memset(ones1[0], 1.0), tag=f"{pfx}one1")
    prog.wr(o, ones1[1])
    o = prog.add("v", lambda e: e.memset(onescol[0], 1.0), tag=f"{pfx}onec")
    prog.wr(o, onescol[1])
    lo = prog.add("sd", lambda e: e.dma_start(cmask[0], wd["coremask"][:]),
                  tag=f"{pfx}ldcm", key=cmask[1].name)
    prog.wr(lo, cmask[1])
    lo = prog.add("sd", lambda e: e.dma_start(
        wgs[0].rearrange("p (kc e) -> p kc e", e=NEXP),
        wd["Wg"][:].rearrange("(kc p) e -> p kc e", p=128)), tag=f"{pfx}ldwg", key=wgs[1].name)
    prog.wr(lo, wgs[1])
    lo = prog.add("sd", lambda e: e.dma_start(bgrow[0], wd["bg"][:].rearrange("(o e) -> o e", o=1)),
                  tag=f"{pfx}ldbg", key=bgrow[1].name)
    prog.wr(lo, bgrow[1])
    cs_ps = (pp[2][0][0:8, 0:1], pp[2][1])
    mmgroup(prog, cs_ps, [(wgs[0][:, kc * NEXP:(kc + 1) * NEXP], onescol[0]) for kc in range(DC)],
            bufs_rd=[wgs[1], onescol[1]], tag=f"{pfx}cs")
    o = prog.add("v", lambda e: e.tensor_scalar(ncs[0], cs_ps[0], -1.0, None, ALU.mult), tag=f"{pfx}ncs")
    prog.rd(o, cs_ps[1]); prog.wr(o, ncs[1])
    bg_ps = (pp[3][0][0:8, 0:1], pp[3][1])
    mmgroup(prog, bg_ps, [(bgrow[0], ones1[0])], bufs_rd=[bgrow[1], ones1[1]], tag=f"{pfx}bgT")
    o = prog.add("v", lambda e: e.tensor_copy(bgcol[0], bg_ps[0]), tag=f"{pfx}bgc")
    prog.rd(o, bg_ps[1]); prog.wr(o, bgcol[1])
    for ee in range(NEXP):
        for q4 in range(4):
            lo = prog.add("sd", lambda e, ee=ee, q4=q4: e.dma_start(
                b1row[0], wd["b1"][:][ee, q4 * 1024:(q4 + 1) * 1024].rearrange("(o f) -> o f", o=1)),
                tag=f"{pfx}ldb1{ee}_{q4}", key=b1row[1].name)
            prog.wr(lo, b1row[1])
            for m8 in range(8):
                m = q4 * 8 + m8
                pg = pp[2 + m % 2]
                mmgroup(prog, (pg[0][:, 0:1], pg[1]), [(b1row[0][:, m8 * 128:(m8 + 1) * 128], ones1[0])],
                        bufs_rd=[b1row[1], ones1[1]], tag=f"{pfx}b1T{ee}_{m}")
                o = prog.add("v", lambda e, ee=ee, m=m, pg=pg: e.tensor_copy(b1c[0][:, ee * 32 + m: ee * 32 + m + 1], pg[0][:, 0:1]),
                             tag=f"{pfx}b1Te{ee}_{m}")
                prog.rd(o, pg[1]); prog.wr(o, b1c[1])
        lo = prog.add("sd", lambda e, ee=ee: e.dma_start(
            b1row[0][:, 0:1024], wd["b2"][:][ee].rearrange("(o f) -> o f", o=1)),
            tag=f"{pfx}ldb2{ee}", key=b1row[1].name)
        prog.wr(lo, b1row[1])
        for m8 in range(DC):
            pg = pp[2 + m8 % 2]
            mmgroup(prog, (pg[0][:, 0:1], pg[1]), [(b1row[0][:, m8 * 128:(m8 + 1) * 128], ones1[0])],
                    bufs_rd=[b1row[1], ones1[1]], tag=f"{pfx}b2T{ee}_{m8}")
            o = prog.add("v", lambda e, ee=ee, m8=m8, pg=pg: e.tensor_copy(b2c[0][:, ee * DC + m8: ee * DC + m8 + 1], pg[0][:, 0:1]),
                         tag=f"{pfx}b2Te{ee}_{m8}")
            prog.rd(o, pg[1]); prog.wr(o, b2c[1])

    X8b = [[(xbat[i][0][:, c * MB:(c + 1) * MB][:, 0:BN], xbat[i][1]) for c in range(DC)]
           for i in range(2)]
    for b in range(NB):
        dbl = 0
        hf = b % 2
        if hf == 0:
            src = xin[:].rearrange("c p t -> p c t")[:, :, (b // 2) * MB:(b // 2 + 1) * MB]
            op = prog.add("sd", lambda e, d=xbat[0][0].rearrange("p (c t) -> p c t", c=DC), s2=src: e.dma_start(d, s2),
                          tag=f"{pfx}ldx{b}", key=xbat[0][1].name)
            prog.wr(op, xbat[0][1])
        X8h = [(xbat[0][0][:, c * MB + hf * BN: c * MB + (hf + 1) * BN], xbat[0][1]) for c in range(DC)]
        emit_ln(prog, X8h, None, lnsc, ones, eps=eps, norm=False)
        mmgroup(prog, lg_ps, [(wgs[0][:, kc * NEXP:(kc + 1) * NEXP], X8h[kc][0]) for kc in range(DC)],
                bufs_rd=[wgs[1], xbat[0][1]], tag=f"{pfx}raw{b}")
        o = prog.add("v", lambda e: e.tensor_mul(t2[0], lg_ps[0], lnsc["rstd"][0][0:8, :]), tag=f"{pfx}l1{b}")
        prog.rd(o, lg_ps[1], lnsc["rstd"][1]); prog.wr(o, t2[1])
        o = prog.add("v", lambda e: e.scalar_tensor_tensor(t2[0], lnsc["mr"][0][0:8, :], ncs[0], t2[0], ALU.mult, ALU.add),
                     tag=f"{pfx}l2{b}")
        prog.rd(o, lnsc["mr"][1], ncs[1], t2[1]); prog.wr(o, t2[1])
        o = prog.add("v", lambda e, b=b: e.tensor_scalar(lg_sb[0][:, b * BN:(b + 1) * BN], t2[0], bgcol[0], None, ALU.add),
                     tag=f"{pfx}l3{b}")
        prog.rd(o, t2[1], bgcol[1]); prog.wr(o, lg_sb[1])

    for half in range(2):
        pg = pp[2 + half % 2]
        for t in range(16):
            gt = half * 16 + t
            to = prog.add("t", lambda e, gt=gt, t=t, pg=pg: e.transpose(
                pg[0][:, t * 8:(t + 1) * 8], lg_sb[0][:, gt * 128:(gt + 1) * 128],
                identf[0][0:8, 0:8]), tag=f"{pfx}lgT{gt}")
            prog.rd(to, lg_sb[1], identf[1]); prog.wr(to, pg[1])
        o = prog.add("v", lambda e, half=half, pg=pg: e.tensor_copy(
            lgt[0][:, half * 128:(half + 1) * 128], pg[0][:, 0:128]), tag=f"{pfx}lgTe{half}")
        prog.rd(o, pg[1]); prog.wr(o, lgt[1])

    def rt(ap, e=8):
        return ap.rearrange("p (g e) -> p g e", e=e)

    o = prog.add("v", lambda e: e.tensor_reduce(rt(m1[0], 1), rt(lgt[0]), axis=AX.X, op=ALU.max), tag=f"{pfx}m1")
    prog.rd(o, lgt[1]); prog.wr(o, m1[1])
    x1, x2 = bc2(rt(lgt[0]), rt(m1[0], 1))
    o = prog.add("v", lambda e, x1=x1, x2=x2: e.tensor_tensor(rt(tt_[0]), x1, x2, ALU.is_equal), tag=f"{pfx}eq")
    prog.rd(o, lgt[1], m1[1]); prog.wr(o, tt_[1])
    o = prog.add("v", lambda e: e.tensor_scalar(tt_[0], tt_[0], -BIG, BIG, ALU.mult, ALU.add), tag=f"{pfx}tb")
    prog.rd(o, tt_[1]); prog.wr(o, tt_[1])
    o = prog.add("v", lambda e: e.tensor_add(tt_[0], tt_[0], iota[0][:, 0:NT * 8]), tag=f"{pfx}ti")
    prog.rd(o, tt_[1], iota[1]); prog.wr(o, tt_[1])
    o = prog.add("v", lambda e: e.tensor_reduce(rt(idx[0], 1), rt(tt_[0]), axis=AX.X, op=ALU.min), tag=f"{pfx}i1")
    prog.rd(o, tt_[1]); prog.wr(o, idx[1])
    x1, x2 = bc2(rt(iota[0][:, 0:NT * 8]), rt(idx[0], 1))
    o = prog.add("v", lambda e, x1=x1, x2=x2: e.tensor_tensor(rt(OHa[0][0]), x1, x2, ALU.is_equal), tag=f"{pfx}oh1")
    prog.rd(o, iota[1], idx[1]); prog.wr(o, OHa[0][1])
    o = prog.add("v", lambda e: e.scalar_tensor_tensor(msk[0], OHa[0][0], -BIG, lgt[0], ALU.mult, ALU.add), tag=f"{pfx}mk")
    prog.rd(o, OHa[0][1], lgt[1]); prog.wr(o, msk[1])
    o = prog.add("v", lambda e: e.tensor_reduce(rt(mx2[0], 1), rt(msk[0]), axis=AX.X, op=ALU.max), tag=f"{pfx}m2")
    prog.rd(o, msk[1]); prog.wr(o, mx2[1])
    x1, x2 = bc2(rt(msk[0]), rt(mx2[0], 1))
    o = prog.add("v", lambda e, x1=x1, x2=x2: e.tensor_tensor(rt(tt_[0]), x1, x2, ALU.is_equal), tag=f"{pfx}eq2")
    prog.rd(o, msk[1], mx2[1]); prog.wr(o, tt_[1])
    o = prog.add("v", lambda e: e.tensor_scalar(tt_[0], tt_[0], -BIG, BIG, ALU.mult, ALU.add), tag=f"{pfx}tb2")
    prog.rd(o, tt_[1]); prog.wr(o, tt_[1])
    o = prog.add("v", lambda e: e.tensor_add(tt_[0], tt_[0], iota[0][:, 0:NT * 8]), tag=f"{pfx}ti2")
    prog.rd(o, tt_[1], iota[1]); prog.wr(o, tt_[1])
    o = prog.add("v", lambda e: e.tensor_reduce(rt(idx[0], 1), rt(tt_[0]), axis=AX.X, op=ALU.min), tag=f"{pfx}i2")
    prog.rd(o, tt_[1]); prog.wr(o, idx[1])
    x1, x2 = bc2(rt(iota[0][:, 0:NT * 8]), rt(idx[0], 1))
    o = prog.add("v", lambda e, x1=x1, x2=x2: e.tensor_tensor(rt(OHa[1][0]), x1, x2, ALU.is_equal), tag=f"{pfx}oh2")
    prog.rd(o, iota[1], idx[1]); prog.wr(o, OHa[1][1])
    o = prog.add("v", lambda e: e.tensor_sub(m1[0], m1[0], mx2[0]), tag=f"{pfx}d12")
    prog.rd(o, m1[1], mx2[1]); prog.wr(o, m1[1])
    o = prog.add("s", lambda e: e.activation(gv[0][0], m1[0], AF.Sigmoid), tag=f"{pfx}sg")
    prog.rd(o, m1[1]); prog.wr(o, gv[0][1])
    o = prog.add("v", lambda e: e.tensor_scalar(gv[1][0], gv[0][0], -1.0, 1.0, ALU.mult, ALU.add), tag=f"{pfx}g2")
    prog.rd(o, gv[0][1]); prog.wr(o, gv[1][1])

    for k in range(2):
        pg = pp[2 + k % 2]
        mmgroup(prog, (pg[0][:, 0:NT * 8], pg[1]), [(upst[0], OHa[k][0])],
                bufs_rd=[upst[1], OHa[k][1]], tag=f"{pfx}pa{k}")
        o = prog.add("v", lambda e, k=k, pg=pg: e.tensor_copy(Pin[0][:, k * NT * 8:(k + 1) * NT * 8], pg[0][:, 0:NT * 8]),
                     tag=f"{pfx}pae{k}")
        prog.rd(o, pg[1]); prog.wr(o, Pin[1])
        mmgroup(prog, (rw_ps[0][:, 0:NT * 8], rw_ps[1]), [(ones[0][:, 0:1], OHa[k][0])],
                bufs_rd=[ones[1], OHa[k][1]], tag=f"{pfx}ct{k}")
        o = prog.add("v", lambda e, k=k: e.tensor_copy(crow[0][0][:, k * NT * 8:(k + 1) * NT * 8], rw_ps[0][:, 0:NT * 8]),
                     tag=f"{pfx}cte{k}")
        prog.rd(o, rw_ps[1]); prog.wr(o, crow[0][1])

    cur = 0
    sh = 8
    while sh < NT * 8:
        nxt = 1 - cur
        for k in range(2):
            base = k * NT * 8
            o = prog.add("v", lambda e, cur=cur, nxt=nxt, sh=sh, base=base: e.tensor_copy(
                crow[nxt][0][:, base:base + sh], crow[cur][0][:, base:base + sh]), tag=f"{pfx}sch{sh}_{k}")
            prog.rd(o, crow[cur][1]); prog.wr(o, crow[nxt][1])
            o = prog.add("v", lambda e, cur=cur, nxt=nxt, sh=sh, base=base: e.tensor_add(
                crow[nxt][0][:, base + sh:base + NT * 8], crow[cur][0][:, base + sh:base + NT * 8],
                crow[cur][0][:, base:base + NT * 8 - sh]), tag=f"{pfx}sca{sh}_{k}")
            prog.rd(o, crow[cur][1]); prog.wr(o, crow[nxt][1])
        cur = nxt
        sh *= 2
    for k in range(2):
        o = prog.add("v", lambda e, k=k, cur=cur: e.tensor_copy(
            trow[0][:, k * 8:(k + 1) * 8], crow[cur][0][:, k * NT * 8 + (NT - 1) * 8: (k + 1) * NT * 8]),
            tag=f"{pfx}tot{k}")
        prog.rd(o, crow[cur][1]); prog.wr(o, trow[1])
        o = prog.add("v", lambda e, k=k: e.memset(orow[0][:, k * NT * 8:k * NT * 8 + 8], 0.0), tag=f"{pfx}oz{k}")
        prog.wr(o, orow[1])
        o = prog.add("v", lambda e, k=k, cur=cur: e.tensor_copy(
            orow[0][:, k * NT * 8 + 8:(k + 1) * NT * 8], crow[cur][0][:, k * NT * 8:(k + 1) * NT * 8 - 8]),
            tag=f"{pfx}osh{k}")
        prog.rd(o, crow[cur][1]); prog.wr(o, orow[1])

    so = prog.add("sd", lambda e: e.dma_start(cc_in[0][:], trow[0]), tag=f"{pfx}stcc", key=cc_in[1].name)
    prog.rd(so, trow[1]); prog.wr(so, cc_in[1])
    cco = prog.add("p", lambda e: e.collective_compute(
        "AllGather", ALU.bypass, replica_groups=[[0, 1, 2, 3, 4, 5, 6, 7]],
        ins=[cc_in[0][:].opt()], outs=[cc_out[0][:].opt()]), tag=f"{pfx}cc")
    prog.rd(cco, cc_in[1]); prog.wr(cco, cc_out[1])
    lo = prog.add("sd", lambda e: e.dma_start(G_sb[0], cc_out[0][:]), tag=f"{pfx}ldG", key=G_sb[1].name)
    prog.rd(lo, cc_out[1]); prog.wr(lo, G_sb[1])

    mmgroup(prog, (rw_ps[0][0:1, 0:16], rw_ps[1]), [(cmask[0][:, 0:1], G_sb[0])],
            bufs_rd=[cmask[1], G_sb[1]], tag=f"{pfx}off1")
    mmgroup(prog, (rw_ps[0][0:1, 8:16], rw_ps[1]), [(cmask[0][:, 1:2], G_sb[0][:, 0:8])],
            bufs_rd=[cmask[1], G_sb[1]], tag=f"{pfx}off2", start=False)
    o = prog.add("v", lambda e: e.tensor_copy(offr[0], rw_ps[0][0:1, 0:16]), tag=f"{pfx}offe")
    prog.rd(o, rw_ps[1]); prog.wr(o, offr[1])

    for k in range(2):
        a3 = orow[0][:, k * NT * 8:(k + 1) * NT * 8].rearrange("p (g e) -> p g e", e=8)
        b3 = offr[0][:, k * 8:(k + 1) * 8].rearrange("p (g e) -> p g e", e=8)
        x1, x2 = bc2(a3, b3)
        o = prog.add("v", lambda e, x1=x1, x2=x2: e.tensor_tensor(x1, x1, x2, ALU.add), tag=f"{pfx}oadd{k}")
        prog.rd(o, offr[1], orow[1]); prog.wr(o, orow[1])

    for k in range(2):
        mmgroup(prog, (pp[2][0][:, 0:NT * 8], pp[2][1]), [(onesf[0][0:1, :], orow[0][:, k * NT * 8:(k + 1) * NT * 8])],
                bufs_rd=[onesf[1], orow[1]], tag=f"{pfx}ob{k}")
        o = prog.add("v", lambda e, k=k: e.tensor_add(pos[0], pp[2][0][:, 0:NT * 8], Pin[0][:, k * NT * 8:(k + 1) * NT * 8]),
                     tag=f"{pfx}pp{k}")
        prog.rd(o, pp[2][1], Pin[1]); prog.wr(o, pos[1])
        o = prog.add("v", lambda e, k=k: e.tensor_mul(tt_[0], pos[0], OHa[k][0]), tag=f"{pfx}pm{k}")
        prog.rd(o, pos[1], OHa[k][1]); prog.wr(o, tt_[1])
        o = prog.add("v", lambda e: e.tensor_reduce(rt(pat[0], 1), rt(tt_[0]), axis=AX.X, op=ALU.add), tag=f"{pfx}pr{k}")
        prog.rd(o, tt_[1]); prog.wr(o, pat[1])
        o = prog.add("v", lambda e: e.tensor_scalar(keep[0], pat[0], CAP - 0.5, None, ALU.is_lt), tag=f"{pfx}kp{k}")
        prog.rd(o, pat[1]); prog.wr(o, keep[1])
        o = prog.add("v", lambda e, k=k: e.tensor_mul(ck[0], keep[0], gv[k][0]), tag=f"{pfx}ckm{k}")
        prog.rd(o, keep[1], gv[k][1]); prog.wr(o, ck[1])
        x1, x2 = bc2(rt(OHa[k][0]), rt(ck[0], 1))
        if k == 0:
            o = prog.add("v", lambda e, x1=x1, x2=x2: e.tensor_tensor(rt(w_sb[0]), x1, x2, ALU.mult), tag=f"{pfx}w1{k}")
            prog.rd(o, OHa[k][1], ck[1]); prog.wr(o, w_sb[1])
        else:
            o = prog.add("v", lambda e, x1=x1, x2=x2: e.tensor_tensor(rt(tt_[0]), x1, x2, ALU.mult), tag=f"{pfx}w2a{k}")
            prog.rd(o, OHa[k][1], ck[1]); prog.wr(o, tt_[1])
            o = prog.add("v", lambda e: e.tensor_add(w_sb[0], w_sb[0], tt_[0]), tag=f"{pfx}w2b{k}")
            prog.rd(o, tt_[1]); prog.wr(o, w_sb[1])

    scope_end()
    XL = [sb(nc, f"{pfx}_xl{c}", [128, MB], BF16) for c in range(DC)]
    hh = [sb(nc, f"{pfx}_hh{m}", [128, MB], BF16) for m in range(16)]
    acc = [sb(nc, f"{pfx}_acc{c}", [128, MB], F32) for c in range(DC)]
    w1sl = [sb(nc, f"{pfx}_w1sl{i}", [128, D], BF16) for i in range(2)]
    w2sl = [sb(nc, f"{pfx}_w2sl0", [128, 4096], BF16)] * 2
    wrow = sb(nc, f"{pfx}_wrow", [1, MB], F32)
    wbc = sb(nc, f"{pfx}_wbc", [128, MB], F32)
    tmp = sb(nc, f"{pfx}_tmp", [128, BN], BF16)

    for B4 in range(TC // MB):
        src = xin[:].rearrange("c p t -> p c t")[:, :, B4 * MB:(B4 + 1) * MB]
        op = prog.add("sd", lambda e, d=xbat[0][0].rearrange("p (c t) -> p c t", c=DC), s2=src: e.dma_start(d, s2),
                      tag=f"{pfx}ldx2{B4}", key=xbat[0][1].name)
        prog.wr(op, xbat[0][1])
        for hf in range(2):
            X8h = [(xbat[0][0][:, c * MB + hf * BN: c * MB + (hf + 1) * BN], xbat[0][1]) for c in range(DC)]
            XLh = [(XL[c][0][:, hf * BN:(hf + 1) * BN], XL[c][1]) for c in range(DC)]
            emit_ln(prog, X8h, XLh, lnsc, ones, eps=eps)
        for ee in range(NEXP):
            for t8 in range(8):
                tile = B4 * 8 + t8
                pg = pp[2 + (t8 // 4) % 2]
                to = prog.add("t", lambda e, tile=tile, t8=t8, ee=ee, pg=pg: e.transpose(
                    pg[0][0:1, (t8 % 4) * 128:(t8 % 4) * 128 + 128],
                    w_sb[0][:, tile * 8 + ee: tile * 8 + ee + 1], identf[0]),
                    tag=f"{pfx}wT{ee}_{B4}_{t8}")
                prog.rd(to, w_sb[1], identf[1]); prog.wr(to, pg[1])
                if t8 % 4 == 3:
                    o = prog.add("v", lambda e, t8=t8, pg=pg: e.tensor_copy(
                        wrow[0][:, (t8 - 3) * 128:(t8 + 1) * 128], pg[0][0:1, 0:512]), tag=f"{pfx}wTe{ee}_{B4}_{t8}")
                    prog.rd(o, pg[1]); prog.wr(o, wrow[1])
            for hf in range(2):
                pg = pp[2 + hf % 2]
                mmgroup(prog, pg, [(onesf[0][0:1, :], wrow[0][:, hf * BN:(hf + 1) * BN])],
                        bufs_rd=[onesf[1], wrow[1]], tag=f"{pfx}wbc{ee}_{B4}_{hf}")
                o = prog.add("v", lambda e, hf=hf, pg=pg: e.tensor_copy(wbc[0][:, hf * BN:(hf + 1) * BN], pg[0]),
                             tag=f"{pfx}wbce{ee}_{B4}_{hf}")
                prog.rd(o, pg[1]); prog.wr(o, wbc[1])
            for grp in range(2):
                for m16 in range(16):
                    m = grp * 16 + m16
                    slot = w1sl[m % 2]
                    src2 = w1bf[0][:][ee].rearrange("(kc p) f -> p kc f", p=128)[:, :, m * 128:(m + 1) * 128]
                    lo = prog.add("gd", lambda e, d=slot[0].rearrange("p (kc f) -> p kc f", f=128), s2=src2: e.dma_start(d, s2),
                                  tag=f"{pfx}lw1{ee}_{B4}_{m}", key=slot[1].name)
                    prog.rd(lo, w1bf[1]); prog.wr(lo, slot[1])
                    for hf in range(2):
                        pg = pp[2 + (m * 2 + hf) % 2]
                        mmgroup(prog, pg,
                                [(slot[0][:, kc * 128:(kc + 1) * 128], XL[kc][0][:, hf * BN:(hf + 1) * BN]) for kc in range(DC)],
                                bufs_rd=[slot[1]] + [XL[kc][1] for kc in range(DC)],
                                tag=f"{pfx}w1m{ee}_{B4}_{m}_{hf}")
                        o = prog.add("s", lambda e, ee=ee, m=m, pg=pg: e.activation(
                            tmp[0], pg[0], AF.Gelu, bias=b1c[0][:, ee * 32 + m: ee * 32 + m + 1]),
                            tag=f"{pfx}gel{ee}_{B4}_{m}_{hf}")
                        prog.rd(o, pg[1], b1c[1]); prog.wr(o, tmp[1])
                        o = prog.add("v", lambda e, m16=m16, hf=hf: e.tensor_mul(
                            hh[m16][0][:, hf * BN:(hf + 1) * BN], tmp[0], wbc[0][:, hf * BN:(hf + 1) * BN]),
                            tag=f"{pfx}hw{ee}_{B4}_{m}_{hf}")
                        prog.rd(o, tmp[1], wbc[1]); prog.wr(o, hh[m16][1])
                for m2 in range(DC):
                    slot = w2sl[m2 % 2]
                    src2 = w2bf[0][:][ee].rearrange("(kc p) f -> p kc f", p=128)[
                        :, grp * 16:(grp + 1) * 16, m2 * 128:(m2 + 1) * 128]
                    lo = prog.add("gd", lambda e, d=slot[0][:, 0:2048].rearrange("p (kc f) -> p kc f", f=128), s2=src2: e.dma_start(d, s2),
                                  tag=f"{pfx}lw2{ee}_{B4}_{grp}_{m2}", key=slot[1].name)
                    prog.rd(lo, w2bf[1]); prog.wr(lo, slot[1])
                    for hf in range(2):
                        pg = pp[2 + (m2 * 2 + hf) % 2]
                        mmgroup(prog, pg,
                                [(slot[0][:, kc * 128:(kc + 1) * 128], hh[kc][0][:, hf * BN:(hf + 1) * BN]) for kc in range(16)],
                                bufs_rd=[slot[1]] + [hh[kc][1] for kc in range(16)],
                                tag=f"{pfx}w2m{ee}_{B4}_{grp}_{m2}_{hf}")
                        if ee == 0 and grp == 0:
                            o = prog.add("v", lambda e, m2=m2, hf=hf, pg=pg: e.tensor_copy(
                                acc[m2][0][:, hf * BN:(hf + 1) * BN], pg[0]), tag=f"{pfx}ac{ee}_{B4}_{grp}_{m2}_{hf}")
                        else:
                            o = prog.add("v", lambda e, m2=m2, hf=hf, pg=pg: e.tensor_add(
                                acc[m2][0][:, hf * BN:(hf + 1) * BN], acc[m2][0][:, hf * BN:(hf + 1) * BN], pg[0]),
                                tag=f"{pfx}ac{ee}_{B4}_{grp}_{m2}_{hf}")
                        prog.rd(o, pg[1], acc[m2][1]); prog.wr(o, acc[m2][1])
            for m2 in range(DC):
                o = prog.add("v", lambda e, ee=ee, m2=m2: e.scalar_tensor_tensor(
                    acc[m2][0], wbc[0], b2c[0][:, ee * DC + m2: ee * DC + m2 + 1], acc[m2][0], ALU.mult, ALU.add),
                    tag=f"{pfx}b2a{ee}_{B4}_{m2}")
                prog.rd(o, wbc[1], b2c[1], acc[m2][1]); prog.wr(o, acc[m2][1])
        for c in range(DC):
            o = prog.add("v", lambda e, c=c: e.tensor_add(
                acc[c][0], acc[c][0], xbat[0][0][:, c * MB:(c + 1) * MB]), tag=f"{pfx}res{B4}_{c}")
            prog.rd(o, acc[c][1], xbat[0][1]); prog.wr(o, acc[c][1])
        for c in range(DC):
            dstc = xout[:][c, :, B4 * MB:(B4 + 1) * MB]
            so = prog.add("sd", lambda e, d=dstc, s2=acc[c][0]: e.dma_start(d, s2),
                          tag=f"{pfx}st{B4}_{c}", key=acc[c][1].name)
            prog.rd(so, acc[c][1])
    return {"w1bf": w1bf, "w2bf": w2bf}



_CACHE = {}


def _get(name, buildfn):
    if name not in _CACHE:
        _CACHE[name] = buildfn()
    return _CACHE[name]


def _mk_simple(name, build, wspecs, cnames):
    def bf():
        consts = host_consts()
        nc = bass.Bass("TRN2", target_bir_lowering=False)
        xin = nc.dram_tensor("xin", [DC, 128, TC], F32, kind="ExternalInput")
        xout = nc.dram_tensor("xout", [DC, 128, TC], F32, kind="ExternalOutput")
        wd = {k: nc.dram_tensor(k, shp, dt, kind="ExternalInput")
              for k, shp, dt in wspecs}
        prog = Prog(nc, name)
        cst = load_consts(nc, prog, consts, cnames)
        build(nc, prog, xin, xout, wd, cst)
        prog.run()
        return nc, consts
    return bf


def _run(nc, consts, cnames, fm_shards, wmap):
    in_maps = []
    for c in range(NCORES):
        m = {"xin": fm_shards[c]}
        m.update(wmap)
        for k in cnames:
            m[k] = consts[k]
        in_maps.append(m)
    res = run_bass_kernel_spmd(nc, in_maps, core_ids=list(range(NCORES)))
    return [np.asarray(r["xout"]) for r in res.results]


def build_fused(nc):
    consts = host_consts()
    xio = [nc.dram_tensor(f"xs{i}", [DC, 128, TC], F32,
                          kind=("ExternalInput" if i == 0 else
                                "ExternalOutput" if i == 8 else "Internal"))
           for i in range(9)]
    wn = {}
    def W(name, shape, dt=F32):
        wn[name] = nc.dram_tensor(name, shape, dt, kind="ExternalInput")
        return wn[name]

    prog = Prog(nc, "fz")
    cn = ["ones128", "onesf", "identbf", "identf", "causal128", "ropecos",
          "ropesin", "ropeRT", "iota8t", "uppstrict"]
    cst = load_consts(nc, prog, consts, cn)

    scope_begin()
    wd = {k: W("L0aa_" + k, shp) for k, shp in
          [("W_kv", [D, KVR]), ("g_kv", [KVR]), ("W_kdec", [KVR, D]),
           ("W_vdec", [KVR, D]), ("W_q", [D, QR]), ("W_qdec", [QR, D]),
           ("W_krope", [D, QR]), ("W_qrope", [QR, QR]), ("W_o", [D, D])]}
    build_attn_mla(nc, prog, xio[0], xio[1], wd, cst, pfx="s0")
    scope_end(); prog.fence("f0")

    scope_begin()
    wg_d = W("L0f_wg", [HID, D], I32)
    wu_d = W("L0f_wu", [HID, D], I32)
    wd_d = W("L0f_wd", [D, HID], I32)
    ffn_pre = build_ffn(nc, prog, xio[1], xio[2], wg_d, wu_d, wd_d, cst, pfx="s1")
    scope_end(); prog.fence("f1")

    scope_begin()
    wd = {k: W("L0ab_" + k, shp) for k, shp in
          [("wq", [D, D]), ("wk", [D, D]), ("wv", [D, D]), ("wo", [D, D]), ("bo", [D])]}
    build_attn_lin(nc, prog, xio[2], xio[3], wd, cst, pfx="s2")
    scope_end(); prog.fence("f2")

    scope_begin()
    build_ffn(nc, prog, xio[3], xio[4], wg_d, wu_d, wd_d, cst, pre=ffn_pre, pfx="s3")
    scope_end(); prog.fence("f3")

    scope_begin()
    wd = {k: W("L1aa_" + k, shp) for k, shp in
          [("W_down", [D, D]), ("W_unify", [D, D]), ("b_unify", [D]),
           ("tokeys", [NH, 64, 64]), ("toqueries", [NH, 64, 64]), ("tovalues", [NH, 64, 64])]}
    build_attn_pa(nc, prog, xio[4], xio[5], wd, cst, pfx="s4")
    scope_end(); prog.fence("f4")

    scope_begin()
    wd = {"Wg": W("L1f_Wg", [D, NEXP]), "bg": W("L1f_bg", [NEXP]),
          "W1": W("L1f_W1", [NEXP, D, 4096]), "b1": W("L1f_b1", [NEXP, 4096]),
          "W2": W("L1f_W2", [NEXP, 4096, D]), "b2": W("L1f_b2", [NEXP, D]),
          "coremask": W("coremask", [8, 2])}
    moe_pre = build_moe_fused(nc, prog, xio[5], xio[6], wd, cst, pfx="s5")
    scope_end(); prog.fence("f5")

    scope_begin()
    wd6 = {k: W("L1ab_" + k, shp) for k, shp in
           [("wq", [D, D]), ("bq", [D]), ("wk", [D, D]), ("bk", [D]),
            ("wv", [D, D]), ("bv", [D]), ("wo", [D, D]), ("bo", [D])]}
    build_attn_b(nc, prog, xio[6], xio[7], wd6, cst, pfx="s6")
    scope_end(); prog.fence("f6")

    scope_begin()
    build_moe_fused(nc, prog, xio[7], xio[8], wd, cst, pfx="s7", pre=moe_pre)
    scope_end()

    prog.run()
    return consts, cn


def kernel(**z):
    f32 = np.float32
    x = np.asarray(z["L0_attn_a_x"], dtype=f32).reshape(-1, D)

    def bf():
        nc = bass.Bass("TRN2", target_bir_lowering=False)
        consts, cn = build_fused(nc)
        return nc, consts, cn

    nc, consts, cn = _get("fused", bf)
    in_maps = []
    for c in range(NCORES):
        m = {"xs0": to_fm(x[c * TC:(c + 1) * TC])}
        for k in cn:
            m[k] = consts[k]
        for k in ["W_kv", "g_kv", "W_kdec", "W_vdec", "W_q", "W_qdec",
                  "W_krope", "W_qrope", "W_o"]:
            m["L0aa_" + k] = np.asarray(z["L0_attn_a_" + k])
        m["L0f_wg"] = np.asarray(z["L0_ffn_w_gate"])
        m["L0f_wu"] = np.asarray(z["L0_ffn_w_up"])
        m["L0f_wd"] = np.asarray(z["L0_ffn_w_down"])
        for kk, src in [("wq", "Wq"), ("wk", "Wk"), ("wv", "Wv"), ("wo", "Wo"), ("bo", "bo")]:
            m["L0ab_" + kk] = np.asarray(z["L0_attn_b_" + src])
        for k in ["W_down", "tokeys", "toqueries", "tovalues", "W_unify", "b_unify"]:
            m["L1aa_" + k] = np.asarray(z["L1_attn_a_" + k])
        for k in ["Wg", "bg", "W1", "b1", "W2", "b2"]:
            m["L1f_" + k] = np.asarray(z["L1_ffn_" + k])
        for kk, src in [("wq", "Wq"), ("bq", "bq"), ("wk", "Wk"), ("bk", "bk"),
                        ("wv", "Wv"), ("bv", "bv"), ("wo", "Wo"), ("bo", "bo")]:
            m["L1ab_" + kk] = np.asarray(z["L1_attn_b_" + src])
        cm = np.zeros((8, 2), np.float32)
        cm[:c, 0] = 1.0
        cm[:, 1] = 1.0
        m["coremask"] = cm
        in_maps.append(m)
    res = run_bass_kernel_spmd(nc, in_maps, core_ids=list(range(NCORES)))
    out = np.concatenate([from_fm(np.asarray(r["xs8"])) for r in res.results], axis=0)
    return out.reshape(16, TSEQ, D).astype(np.float32)


def kernel_unfused(**z):
    f32 = np.float32
    x = np.asarray(z["L0_attn_a_x"], dtype=f32).reshape(-1, D)
    fm = [to_fm(x[c * TC:(c + 1) * TC]) for c in range(NCORES)]

    cn = ["ones128", "identbf", "causal128", "ropecos", "ropesin", "ropeRT"]
    wspecs = [("W_kv", [D, KVR], F32), ("g_kv", [KVR], F32), ("W_kdec", [KVR, D], F32),
              ("W_vdec", [KVR, D], F32), ("W_q", [D, QR], F32), ("W_qdec", [QR, D], F32),
              ("W_krope", [D, QR], F32), ("W_qrope", [QR, QR], F32), ("W_o", [D, D], F32)]
    nc, consts = _get("s0", _mk_simple("ml", build_attn_mla, wspecs, cn))
    wmap = {k: np.asarray(z["L0_attn_a_" + k]) for k in
            ["W_kv", "g_kv", "W_kdec", "W_vdec", "W_q", "W_qdec", "W_krope", "W_qrope", "W_o"]}
    fm = _run(nc, consts, cn, fm, wmap)

    def mk_ffn():
        consts = host_consts()
        nc = bass.Bass("TRN2", target_bir_lowering=False)
        xin = nc.dram_tensor("xin", [DC, 128, TC], F32, kind="ExternalInput")
        xout = nc.dram_tensor("xout", [DC, 128, TC], F32, kind="ExternalOutput")
        wg_d = nc.dram_tensor("wg", [HID, D], I32, kind="ExternalInput")
        wu_d = nc.dram_tensor("wu", [HID, D], I32, kind="ExternalInput")
        wd_d = nc.dram_tensor("wd", [D, HID], I32, kind="ExternalInput")
        prog = Prog(nc, "ffn")
        cst = load_consts(nc, prog, consts, ["ones128"])
        build_ffn(nc, prog, xin, xout, wg_d, wu_d, wd_d, cst)
        prog.run()
        return nc, consts

    nc, consts = _get("ffn", mk_ffn)
    wmap = {"wg": np.asarray(z["L0_ffn_w_gate"]), "wu": np.asarray(z["L0_ffn_w_up"]),
            "wd": np.asarray(z["L0_ffn_w_down"])}
    fm = _run(nc, consts, ["ones128"], fm, wmap)

    cn = ["ones128", "onesf", "identbf"]
    wspecs = [("wq", [D, D], F32), ("wk", [D, D], F32), ("wv", [D, D], F32),
              ("wo", [D, D], F32), ("bo", [D], F32)]
    nc2, consts2 = _get("s2", _mk_simple("al", build_attn_lin, wspecs, cn))
    wmap = {"wq": np.asarray(z["L0_attn_b_Wq"]), "wk": np.asarray(z["L0_attn_b_Wk"]),
            "wv": np.asarray(z["L0_attn_b_Wv"]), "wo": np.asarray(z["L0_attn_b_Wo"]),
            "bo": np.asarray(z["L0_attn_b_bo"])}
    fm = _run(nc2, consts2, cn, fm, wmap)

    nc, consts = _get("ffn", mk_ffn)
    wmap = {"wg": np.asarray(z["L0_ffn_w_gate"]), "wu": np.asarray(z["L0_ffn_w_up"]),
            "wd": np.asarray(z["L0_ffn_w_down"])}
    fm = _run(nc, consts, ["ones128"], fm, wmap)

    cn = ["ones128", "identbf"]
    wspecs = [("W_down", [D, D], F32), ("W_unify", [D, D], F32), ("b_unify", [D], F32),
              ("tokeys", [NH, 64, 64], F32), ("toqueries", [NH, 64, 64], F32),
              ("tovalues", [NH, 64, 64], F32)]
    nc4, consts4 = _get("s4", _mk_simple("pa", build_attn_pa, wspecs, cn))
    wmap = {k: np.asarray(z["L1_attn_a_" + k]) for k in
            ["W_down", "tokeys", "toqueries", "tovalues", "W_unify", "b_unify"]}
    fm = _run(nc4, consts4, cn, fm, wmap)

    moe_w = (np.asarray(z["L1_ffn_Wg"]), np.asarray(z["L1_ffn_bg"]),
             np.asarray(z["L1_ffn_W1"]), np.asarray(z["L1_ffn_b1"]),
             np.asarray(z["L1_ffn_W2"]), np.asarray(z["L1_ffn_b2"]))
    fm = _moe_call_fm(fm, *moe_w)

    cn = ["ones128", "onesf", "identbf", "causal128"]
    wspecs = [("wq", [D, D], F32), ("bq", [D], F32), ("wk", [D, D], F32), ("bk", [D], F32),
              ("wv", [D, D], F32), ("bv", [D], F32), ("wo", [D, D], F32), ("bo", [D], F32)]
    nc6, consts6 = _get("s6", _mk_simple("ab", build_attn_b, wspecs, cn))
    wmap = {"wq": np.asarray(z["L1_attn_b_Wq"]), "bq": np.asarray(z["L1_attn_b_bq"]),
            "wk": np.asarray(z["L1_attn_b_Wk"]), "bk": np.asarray(z["L1_attn_b_bk"]),
            "wv": np.asarray(z["L1_attn_b_Wv"]), "bv": np.asarray(z["L1_attn_b_bv"]),
            "wo": np.asarray(z["L1_attn_b_Wo"]), "bo": np.asarray(z["L1_attn_b_bo"])}
    fm = _run(nc6, consts6, cn, fm, wmap)

    fm = _moe_call_fm(fm, *moe_w)

    out = np.concatenate([from_fm(f) for f in fm], axis=0)
    return out.reshape(16, TSEQ, D).astype(np.float32)


def _moe_call_fm(fm, Wg, bg, W1, b1, W2, b2):
    consts = host_consts()

    def mk_lg():
        nc = bass.Bass("TRN2", target_bir_lowering=False)
        xin = nc.dram_tensor("xin", [DC, 128, TC], F32, kind="ExternalInput")
        lg_out = nc.dram_tensor("lgout", [8, TC], F32, kind="ExternalOutput")
        wg_d = nc.dram_tensor("Wg", [D, NEXP], F32, kind="ExternalInput")
        bg_d = nc.dram_tensor("bg", [NEXP], F32, kind="ExternalInput")
        prog = Prog(nc, "mg")
        cst = load_consts(nc, prog, consts, ["ones128", "onesf"])
        build_moe_logits(nc, prog, xin, lg_out, wg_d, bg_d, cst)
        prog.run()
        return nc, consts

    nc, cs = _get("moelg", mk_lg)
    in_maps = [{"xin": fm[c], "Wg": Wg, "bg": bg, "ones128": cs["ones128"],
                "onesf": cs["onesf"]} for c in range(NCORES)]
    res = run_bass_kernel_spmd(nc, in_maps, core_ids=list(range(NCORES)))
    logits = np.concatenate([np.asarray(r["lgout"]).T for r in res.results], axis=0)
    lgt = np.ascontiguousarray(logits.reshape(NTILES, 128, NEXP).transpose(1, 0, 2))

    def mk_gt():
        nc = bass.Bass("TRN2", target_bir_lowering=False)
        lgt_d = nc.dram_tensor("lgt", [128, NTILES, NEXP], F32, kind="ExternalInput")
        w_out = nc.dram_tensor("wout", [128, NTILES, NEXP], F32, kind="ExternalOutput")
        prog = Prog(nc, "gt")
        cst = load_consts(nc, prog, consts, ["ones128", "onesf", "iota8t", "uppstrict"])
        build_moe_gate(nc, prog, lgt_d, w_out, cst)
        prog.run()
        return nc, consts

    nc, cs = _get("moegt", mk_gt)
    in_maps = [{"lgt": lgt, "ones128": cs["ones128"], "onesf": cs["onesf"],
                "iota8t": cs["iota8t"], "uppstrict": cs["uppstrict"]}
               for c in range(NCORES)]
    res = run_bass_kernel_spmd(nc, in_maps, core_ids=list(range(NCORES)))
    w_tiled = np.asarray(res.results[0]["wout"])
    w_all = np.ascontiguousarray(w_tiled.transpose(1, 0, 2).reshape(32768, NEXP))

    def mk_mlp():
        nc = bass.Bass("TRN2", target_bir_lowering=False)
        xin = nc.dram_tensor("xin", [DC, 128, TC], F32, kind="ExternalInput")
        xout = nc.dram_tensor("xout", [DC, 128, TC], F32, kind="ExternalOutput")
        w_own_d = nc.dram_tensor("wown", [128, 32, NEXP], F32, kind="ExternalInput")
        wT_d = nc.dram_tensor("wT", [NEXP, TC], F32, kind="ExternalInput")
        w1_d = nc.dram_tensor("W1", [NEXP, D, 4096], F32, kind="ExternalInput")
        b1_d = nc.dram_tensor("b1", [NEXP, 4096], F32, kind="ExternalInput")
        w2_d = nc.dram_tensor("W2", [NEXP, 4096, D], F32, kind="ExternalInput")
        b2_d = nc.dram_tensor("b2", [NEXP, D], F32, kind="ExternalInput")
        prog = Prog(nc, "mm")
        cst = load_consts(nc, prog, consts, ["ones128", "onesf"])
        build_moe_mlp(nc, prog, xin, xout, w_own_d, wT_d, w1_d, b1_d, w2_d, b2_d, cst)
        prog.run()
        return nc, consts

    nc, cs = _get("moemlp", mk_mlp)
    in_maps = []
    for c in range(NCORES):
        w_own = w_all[c * TC:(c + 1) * TC]
        in_maps.append({
            "xin": fm[c],
            "wown": np.ascontiguousarray(w_own.reshape(32, 128, NEXP).transpose(1, 0, 2)),
            "wT": np.ascontiguousarray(w_own.T),
            "W1": W1, "b1": b1, "W2": W2, "b2": b2,
            "ones128": cs["ones128"], "onesf": cs["onesf"]})
    res = run_bass_kernel_spmd(nc, in_maps, core_ids=list(range(NCORES)))
    return [np.asarray(r["xout"]) for r in res.results]
